# version 01
# baseline (speedup 1.0000x reference)
import contextlib
import numpy as np
import ml_dtypes
import concourse.bass as bass
import concourse.mybir as mybir
from concourse.bass_utils import run_bass_kernel_spmd

F32 = mybir.dt.float32
BF16 = mybir.dt.bfloat16
I32 = mybir.dt.int32
AF = mybir.ActivationFunctionType
ALU = mybir.AluOpType
AX = mybir.AxisListType
BFNP = ml_dtypes.bfloat16

RND_C = 12582912.0

ENGINES = ("tensor", "vector", "scalar", "gpsimd", "sync")
DOMAINS = ENGINES + tuple("q_" + e for e in ENGINES)


class Res:
    __slots__ = ("name", "last_writer", "readers")

    def __init__(self, name):
        self.name = name
        self.last_writer = None
        self.readers = []


class Op:
    __slots__ = ("plan_idx", "engine", "domain", "fn", "reads", "writes", "deps",
                 "tag", "is_dma", "seq", "waits", "inc_needed", "sem_count", "vc",
                 "collective")

    def __init__(self, plan_idx, engine, fn, reads, writes, tag, is_dma, collective):
        self.plan_idx = plan_idx
        self.engine = engine
        self.is_dma = is_dma
        self.collective = collective
        if collective:
            self.domain = f"cc{plan_idx}"
        elif is_dma:
            self.domain = "q_" + engine
        else:
            self.domain = engine
        self.fn = fn
        self.reads = reads
        self.writes = writes
        self.deps = set()
        self.tag = tag
        self.seq = 0
        self.waits = []
        self.inc_needed = False
        self.sem_count = 0
        self.vc = None


class Prog:
    def __init__(self, same_engine_sync=True):
        self.ops = []
        self.same_engine_sync = same_engine_sync
        self._res_count = 0

    def res(self, name=None):
        self._res_count += 1
        return Res(name or f"r{self._res_count}")

    def op(self, engine, fn, reads=(), writes=(), tag="", dma=False, collective=False):
        o = Op(len(self.ops), engine, fn, tuple(reads), tuple(writes), tag, dma,
               collective)
        deps = set()
        for r in o.reads:
            if r.last_writer is not None:
                deps.add(r.last_writer)
        for r in o.writes:
            if r.last_writer is not None:
                deps.add(r.last_writer)
            for rd in r.readers:
                deps.add(rd)
        deps.discard(o)
        o.deps = deps
        for r in o.reads:
            r.readers.append(o)
        for r in o.writes:
            r.last_writer = o
            r.readers = []
        self.ops.append(o)
        return o

    def dma(self, engine, fn, reads=(), writes=(), tag=""):
        return self.op(engine, fn, reads, writes, tag, dma=True)

    def _schedule(self):
        self.domains = list(DOMAINS) + [o.domain for o in self.ops if o.collective]
        seq = {d: 0 for d in self.domains}
        for o in self.ops:
            seq[o.domain] += 1
            o.seq = seq[o.domain]
        vc_eng = {e: {d: 0 for d in self.domains} for e in ENGINES}
        eng_seq = {e: 0 for e in ENGINES}
        for o in self.ops:
            vc = vc_eng[o.engine]
            is_plain = (not o.is_dma) and (not o.collective)
            waits = []
            for d in sorted(o.deps, key=lambda d: d.plan_idx):
                d_plain = (not d.is_dma) and (not d.collective)
                if d_plain and d.engine == o.engine and not self.same_engine_sync:
                    continue
                if vc[d.domain] >= d.seq:
                    continue
                waits.append(d)
            for d in waits:
                d.inc_needed = True
                for dom, v in d.vc.items():
                    if v > vc[dom]:
                        vc[dom] = v
                if d.seq > vc[d.domain]:
                    vc[d.domain] = d.seq
            o.waits = waits
            if is_plain:
                eng_seq[o.engine] = o.seq
            myvc = dict(vc)
            myvc[o.domain] = max(myvc[o.domain], o.seq)
            if is_plain:
                myvc[o.engine] = max(myvc[o.engine], o.seq)
            else:
                myvc[o.engine] = max(myvc[o.engine], eng_seq[o.engine])
            o.vc = myvc
        cnt = {d: 0 for d in self.domains}
        for o in self.ops:
            if o.collective:
                cnt[o.domain] = 1
                o.sem_count = 1
                o.inc_needed = True
            elif o.is_dma:
                cnt[o.domain] += 16
                o.sem_count = cnt[o.domain]
                o.inc_needed = True
            elif o.inc_needed:
                cnt[o.domain] += 1
                o.sem_count = cnt[o.domain]
        self.sem_totals = cnt

    def build(self, nc, verbose=False):
        self._schedule()
        with contextlib.ExitStack() as stack:
            sems = {}
            for d in self.domains:
                if self.sem_totals[d] > 0:
                    sems[d] = stack.enter_context(nc.semaphore(f"sem_{d}"))
            block = stack.enter_context(nc.Block())

            def emit_engine(eng_name, eng):
                for o in self.ops:
                    if o.engine != eng_name:
                        continue
                    for w in o.waits:
                        eng.wait_ge(sems[w.domain], w.sem_count)
                    inst = o.fn(eng)
                    if o.inc_needed:
                        assert inst is not None, (o.tag, "fn must return instruction")
                        if o.collective:
                            inst.then_inc(sems[o.domain])
                        elif o.is_dma:
                            inst.then_inc(sems[o.domain], 16)
                        else:
                            inst.then_inc(sems[o.domain], 1)

            @block.tensor
            def _(eng):
                emit_engine("tensor", eng)

            @block.vector
            def _(eng):
                emit_engine("vector", eng)

            @block.scalar
            def _(eng):
                emit_engine("scalar", eng)

            @block.gpsimd
            def _(eng):
                emit_engine("gpsimd", eng)

            @block.sync
            def _(eng):
                emit_engine("sync", eng)
        if verbose:
            nw = sum(len(o.waits) for o in self.ops)
            print(f"[fw] ops={len(self.ops)} waits={nw} sems={ {d: c for d, c in self.sem_totals.items() if c} }")


class Ctx:
    def __init__(self, name):
        self.nc = bass.Bass("TRN2", target_bir_lowering=False)
        self.p = Prog()
        self.stack = contextlib.ExitStack()
        self.r = {}
        self.t = {}
        self.n_sb_bytes = 0
        self.inputs = []
        self.outputs = []

    def inp(self, name, shape, dtype=F32):
        self.inputs.append(name)
        return self.nc.dram_tensor(name, shape, dtype, kind="ExternalInput").ap()

    def out(self, name, shape, dtype=F32):
        self.outputs.append(name)
        ap = self.nc.dram_tensor(name, shape, dtype, kind="ExternalOutput").ap()
        self.r["out:" + name] = self.p.res("out:" + name)
        return ap

    def dram(self, name, shape, dtype=F32):
        ap = self.nc.dram_tensor(name, shape, dtype, kind="Internal").ap()
        self.r[name] = self.p.res(name)
        return ap

    def sb(self, name, shape, dtype=F32):
        h = self.stack.enter_context(self.nc.sbuf_tensor(name, list(shape), dtype))
        self.t[name] = h
        self.r[name] = self.p.res(name)
        per_part = int(np.prod(shape[1:])) * mybir.dt.size(dtype)
        self.n_sb_bytes += per_part
        return h

    def ps(self, name, shape, dtype=F32):
        h = self.stack.enter_context(self.nc.psum_tensor(name, list(shape), dtype))
        self.t[name] = h
        self.r[name] = self.p.res(name)
        return h

    def psum_pool(self):
        self.ps("pmm0", [128, 512], F32)
        self.ps("pmm1", [128, 512], F32)
        self.ps("pmm2", [128, 512], F32)
        self.ps("pmm3", [128, 512], F32)
        self.ps("pwide", [128, 1024], F32)
        self.ps("ptrb", [128, 1024], BF16)
        self.ps("paux", [128, 512], F32)

    def sb_shared(self, name, shape, dtype=F32):
        if name in self.t:
            return self.t[name], self.r[name]
        h = self.sb(name, shape, dtype)
        return h, self.r[name]

    def const_col(self, value, dtype=F32):
        key = (value, dtype)
        if not hasattr(self, "_cc"):
            self._cc = {}
        if key not in self._cc:
            nm = f"cc_{len(self._cc)}"
            h = self.sb(nm, [128, 1], dtype)
            r = self.r[nm]
            self.p.op("gpsimd", lambda e: e.memset(h[:, :], value), writes=[r], tag=nm)
            self._cc[key] = (h, r)
        return self._cc[key]

    def close(self):
        self.stack.close()

    def consts(self):
        p, nc = self.p, self.nc
        ident = self.sb("c_ident", [128, 128], F32)
        ident_b = self.sb("c_ident_b", [128, 128], BF16)
        ones = self.sb("c_ones", [128, 128], F32)
        ones_b = self.sb("c_ones_b", [128, 128], BF16)
        for nm, h, fill in (("c_ident", ident, "id"), ("c_ident_b", ident_b, "id"),
                            ("c_ones", ones, "1"), ("c_ones_b", ones_b, "1")):
            r = self.r[nm]
            p.op("gpsimd", lambda e, h=h: e.memset(h[:, :], 1.0), writes=[r], tag=nm)
            if fill == "id":
                p.op("gpsimd", lambda e, h=h: e.affine_select(
                    out=h[:, :], in_=h[:, :], compare_op=ALU.is_equal, fill=0.0,
                    base=0, pattern=[[-1, 128]], channel_multiplier=1),
                    reads=[r], writes=[r], tag=nm + "_sel")


def emit_mod(ctx, c_row_ap, wmod_ap, tag, sections=("sa", "ca1", "ga", "sb", "cb1", "gb"), wst=None):
    p, nc = ctx.p, ctx.nc
    crow = ctx.sb(tag + "_crow", [1, 1024], F32)
    rcrow = ctx.r[tag + "_crow"]
    p.dma("sync", lambda e: e.dma_start(out=crow[:, :], in_=c_row_ap), writes=[rcrow], tag=tag + "_ldc")
    p.op("scalar", lambda e: e.activation(crow[:, :], crow[:, :], AF.Silu),
         reads=[rcrow], writes=[rcrow], tag=tag + "_silu")
    silc = ctx.sb(tag + "_silc", [128, 8], F32)
    rsilc = ctx.r[tag + "_silc"]
    paux, rpaux = ctx.t["paux"], ctx.r["paux"]

    def tr_all(e):
        inst = None
        for c in range(8):
            inst = e.transpose(paux[:, c:c + 1], crow[0:1, c * 128:(c + 1) * 128],
                               ctx.t["c_ident"][0:1, 0:1])
        return inst
    p.op("tensor", tr_all, reads=[rcrow, ctx.r["c_ident"]], writes=[rpaux], tag=tag + "_tr")
    p.op("vector", lambda e: e.tensor_copy(silc[:, :], paux[:, 0:8]), reads=[rpaux], writes=[rsilc], tag=tag + "_trev")

    if wst is None:
        wbuf = [ctx.sb(tag + f"_wbuf{i}", [128, 1536], F32) for i in range(2)]
        rwbuf = [ctx.r[tag + f"_wbuf{i}"] for i in range(2)]
    else:
        wbuf = [wst[0][0], wst[1][0]]
        rwbuf = [wst[0][1], wst[1][1]]
    pmod = [ctx.t[f"pmm{i}"] for i in range(4)]
    rpmod = [ctx.r[f"pmm{i}"] for i in range(4)]
    names_all = ["sa", "ca1", "ga", "sb", "cb1", "gb"]
    sec_idx = [i for i, nm in enumerate(names_all) if nm in sections]
    modrow = ctx.sb(tag + "_modrow", [1, 1024 * len(sec_idx)], F32)
    rmodrow = ctx.r[tag + "_modrow"]
    wpiece = (wbuf[0].shape[1] // 512) * 512
    npieces = 6144 // wpiece
    for half in range(npieces):
        for c in range(8):
            b = c % 2
            p.dma("sync", lambda e, c=c, b=b, half=half, wpiece=wpiece: e.dma_start(
                out=wbuf[b][:, 0:wpiece], in_=wmod_ap[c * 128:(c + 1) * 128, half * wpiece:(half + 1) * wpiece]),
                writes=[rwbuf[b]], tag=f"{tag}_ldw{half}_{c}")

            def mms(e, c=c, b=b, half=half, wpiece=wpiece):
                inst = None
                for g6 in range(wpiece // 512):
                    g = half * (wpiece // 512) + g6
                    bank, row = g % 4, (g // 4) * 32
                    inst = e.matmul(pmod[bank][row:row + 1, :], silc[:, c:c + 1],
                                    wbuf[b][:, g6 * 512:(g6 + 1) * 512],
                                    start=(c == 0), stop=(c == 7))
                return inst
            p.op("tensor", mms, reads=[rwbuf[b], rsilc], writes=rpmod, tag=f"{tag}_mm{half}_{c}")

    def evac(e):
        inst = None
        for k, i in enumerate(sec_idx):
            for gg in range(2):
                g = i * 2 + gg
                bank, row = g % 4, (g // 4) * 32
                inst = e.tensor_copy(modrow[:, (k * 2 + gg) * 512:(k * 2 + gg + 1) * 512], pmod[bank][row:row + 1, :])
        return inst
    p.op("vector", evac, reads=rpmod, writes=[rmodrow], tag=tag + "_evac")

    result = {}
    names_i = [(k, names_all[i]) for k, i in enumerate(sec_idx)]
    pbc, rpbc = ctx.t["pwide"], ctx.r["pwide"]
    if "ga" in sections or "gb" in sections:
        tmp, rtmp = ctx.sb_shared("_mod_tmprow", [1, 1024], F32)
        sgn, rsgn = ctx.sb_shared("_mod_sgnrow", [1, 1024], F32)
    for i, nm in names_i:
        full = tag + "_" + nm
        dst = ctx.sb(full, [128, 1024], F32)
        rdst = ctx.r[full]
        sec = modrow[:, i * 1024:(i + 1) * 1024]
        if nm in ("ga", "gb"):
            p.op("scalar", lambda e, sec=sec: e.sign(sgn[:, :], sec), reads=[rmodrow], writes=[rsgn], tag=full + "_sg")
            p.op("scalar", lambda e, sec=sec: e.activation(tmp[:, :], sec, AF.Abs),
                 reads=[rmodrow], writes=[rtmp], tag=full + "_abs")
            p.op("vector", lambda e: e.tensor_scalar_max(tmp[:, :], tmp[:, :], 0.25),
                 reads=[rtmp], writes=[rtmp], tag=full + "_mx")
            p.op("vector", lambda e: e.tensor_mul(tmp[:, :], tmp[:, :], sgn[:, :]),
                 reads=[rtmp, rsgn], writes=[rtmp], tag=full + "_ms")
            src_row, rsrc = tmp, rtmp
        else:
            src_row, rsrc = sec, rmodrow

        def bc(e, src_row=src_row):
            e.matmul(pbc[:, 0:512], ctx.t["c_ones"][0:1, :], src_row[:, 0:512], start=True, stop=True)
            return e.matmul(pbc[:, 512:1024], ctx.t["c_ones"][0:1, :], src_row[:, 512:1024], start=True, stop=True)
        p.op("tensor", bc, reads=[rsrc, ctx.r["c_ones"]], writes=[rpbc], tag=full + "_bc")
        add1 = 1.0 if nm in ("ca1", "cb1") else 0.0
        p.op("vector", lambda e, dst=dst, add1=add1: e.tensor_scalar_add(dst[:, :], pbc[:, :], add1),
             reads=[rpbc], writes=[rdst], tag=full + "_ev")
        result[nm] = (dst, rdst)
    return result


def emit_abs_reduce(ctx, nm, in_ap, rin, flavor=ALU.max):
    p = ctx.p
    col = ctx.sb(nm + "_col", [128, 1], F32)
    rcol = ctx.r[nm + "_col"]
    paux, rpaux = ctx.t["paux"], ctx.r["paux"]
    sc = ctx.sb(nm + "_sc", [1, 1], F32)
    rsc = ctx.r[nm + "_sc"]
    p.op("vector", lambda e: e.tensor_reduce(col[:, :], in_ap, AX.X, flavor, apply_absolute_value=True),
         reads=[rin], writes=[rcol], tag=nm + "_r1")
    p.op("tensor", lambda e: e.transpose(paux[0:1, 0:128], col[:, :], ctx.t["c_ident"][:, :]),
         reads=[rcol, ctx.r["c_ident"]], writes=[rpaux], tag=nm + "_tr")
    p.op("vector", lambda e: e.tensor_reduce(sc[:, :], paux[0:1, 0:128], AX.X, flavor),
         reads=[rpaux], writes=[rsc], tag=nm + "_r2")
    return sc, rsc


def emit_ln_mod(ctx, tag, tile_src, ca1, sa, hT, rhT, n_tiles=16):
    p = ctx.p
    ca1_t, rca1 = ca1
    sa_t, rsa = sa
    mean, rmean = ctx.sb_shared("_lns_mean", [128, 1], F32)
    sq, rsq = ctx.sb_shared("_lns_sq", [128, 1024], F32)
    ssq, rssq = ctx.sb_shared("_lns_ssq", [128, 1], F32)
    var, rvar = ctx.sb_shared("_lns_var", [128, 1], F32)
    rstd, rrstd = ctx.sb_shared("_lns_rstd", [128, 1], F32)
    nmr, rnmr = ctx.sb_shared("_lns_nmr", [128, 1], F32)
    xn, rxn = ctx.sb_shared("_lns_xn", [128, 1024], F32)
    hbf, rhbf = ctx.sb_shared("_lns_hbf", [128, 1024], BF16)
    ptr, rptr = ctx.t["ptrb"], ctx.r["ptrb"]
    for t in range(n_tiles):
        xt, rx = tile_src(t)
        p.op("vector", lambda e, xt=xt: e.tensor_reduce(mean[:, :], xt, AX.X, ALU.add),
             reads=[rx], writes=[rmean], tag=f"{tag}_m{t}")
        p.op("vector", lambda e: e.tensor_scalar_mul(mean[:, :], mean[:, :], 1.0 / 1024),
             reads=[rmean], writes=[rmean], tag=f"{tag}_ms{t}")
        p.op("scalar", lambda e, xt=xt: e.activation(sq[:, :], xt, AF.Square, accum_out=ssq[:, :]),
             reads=[rx], writes=[rsq, rssq], tag=f"{tag}_sq{t}")
        p.op("vector", lambda e: e.scalar_tensor_tensor(var[:, :], mean[:, :], -1.0, mean[:, :], ALU.mult, ALU.mult),
             reads=[rmean], writes=[rvar], tag=f"{tag}_v1{t}")
        p.op("vector", lambda e: e.scalar_tensor_tensor(var[:, :], ssq[:, :], 1.0 / 1024, var[:, :], ALU.mult, ALU.add),
             reads=[rssq, rvar], writes=[rvar], tag=f"{tag}_v2{t}")
        eps_c, reps = ctx.const_col(1e-5)
        nh_c, rnh = ctx.const_col(-0.5)
        p.op("scalar", lambda e: e.activation(rstd[:, :], var[:, :], AF.Ln, bias=eps_c[:, :]),
             reads=[rvar, reps], writes=[rrstd], tag=f"{tag}_ln{t}")
        p.op("scalar", lambda e: e.activation(rstd[:, :], rstd[:, :], AF.Exp, scale=nh_c[:, :]),
             reads=[rrstd, rnh], writes=[rrstd], tag=f"{tag}_ex{t}")
        p.op("vector", lambda e: e.scalar_tensor_tensor(nmr[:, :], mean[:, :], -1.0, rstd[:, :], ALU.mult, ALU.mult),
             reads=[rmean, rrstd], writes=[rnmr], tag=f"{tag}_nm{t}")
        p.op("scalar", lambda e, xt=xt: e.activation(xn[:, :], xt, AF.Identity, bias=nmr[:, :], scale=rstd[:, :]),
             reads=[rx, rnmr, rrstd], writes=[rxn], tag=f"{tag}_xn{t}")
        p.op("vector", lambda e: e.tensor_mul(xn[:, :], xn[:, :], ca1_t[:, :]),
             reads=[rxn, rca1], writes=[rxn], tag=f"{tag}_sc{t}")
        p.op("vector", lambda e: e.tensor_add(hbf[:, :], xn[:, :], sa_t[:, :]),
             reads=[rxn, rsa], writes=[rhbf], tag=f"{tag}_ad{t}")

        def trs(e, t=t):
            inst = None
            for j in range(8):
                inst = e.transpose(ptr[:, j * 128:(j + 1) * 128], hbf[:, j * 128:(j + 1) * 128],
                                   ctx.t["c_ident_b"][:, :])
            return inst
        p.op("tensor", trs, reads=[rhbf, ctx.r["c_ident_b"]], writes=[rptr], tag=f"{tag}_tr{t}")
        tpp = hT.shape[1] // 8
        p.op("vector", lambda e, t=t, tpp=tpp: e.tensor_copy(
            bass.AP(hT, t * 128, [[8 * tpp, 128], [tpp, 8], [1, 128]]),
            ptr[:, :].rearrange("p (j q) -> p j q", j=8)),
            reads=[rptr], writes=[rhT], tag=f"{tag}_te{t}")


def make_x_stream(ctx, tag, x_in, n_tiles=16, nbuf=3):
    bufs = [ctx.sb(f"{tag}_xs{i}", [128, 1024], F32) for i in range(nbuf)]
    rbufs = [ctx.r[f"{tag}_xs{i}"] for i in range(nbuf)]

    def tile_src(t):
        b = t % nbuf
        ctx.p.dma("sync", lambda e, t=t, b=b: e.dma_start(
            out=bufs[b][:, :], in_=x_in[t * 128:(t + 1) * 128, :]),
            writes=[rbufs[b]], tag=f"{tag}_ldx{t}")
        return bufs[b][:, :], rbufs[b]
    return tile_src


def np_bf(x):
    return np.ascontiguousarray(x).astype(BFNP)


def build_l0_qkv(dbg=False):
    ctx = Ctx("l0qkv")
    nc, p = ctx.nc, ctx.p
    x_in = ctx.inp("x", [2048, 1024])
    c_in = ctx.inp("c_row", [1, 1024])
    wmod_in = ctx.inp("wmod", [1024, 6144])
    wat_in = ctx.inp("wattnT", [1024, 3072])
    qt_out = ctx.out("QT", [1024, 2048], BF16)
    kt_out = ctx.out("KT", [1024, 2048], BF16)
    v_out = ctx.out("V", [2048, 1024], BF16)

    ctx.psum_pool()
    ctx.consts()
    wstA = ctx.sb("wstA", [128, 3072], F32)
    wstB = ctx.sb("wstB", [128, 3072], F32)
    rwstA, rwstB = ctx.r["wstA"], ctx.r["wstB"]
    wst = [(wstA, rwstA), (wstB, rwstB)]

    mod = emit_mod(ctx, c_in, wmod_in, "mod0", sections=("sa", "ca1"), wst=wst)

    hT = ctx.sb("hT", [128, 8 * 2048], BF16)
    rhT = ctx.r["hT"]
    tile_src = make_x_stream(ctx, "l1x", x_in)
    emit_ln_mod(ctx, "ln0a", tile_src, mod["ca1"], mod["sa"], hT, rhT)

    if dbg:
        dsa = ctx.out("dbg_sa", [128, 1024])
        p.dma("sync", lambda e: e.dma_start(out=dsa, in_=ctx.t["mod0_sa"][:, :]),
              reads=[mod["sa"][1]], writes=[ctx.r["out:dbg_sa"]], tag="dbg_sa")
        dca = ctx.out("dbg_ca1", [128, 1024])
        p.dma("sync", lambda e: e.dma_start(out=dca, in_=ctx.t["mod0_ca1"][:, :]),
              reads=[mod["ca1"][1]], writes=[ctx.r["out:dbg_ca1"]], tag="dbg_ca")
        didb = ctx.out("dbg_idb", [128, 128], BF16)
        p.dma("sync", lambda e: e.dma_start(out=didb, in_=ctx.t["c_ident_b"][:, :]),
              reads=[ctx.r["c_ident_b"]], writes=[ctx.r["out:dbg_idb"]], tag="dbg_idb")
        dmod = ctx.out("dbg_modrow", [1, 6144])
        rmodrow_d = ctx.r["mod0_modrow"]
        p.dma("sync", lambda e: e.dma_start(out=dmod, in_=ctx.t["mod0_modrow"][:, :]),
              reads=[rmodrow_d], writes=[ctx.r["out:dbg_modrow"]], tag="dbg_mod")
        dhT = ctx.out("dbg_hT", [1024, 2048], BF16)
        for j in range(8):
            p.dma("sync", lambda e, j=j: e.dma_start(out=dhT[j * 128:(j + 1) * 128, :], in_=hT[:, j * 2048:(j + 1) * 2048]),
                  reads=[rhT], writes=[ctx.r["out:dbg_hT"]], tag=f"dbg_hT{j}")

    wbuf, rwbuf = [wstA, wstB], [rwstA, rwstB]
    acc8 = ctx.sb("acc8", [128, 8], F32)
    racc8 = ctx.r["acc8"]
    for c in range(8):
        b = c % 2
        p.dma("sync", lambda e, c=c, b=b: e.dma_start(out=wbuf[b][:, :], in_=wat_in[c * 128:(c + 1) * 128, :]),
              writes=[rwbuf[b]], tag=f"qw_ld{c}")
        p.op("vector", lambda e, c=c, b=b: e.tensor_reduce(acc8[:, c:c + 1], wbuf[b][:, :], AX.X, ALU.add, apply_absolute_value=True),
             reads=[rwbuf[b]], writes=[racc8], tag=f"qw_abs{c}")
    stot, rstot = emit_abs_reduce(ctx, "qws", acc8[:, :], racc8, flavor=ALU.add)
    p.op("vector", lambda e: e.tensor_scalar(stot[:, :], stot[:, :], 1.0 / (3072 * 1024), 1e-8, ALU.mult, ALU.max),
         reads=[rstot], writes=[rstot], tag="qw_sc")
    rinv = ctx.sb("qw_rinv", [1, 1], F32)
    rrinv = ctx.r["qw_rinv"]
    p.op("vector", lambda e: e.reciprocal(rinv[:, :], stot[:, :]), reads=[rstot], writes=[rrinv], tag="qw_rc")
    paux, rpaux = ctx.t["paux"], ctx.r["paux"]

    def bc2(e):
        e.matmul(paux[:, 0:1], ctx.t["c_ones"][0:1, :], rinv[:, :], start=True, stop=True)
        return e.matmul(paux[:, 1:2], ctx.t["c_ones"][0:1, :], stot[:, :], start=True, stop=True)
    p.op("tensor", bc2, reads=[rrinv, rstot, ctx.r["c_ones"]], writes=[rpaux], tag="qw_bc")
    rcol = ctx.sb("qw_rcol", [128, 2], F32)
    rrcol = ctx.r["qw_rcol"]
    p.op("vector", lambda e: e.tensor_copy(rcol[:, :], paux[:, 0:2]), reads=[rpaux], writes=[rrcol], tag="qw_bce")
    scq = ctx.sb("qw_scq", [128, 1], F32)
    rscq = ctx.r["qw_scq"]
    p.op("vector", lambda e: e.tensor_scalar_mul(scq[:, :], rcol[:, 1:2], 0.125),
         reads=[rrcol], writes=[rscq], tag="qw_scq")

    if dbg:
        dac = ctx.out("dbg_acc8", [128, 8])
        p.dma("sync", lambda e: e.dma_start(out=dac, in_=acc8[:, :]),
              reads=[racc8], writes=[ctx.r["out:dbg_acc8"]], tag="dbg_ac")
        dsc = ctx.out("dbg_sc", [1, 2])
        p.dma("sync", lambda e: e.dma_start(out=dsc[0:1, 0:1], in_=stot[:, :]),
              reads=[rstot], writes=[ctx.r["out:dbg_sc"]], tag="dbg_sc1")
        p.dma("sync", lambda e: e.dma_start(out=dsc[0:1, 1:2], in_=rinv[:, :]),
              reads=[rrinv], writes=[ctx.r["out:dbg_sc"]], tag="dbg_sc2")

    qw = ctx.sb("qw", [128, 8 * 3072], BF16)
    rqw = ctx.r["qw"]
    qtmp = ctx.sb("qtmp", [128, 3072], F32)
    rqtmp = ctx.r["qtmp"]
    for c in range(8):
        b = c % 2
        p.dma("sync", lambda e, c=c, b=b: e.dma_start(out=wbuf[b][:, :], in_=wat_in[c * 128:(c + 1) * 128, :]),
              writes=[rwbuf[b]], tag=f"qw_ld2_{c}")
        p.op("vector", lambda e, b=b: e.tensor_scalar(qtmp[:, :], wbuf[b][:, :], rcol[:, 0:1], RND_C, ALU.mult, ALU.add),
             reads=[rwbuf[b], rrcol], writes=[rqtmp], tag=f"qw_q1_{c}")
        p.op("vector", lambda e: e.tensor_scalar(qtmp[:, :], qtmp[:, :], RND_C, 2.0, ALU.subtract, ALU.min),
             reads=[rqtmp], writes=[rqtmp], tag=f"qw_q2_{c}")
        p.op("vector", lambda e, c=c: e.tensor_scalar_max(qw[:, c * 3072:(c + 1) * 3072], qtmp[:, :], -2.0),
             reads=[rqtmp], writes=[rqw], tag=f"qw_q3_{c}")

    pmm = [ctx.t[f"pmm{i}"] for i in range(4)]
    rpmm = [ctx.r[f"pmm{i}"] for i in range(4)]
    stg = [ctx.sb(f"qk_stg{i}", [128, 2048], BF16) for i in range(2)]
    rstg = [ctx.r[f"qk_stg{i}"] for i in range(2)]
    for j in range(16):
        sb_ = j % 2
        for tg in range(4):
            bank = tg % 4

            def mms(e, j=j, tg=tg, bank=bank):
                inst = None
                for c in range(8):
                    inst = e.matmul(pmm[bank][:, :], qw[:, c * 3072 + j * 128: c * 3072 + (j + 1) * 128],
                                    hT[:, c * 2048 + tg * 512: c * 2048 + (tg + 1) * 512],
                                    start=(c == 0), stop=(c == 7))
                return inst
            p.op("tensor", mms, reads=[rqw, rhT], writes=[rpmm[bank]], tag=f"qk_mm{j}_{tg}")
            scale = scq[:, 0:1] if j < 8 else rcol[:, 1:2]
            rscale = rscq if j < 8 else rrcol
            p.op("scalar", lambda e, tg=tg, bank=bank, sb_=sb_, scale=scale: e.activation(
                stg[sb_][:, tg * 512:(tg + 1) * 512], pmm[bank][:, :], AF.Identity, scale=scale),
                reads=[rpmm[bank], rscale], writes=[rstg[sb_]], tag=f"qk_ev{j}_{tg}")
        dst = qt_out if j < 8 else kt_out
        rdst = ctx.r["out:QT"] if j < 8 else ctx.r["out:KT"]
        jj = j % 8
        p.dma("sync", lambda e, jj=jj, sb_=sb_, dst=dst: e.dma_start(
            out=dst[jj * 128:(jj + 1) * 128, :], in_=stg[sb_][:, :]),
            reads=[rstg[sb_]], writes=[rdst], tag=f"qk_st{j}")

    vstg = [ctx.sb(f"v_stg{i}", [128, 1024], BF16) for i in range(2)]
    rvstg = [ctx.r[f"v_stg{i}"] for i in range(2)]
    for t in range(16):
        sb_ = t % 2
        for hf in range(2):
            bank = hf % 4

            def mmv(e, t=t, hf=hf, bank=bank):
                inst = None
                for c in range(8):
                    inst = e.matmul(pmm[bank][:, :], hT[:, c * 2048 + t * 128: c * 2048 + (t + 1) * 128],
                                    qw[:, c * 3072 + 2048 + hf * 512: c * 3072 + 2048 + (hf + 1) * 512],
                                    start=(c == 0), stop=(c == 7))
                return inst
            p.op("tensor", mmv, reads=[rqw, rhT], writes=[rpmm[bank]], tag=f"v_mm{t}_{hf}")
            p.op("scalar", lambda e, hf=hf, bank=bank, sb_=sb_: e.activation(
                vstg[sb_][:, hf * 512:(hf + 1) * 512], pmm[bank][:, :], AF.Identity, scale=rcol[:, 1:2]),
                reads=[rpmm[bank], rrcol], writes=[rvstg[sb_]], tag=f"v_ev{t}_{hf}")
        p.dma("sync", lambda e, t=t, sb_=sb_: e.dma_start(
            out=v_out[t * 128:(t + 1) * 128, :], in_=vstg[sb_][:, :]),
            reads=[rvstg[sb_]], writes=[ctx.r["out:V"]], tag=f"v_st{t}")

    p.build(nc)
    ctx.close()
    nc.reset()
    return ctx


FLASH_HEADS = 16
FLASH_QG = 4
FLASH_NORM = True


def build_l0_flash(half):
    ctx = Ctx("l0flash")
    nc, p = ctx.nc, ctx.p
    qt_in = ctx.inp("QTo", [1024, 2048], BF16)
    kt_in = ctx.inp("KTf", [1024, 4096], BF16)
    v_in = ctx.inp("Vf", [4096, 1024], BF16)
    yt_out = ctx.out("YT", [1024, 2048], BF16)

    ctx.psum_pool()
    ctx.consts()

    qt = ctx.sb("qt", [128, 8 * 2048], BF16)
    rqt = ctx.r["qt"]
    kt = ctx.sb("kt", [128, 8 * 4096], BF16)
    rkt = ctx.r["kt"]
    vsb = ctx.sb("vsb", [128, 32 * 1040], BF16)
    rvsb = ctx.r["vsb"]
    yt = ctx.sb("yt", [128, 8 * 2048], BF16)
    ryt = ctx.r["yt"]

    for j in range(8):
        p.dma("sync", lambda e, j=j: e.dma_start(out=qt[:, j * 2048:(j + 1) * 2048], in_=qt_in[j * 128:(j + 1) * 128, :]),
              writes=[rqt], tag=f"ldq{j}")
        p.dma("sync", lambda e, j=j: e.dma_start(out=kt[:, j * 4096:(j + 1) * 4096], in_=kt_in[j * 128:(j + 1) * 128, :]),
              writes=[rkt], tag=f"ldk{j}")
    p.op("gpsimd", lambda e: e.memset(bass.AP(vsb, 64, [[33280, 128], [65, 512]]), 1.0),
         writes=[rvsb], tag="vones")
    for t in range(32):
        p.dma("sync", lambda e, t=t: e.dma_start(
            out=bass.AP(vsb, t * 1040, [[33280, 128], [65, 16], [1, 64]]),
            in_=v_in[t * 128:(t + 1) * 128, :].rearrange("p (g d) -> p g d", g=16)),
            writes=[rvsb], tag=f"ldv{t}")

    sqs = ctx.sb("sqs", [128, 2048], BF16)
    rsqs = ctx.r["sqs"]
    n2 = ctx.sb("n2", [1, 33], F32)
    rn2 = ctx.r["n2"]
    paux, rpaux = ctx.t["paux"], ctx.r["paux"]
    norm_srcs = ((qt, rqt, 2048, 0, "q"), (kt, rkt, 4096, 16, "k")) if FLASH_NORM else ()
    for src_t, rsrc, width, base_col, tnm in norm_srcs:
        for ch in range(8):
            for piece in range(width // 2048):
                p.op("scalar", lambda e, src_t=src_t, ch=ch, width=width, piece=piece: e.activation(
                    sqs[:, :], src_t[:, ch * width + piece * 2048: ch * width + (piece + 1) * 2048], AF.Square),
                    reads=[rsrc], writes=[rsqs], tag=f"nsq_{tnm}{ch}_{piece}")
                for sub in range(2):
                    h = ch * 2 + sub
                    for p5 in range(4):
                        first = (piece == 0 and p5 == 0)

                        def nmm(e, sub=sub, p5=p5):
                            b2 = sub * 64
                            return e.matmul(paux[0:1, 0:512], ctx.t["c_ones_b"][b2:b2 + 64, 0:1],
                                            sqs[b2:b2 + 64, p5 * 512:(p5 + 1) * 512], start=True, stop=True)
                        p.op("tensor", nmm, reads=[rsqs, ctx.r["c_ones_b"]], writes=[rpaux], tag=f"nmm_{tnm}{h}_{piece}_{p5}")

                        def nred(e, h=h, base_col=base_col, first=first):
                            if first:
                                return e.tensor_reduce(n2[:, base_col + h:base_col + h + 1], paux[0:1, 0:512], AX.X, ALU.max)
                            e.tensor_reduce(n2[:, 32:33], paux[0:1, 0:512], AX.X, ALU.max)
                            return e.tensor_max(n2[:, base_col + h:base_col + h + 1],
                                                n2[:, base_col + h:base_col + h + 1], n2[:, 32:33])
                        p.op("vector", nred, reads=[rpaux, rn2], writes=[rn2], tag=f"nrd_{tnm}{h}_{piece}_{p5}")

    cr = ctx.sb("cr", [1, 16], F32)
    rcr = ctx.r["cr"]
    if not FLASH_NORM:
        p.op("gpsimd", lambda e: e.memset(n2[:, :], 100.0), writes=[rn2], tag="n2fake")
    p.op("scalar", lambda e: e.activation(n2[:, :], n2[:, :], AF.Sqrt), reads=[rn2], writes=[rn2], tag="nsqrt")
    p.op("vector", lambda e: e.tensor_mul(cr[:, :], n2[:, 0:16], n2[:, 16:32]), reads=[rn2], writes=[rcr], tag="cmul")
    p.op("vector", lambda e: e.tensor_scalar_mul(cr[:, :], cr[:, :], -0.125), reads=[rcr], writes=[rcr], tag="cneg")
    bcneg = ctx.sb("bcneg", [128, 16], F32)
    rbcneg = ctx.r["bcneg"]
    p.op("tensor", lambda e: e.matmul(paux[:, 0:16], ctx.t["c_ones"][0:1, :], cr[:, :], start=True, stop=True),
         reads=[rcr, ctx.r["c_ones"]], writes=[rpaux], tag="cbc")
    p.op("vector", lambda e: e.tensor_copy(bcneg[:, :], paux[:, 0:16]), reads=[rpaux], writes=[rbcneg], tag="cbce")

    pS = [ctx.t["pmm0"], ctx.t["pmm1"]]
    rpS = [ctx.r["pmm0"], ctx.r["pmm1"]]
    pAV, rpAV = ctx.t["pmm2"], ctx.r["pmm2"]
    pBC, rpBC = ctx.t["pmm3"], ctx.r["pmm3"]
    pt_sb = [ctx.sb(f"pt{i}", [128, 512], BF16) for i in range(2)]
    rpt_sb = [ctx.r[f"pt{i}"] for i in range(2)]
    ysb = ctx.sb("ysb", [65, 512], F32)
    rysb = ctx.r["ysb"]
    for h in range(FLASH_HEADS):
        ch, b = h // 2, (h % 2) * 64
        for qg in range(FLASH_QG):
            qstart = (half * 16 + qg * 4) * 128
            nkc = half * 16 + (qg + 1) * 4
            for kc in range(nkc):
                sb_ = kc % 2

                def smm(e, h=h, qg=qg, kc=kc, sb_=sb_, ch=ch, b=b):
                    return e.matmul(pS[sb_][:, :],
                                    kt[b:b + 64, ch * 4096 + kc * 128: ch * 4096 + (kc + 1) * 128],
                                    qt[b:b + 64, ch * 2048 + qg * 512: ch * 2048 + (qg + 1) * 512],
                                    start=True, stop=True)
                p.op("tensor", smm, reads=[rkt, rqt], writes=[rpS[sb_]], tag=f"S{h}_{qg}_{kc}")
                p.op("scalar", lambda e, h=h, sb_=sb_: e.activation(
                    pt_sb[sb_][:, :], pS[sb_][:, :], AF.Exp, bias=bcneg[:, h:h + 1]),
                    reads=[rpS[sb_], rbcneg], writes=[rpt_sb[sb_]], tag=f"E{h}_{qg}_{kc}")
                if kc * 128 + 127 > qstart:
                    p.op("gpsimd", lambda e, sb_=sb_, qstart=qstart, kc=kc: e.affine_select(
                        out=pt_sb[sb_][:, :], in_=pt_sb[sb_][:, :], compare_op=ALU.is_ge,
                        fill=0.0, base=qstart - kc * 128, pattern=[[1, 512]], channel_multiplier=-1),
                        reads=[rpt_sb[sb_]], writes=[rpt_sb[sb_]], tag=f"M{h}_{qg}_{kc}")

                def avmm(e, h=h, kc=kc, sb_=sb_, first=(kc == 0), last=(kc == nkc - 1)):
                    return e.matmul(pAV[0:65, :], vsb[:, kc * 1040 + h * 65: kc * 1040 + (h + 1) * 65],
                                    pt_sb[sb_][:, :], start=first, stop=last)
                p.op("tensor", avmm, reads=[rvsb, rpt_sb[sb_]], writes=[rpAV], tag=f"A{h}_{qg}_{kc}")
            p.op("vector", lambda e: e.tensor_copy(ysb[:, :], pAV[0:65, :]), reads=[rpAV], writes=[rysb], tag=f"Y{h}_{qg}")
            p.op("vector", lambda e: e.reciprocal(ysb[64:65, :], ysb[64:65, :]), reads=[rysb], writes=[rysb], tag=f"R{h}_{qg}")
            p.op("tensor", lambda e: e.matmul(pBC[0:64, :], ctx.t["c_ones"][64:65, 0:64], ysb[64:65, :], start=True, stop=True),
                 reads=[rysb, ctx.r["c_ones"]], writes=[rpBC], tag=f"B{h}_{qg}")
            p.op("vector", lambda e, ch=ch, b=b, qg=qg: e.tensor_tensor(
                yt[b:b + 64, ch * 2048 + qg * 512: ch * 2048 + (qg + 1) * 512],
                ysb[0:64, :], pBC[0:64, :], ALU.mult),
                reads=[rysb, rpBC], writes=[ryt], tag=f"W{h}_{qg}")

    for j in range(8):
        p.dma("sync", lambda e, j=j: e.dma_start(out=yt_out[j * 128:(j + 1) * 128, :], in_=yt[:, j * 2048:(j + 1) * 2048]),
              reads=[ryt], writes=[ctx.r["out:YT"]], tag=f"sty{j}")

    p.build(nc)
    ctx.close()
    nc.reset()
    return ctx


def emit_quant_weight(ctx, tag, w_in, n_chunks, ncols, wst):
    p = ctx.p
    wbuf = [wst[0][0], wst[1][0]]
    rwbuf = [wst[0][1], wst[1][1]]
    accN = ctx.sb(tag + "_acc", [128, n_chunks], F32)
    racc = ctx.r[tag + "_acc"]
    for c in range(n_chunks):
        b = c % 2
        p.dma("sync", lambda e, c=c, b=b: e.dma_start(out=wbuf[b][:, 0:ncols], in_=w_in[c * 128:(c + 1) * 128, :]),
              writes=[rwbuf[b]], tag=f"{tag}_ld{c}")
        p.op("vector", lambda e, c=c, b=b: e.tensor_reduce(accN[:, c:c + 1], wbuf[b][:, 0:ncols], AX.X, ALU.add, apply_absolute_value=True),
             reads=[rwbuf[b]], writes=[racc], tag=f"{tag}_abs{c}")
    stot, rstot = emit_abs_reduce(ctx, tag + "s", accN[:, :], racc, flavor=ALU.add)
    p.op("vector", lambda e: e.tensor_scalar(stot[:, :], stot[:, :], 1.0 / (n_chunks * 128 * ncols), 1e-8, ALU.mult, ALU.max),
         reads=[rstot], writes=[rstot], tag=tag + "_sc")
    rinv = ctx.sb(tag + "_rinv", [1, 1], F32)
    rrinv = ctx.r[tag + "_rinv"]
    p.op("vector", lambda e: e.reciprocal(rinv[:, :], stot[:, :]), reads=[rstot], writes=[rrinv], tag=tag + "_rc")
    paux, rpaux = ctx.t["paux"], ctx.r["paux"]

    def bc2(e):
        e.matmul(paux[:, 0:1], ctx.t["c_ones"][0:1, :], rinv[:, :], start=True, stop=True)
        return e.matmul(paux[:, 1:2], ctx.t["c_ones"][0:1, :], stot[:, :], start=True, stop=True)
    p.op("tensor", bc2, reads=[rrinv, rstot, ctx.r["c_ones"]], writes=[rpaux], tag=tag + "_bc")
    rcol = ctx.sb(tag + "_rcol", [128, 2], F32)
    rrcol = ctx.r[tag + "_rcol"]
    p.op("vector", lambda e: e.tensor_copy(rcol[:, :], paux[:, 0:2]), reads=[rpaux], writes=[rrcol], tag=tag + "_bce")

    qw = ctx.sb(tag + "_qw", [128, n_chunks * ncols], BF16)
    rqw = ctx.r[tag + "_qw"]
    qtmp = ctx.sb(tag + "_qtmp", [128, ncols], F32)
    rqtmp = ctx.r[tag + "_qtmp"]
    for c in range(n_chunks):
        b = c % 2
        p.dma("sync", lambda e, c=c, b=b: e.dma_start(out=wbuf[b][:, 0:ncols], in_=w_in[c * 128:(c + 1) * 128, :]),
              writes=[rwbuf[b]], tag=f"{tag}_ld2_{c}")
        p.op("vector", lambda e, b=b: e.tensor_scalar(qtmp[:, :], wbuf[b][:, 0:ncols], rcol[:, 0:1], RND_C, ALU.mult, ALU.add),
             reads=[rwbuf[b], rrcol], writes=[rqtmp], tag=f"{tag}_q1_{c}")
        p.op("vector", lambda e: e.tensor_scalar(qtmp[:, :], qtmp[:, :], RND_C, 2.0, ALU.subtract, ALU.min),
             reads=[rqtmp], writes=[rqtmp], tag=f"{tag}_q2_{c}")
        p.op("vector", lambda e, c=c: e.tensor_scalar_max(qw[:, c * ncols:(c + 1) * ncols], qtmp[:, :], -2.0),
             reads=[rqtmp], writes=[rqw], tag=f"{tag}_q3_{c}")
    return qw, rqw, rcol, rrcol


def build_l0_route():
    ctx = Ctx("l0route")
    nc, p = ctx.nc, ctx.p
    x_in = ctx.inp("x", [2048, 1024])
    yt_in = ctx.inp("YT", [1024, 2048], BF16)
    wproj_in = ctx.inp("wprojT", [1024, 1024])
    c_in = ctx.inp("c_row", [1, 1024])
    wmod_in = ctx.inp("wmod", [1024, 6144])
    gatew_in = ctx.inp("gatewT", [1024, 9])
    x1_out = ctx.out("x1", [2048, 1024])
    h1t_out = ctx.out("h1T", [1024, 2048], BF16)
    rw_out = ctx.out("rw", [2048, 8])
    cand_out = ctx.out("cand", [2048, 8])
    lg_out = ctx.out("logits", [2048, 9])
    cnt_out = ctx.out("cnt", [1, 8])

    ctx.psum_pool()
    ctx.consts()
    wstA = ctx.sb("wstA", [128, 3072], F32)
    wstB = ctx.sb("wstB", [128, 3072], F32)
    wst = [(wstA, ctx.r["wstA"]), (wstB, ctx.r["wstB"])]
    mod = emit_mod(ctx, c_in, wmod_in, "mod0", sections=("ga", "sb", "cb1"), wst=wst)

    qwp, rqwp, pcol, rpcol = emit_quant_weight(ctx, "qp", wproj_in, 8, 1024, wst)
    gas = ctx.sb("gas", [128, 1024], F32)
    rgas = ctx.r["gas"]
    p.op("vector", lambda e: e.tensor_scalar(gas[:, :], ctx.t["mod0_ga"][:, :], pcol[:, 1:2], None, ALU.mult),
         reads=[mod["ga"][1], rpcol], writes=[rgas], tag="gas")

    yt = ctx.sb("yt", [128, 8 * 2048], BF16)
    ryt = ctx.r["yt"]
    for j in range(8):
        p.dma("sync", lambda e, j=j: e.dma_start(out=yt[:, j * 2048:(j + 1) * 2048], in_=yt_in[j * 128:(j + 1) * 128, :]),
              writes=[ryt], tag=f"ldy{j}")

    xbufs = [ctx.sb(f"xb{i}", [128, 1024], F32) for i in range(3)]
    rxbufs = [ctx.r[f"xb{i}"] for i in range(3)]
    tmpo = ctx.sb("tmpo", [128, 1024], F32)
    rtmpo = ctx.r["tmpo"]
    pwide, rpwide = ctx.t["pwide"], ctx.r["pwide"]

    def tile_src(t):
        b = t % 3
        p.dma("sync", lambda e, t=t, b=b: e.dma_start(out=xbufs[b][:, :], in_=x_in[t * 128:(t + 1) * 128, :]),
              writes=[rxbufs[b]], tag=f"ldx{t}")

        def mms(e, t=t):
            inst = None
            for hf in range(2):
                for c in range(8):
                    inst = e.matmul(pwide[:, hf * 512:(hf + 1) * 512],
                                    yt[:, c * 2048 + t * 128: c * 2048 + (t + 1) * 128],
                                    qwp[:, c * 1024 + hf * 512: c * 1024 + (hf + 1) * 512],
                                    start=(c == 0), stop=(c == 7))
            return inst
        p.op("tensor", mms, reads=[ryt, rqwp], writes=[rpwide], tag=f"op_mm{t}")
        p.op("vector", lambda e: e.tensor_mul(tmpo[:, :], pwide[:, :], gas[:, :]),
             reads=[rpwide, rgas], writes=[rtmpo], tag=f"op_sc{t}")
        p.op("vector", lambda e, b=b: e.tensor_add(xbufs[b][:, :], xbufs[b][:, :], tmpo[:, :]),
             reads=[rtmpo, rxbufs[b]], writes=[rxbufs[b]], tag=f"op_ad{t}")
        p.dma("sync", lambda e, t=t, b=b: e.dma_start(out=x1_out[t * 128:(t + 1) * 128, :], in_=xbufs[b][:, :]),
              reads=[rxbufs[b]], writes=[ctx.r["out:x1"]], tag=f"stx1_{t}")
        return xbufs[b][:, :], rxbufs[b]

    hT = ctx.sb("hT", [128, 8 * 2048], BF16)
    rhT = ctx.r["hT"]
    emit_ln_mod(ctx, "ln0b", tile_src, mod["cb1"], mod["sb"], hT, rhT)
    for j in range(8):
        p.dma("sync", lambda e, j=j: e.dma_start(out=h1t_out[j * 128:(j + 1) * 128, :], in_=hT[:, j * 2048:(j + 1) * 2048]),
              reads=[rhT], writes=[ctx.r["out:h1T"]], tag=f"sth{j}")

    gw_sb = ctx.sb("gw_sb", [128, 8 * 9], BF16)
    rgw_sb = ctx.r["gw_sb"]
    p.dma("gpsimd", lambda e: e.dma_start(out=gw_sb[:, :].rearrange("p (c n) -> p c n", c=8),
                                          in_=gatew_in.rearrange("(c p) n -> p c n", p=128)),
          writes=[rgw_sb], tag="ldgw")
    lg = ctx.sb("lg", [128, 16 * 9], F32)
    rlg = ctx.r["lg"]
    paux, rpaux = ctx.t["paux"], ctx.r["paux"]
    for t in range(16):
        def lmm(e, t=t):
            inst = None
            for c in range(8):
                inst = e.matmul(paux[:, 0:9], hT[:, c * 2048 + t * 128: c * 2048 + (t + 1) * 128],
                                gw_sb[:, c * 9:(c + 1) * 9], start=(c == 0), stop=(c == 7))
            return inst
        p.op("tensor", lmm, reads=[rhT, rgw_sb], writes=[rpaux], tag=f"lg_mm{t}")
        p.op("vector", lambda e, t=t: e.tensor_copy(lg[:, t * 9:(t + 1) * 9], paux[:, 0:9]),
             reads=[rpaux], writes=[rlg], tag=f"lg_ev{t}")
        p.dma("sync", lambda e, t=t: e.dma_start(out=lg_out[t * 128:(t + 1) * 128, :], in_=lg[:, t * 9:(t + 1) * 9]),
              reads=[rlg], writes=[ctx.r["out:logits"]], tag=f"stlg{t}")

    def dynv(tsr, off=0):
        return bass.AP(tsr, off, [[144, 128], [9, 16], [1, 8]])
    def s128(nm):
        h = ctx.sb(nm, [128, 128], F32)
        return h, ctx.r[nm]
    ms, rms = s128("ms")
    absS, rabsS = s128("absS")
    probs, rprobs = s128("probs")
    ex, rex = s128("ex")
    eqs, reqs = s128("eqs")
    scr, rscr = s128("scr")
    scr2, rscr2 = s128("scr2")
    rw_t, rrw = s128("rw_t")
    iота = None
    iot, riot = s128("iot")
    c16 = lambda nm: (ctx.sb(nm, [128, 16], F32), ctx.r[nm])
    thr, rthr = c16("thr")
    selv, rselv = c16("selv")
    lsum, rlsum = c16("lsum")
    csum, rcsum = c16("csum")
    dynk, rdynk = c16("dynk")
    t16, rt16 = c16("t16")
    mults = ctx.sb("mults", [128, 64], F32)
    rmults = ctx.r["mults"]
    selvs = ctx.sb("selvs", [128, 64], F32)
    rselvs = ctx.r["selvs"]

    def v3(tsr, n=8):
        return bass.AP(tsr, 0, [[128, 128], [8, 16], [1, n]])
    def b16(tsr):
        return bass.AP(tsr, 0, [[16, 128], [1, 16], [0, 8]])

    p.op("gpsimd", lambda e: e.iota(iot[:, :].rearrange("p (t e) -> p t e", t=16), [[0, 16], [1, 8]], channel_multiplier=0,
                                    allow_small_or_imprecise_dtypes=True),
         writes=[riot], tag="iota")
    p.op("vector", lambda e: e.tensor_copy(v3(ms), dynv(lg)), reads=[rlg], writes=[rms], tag="mscp")
    p.op("scalar", lambda e: e.activation(absS[:, :], ms[:, :], AF.Abs), reads=[rms], writes=[rabsS], tag="absS")
    p.op("vector", lambda e: e.tensor_reduce(thr[:, :], v3(ms), AX.X, ALU.max), reads=[rms], writes=[rthr], tag="p_max")
    p.op("vector", lambda e: e.tensor_tensor(v3(scr), v3(ms), b16(thr), ALU.subtract), reads=[rms, rthr], writes=[rscr], tag="p_sub")
    p.op("scalar", lambda e: e.activation(probs[:, :], scr[:, :], AF.Exp), reads=[rscr], writes=[rprobs], tag="p_exp")
    p.op("vector", lambda e: e.tensor_reduce(lsum[:, :], v3(probs), AX.X, ALU.add), reads=[rprobs], writes=[rlsum], tag="p_sum")
    p.op("vector", lambda e: e.reciprocal(t16[:, :], lsum[:, :]), reads=[rlsum], writes=[rt16], tag="p_rc")
    p.op("vector", lambda e: e.tensor_tensor(v3(probs), v3(probs), b16(t16), ALU.mult), reads=[rprobs, rt16], writes=[rprobs], tag="p_nm")

    p.op("gpsimd", lambda e: e.memset(rw_t[:, :], 0.0), writes=[rrw], tag="rw0")
    p.op("gpsimd", lambda e: e.memset(csum[:, :], 0.0), writes=[rcsum], tag="cs0")
    p.op("gpsimd", lambda e: e.memset(dynk[:, :], 1.0), writes=[rdynk], tag="dk0")

    for it in range(4):
        p.op("vector", lambda e: e.tensor_reduce(thr[:, :], v3(ms), AX.X, ALU.max), reads=[rms], writes=[rthr], tag=f"i{it}_max")
        p.op("vector", lambda e: e.tensor_tensor(v3(eqs), v3(ms), b16(thr), ALU.is_ge), reads=[rms, rthr], writes=[reqs], tag=f"i{it}_eq")
        p.op("vector", lambda e: e.scalar_tensor_tensor(scr[:, :], iot[:, :], 9.0, eqs[:, :], ALU.subtract, ALU.mult),
             reads=[riot, reqs], writes=[rscr], tag=f"i{it}_s1")
        p.op("vector", lambda e: e.tensor_scalar_add(scr2[:, :], scr[:, :], 9.0),
             reads=[rscr], writes=[rscr2], tag=f"i{it}_s2")
        p.op("vector", lambda e: e.tensor_reduce(selv[:, :], v3(scr2), AX.X, ALU.min), reads=[rscr2], writes=[rselv], tag=f"i{it}_argm")
        p.op("vector", lambda e, it=it: e.tensor_copy(
            bass.AP(selvs, it, [[64, 128], [4, 16]]), selv[:, :]), reads=[rselv], writes=[rselvs], tag=f"i{it}_selst")
        p.op("scalar", lambda e: e.activation(t16[:, :], thr[:, :], AF.Abs), reads=[rthr], writes=[rt16], tag=f"i{it}_athr")
        p.op("vector", lambda e: e.tensor_tensor(v3(scr), v3(absS), b16(t16), ALU.max), reads=[rabsS, rt16], writes=[rscr], tag=f"i{it}_fac")
        p.op("vector", lambda e: e.tensor_tensor(v3(scr2), b16(thr), dynv(lg), ALU.subtract), reads=[rthr, rlg], writes=[rscr2], tag=f"i{it}_d1")
        p.op("vector", lambda e: e.scalar_tensor_tensor(scr[:, :], scr[:, :], -0.2, scr2[:, :], ALU.mult, ALU.add),
             reads=[rscr, rscr2], writes=[rscr], tag=f"i{it}_d2")
        p.op("vector", lambda e: e.tensor_scalar(scr[:, :], scr[:, :], 0.0, None, ALU.is_gt), reads=[rscr], writes=[rscr], tag=f"i{it}_jm")
        p.op("vector", lambda e: e.scalar_tensor_tensor(scr2[:, :], scr[:, :], -1e30, ms[:, :], ALU.mult, ALU.add),
             reads=[rscr, rms], writes=[rscr2], tag=f"i{it}_mk")
        p.op("vector", lambda e: e.tensor_reduce(t16[:, :], v3(scr2), AX.X, ALU.max), reads=[rscr2], writes=[rt16], tag=f"i{it}_gm")
        p.op("vector", lambda e: e.tensor_tensor(v3(scr2), v3(scr2), b16(t16), ALU.subtract), reads=[rscr2, rt16], writes=[rscr2], tag=f"i{it}_gs")
        p.op("scalar", lambda e: e.activation(ex[:, :], scr2[:, :], AF.Exp), reads=[rscr2], writes=[rex], tag=f"i{it}_ge")
        p.op("vector", lambda e: e.tensor_reduce(lsum[:, :], v3(ex), AX.X, ALU.add), reads=[rex], writes=[rlsum], tag=f"i{it}_gl")
        p.op("vector", lambda e: e.tensor_tensor(v3(eqs), v3(iot), b16(selv), ALU.is_equal), reads=[riot, rselv], writes=[reqs], tag=f"i{it}_es")
        p.op("vector", lambda e: e.tensor_tensor(v3(scr), v3(ex), v3(eqs), ALU.mult), reads=[rex, reqs], writes=[rscr], tag=f"i{it}_gn")
        p.op("vector", lambda e: e.tensor_reduce(t16[:, :], v3(scr), AX.X, ALU.add), reads=[rscr], writes=[rt16], tag=f"i{it}_gn2")
        p.op("vector", lambda e: e.reciprocal(lsum[:, :], lsum[:, :]), reads=[rlsum], writes=[rlsum], tag=f"i{it}_gr")
        p.op("vector", lambda e: e.tensor_mul(t16[:, :], t16[:, :], lsum[:, :]), reads=[rt16, rlsum], writes=[rt16], tag=f"i{it}_mu")
        p.op("vector", lambda e, it=it: e.tensor_copy(
            bass.AP(mults, it, [[64, 128], [4, 16]]), t16[:, :]), reads=[rt16], writes=[rmults], tag=f"i{it}_mst")
        p.op("vector", lambda e: e.tensor_tensor(v3(scr), v3(probs), v3(eqs), ALU.mult), reads=[rprobs, reqs], writes=[rscr], tag=f"i{it}_pi")
        p.op("vector", lambda e: e.tensor_reduce(t16[:, :], v3(scr), AX.X, ALU.add), reads=[rscr], writes=[rt16], tag=f"i{it}_pi2")
        p.op("vector", lambda e: e.tensor_add(csum[:, :], csum[:, :], t16[:, :]), reads=[rcsum, rt16], writes=[rcsum], tag=f"i{it}_cs")
        if it < 3:
            p.op("vector", lambda e: e.tensor_scalar(t16[:, :], csum[:, :], 0.75, None, ALU.is_lt), reads=[rcsum], writes=[rt16], tag=f"i{it}_ck")
            p.op("vector", lambda e: e.tensor_add(dynk[:, :], dynk[:, :], t16[:, :]), reads=[rdynk, rt16], writes=[rdynk], tag=f"i{it}_dk")
        p.op("vector", lambda e: e.scalar_tensor_tensor(ms[:, :], eqs[:, :], -1e30, ms[:, :], ALU.mult, ALU.add),
             reads=[reqs, rms], writes=[rms], tag=f"i{it}_msu")

    for it in range(4):
        p.op("vector", lambda e, it=it: e.tensor_scalar(t16[:, :], dynk[:, :], float(-it), 0.0, ALU.add, ALU.is_gt),
             reads=[rdynk], writes=[rt16], tag=f"r{it}_v")
        p.op("vector", lambda e, it=it: e.tensor_mul(t16[:, :], t16[:, :],
             bass.AP(mults, it, [[64, 128], [4, 16]])), reads=[rt16, rmults], writes=[rt16], tag=f"r{it}_m")
        p.op("vector", lambda e, it=it: e.tensor_tensor(v3(eqs), v3(iot),
             bass.AP(selvs, it, [[64, 128], [4, 16], [0, 8]]), ALU.is_equal), reads=[riot, rselvs], writes=[reqs], tag=f"r{it}_e")
        p.op("vector", lambda e: e.tensor_tensor(v3(scr), v3(eqs), b16(t16), ALU.mult), reads=[reqs, rt16], writes=[rscr], tag=f"r{it}_x")
        p.op("vector", lambda e: e.tensor_add(rw_t[:, :], rw_t[:, :], scr[:, :]), reads=[rrw, rscr], writes=[rrw], tag=f"r{it}_a")
    p.op("vector", lambda e: e.tensor_reduce(t16[:, :], v3(rw_t), AX.X, ALU.add), reads=[rrw], writes=[rt16], tag="rw_s")
    p.op("vector", lambda e: e.reciprocal(t16[:, :], t16[:, :]), reads=[rt16], writes=[rt16], tag="rw_r")
    p.op("vector", lambda e: e.tensor_tensor(v3(rw_t), v3(rw_t), b16(t16), ALU.mult), reads=[rrw, rt16], writes=[rrw], tag="rw_n")
    p.op("vector", lambda e: e.tensor_scalar(scr[:, :], rw_t[:, :], 0.0, None, ALU.is_gt), reads=[rrw], writes=[rscr], tag="cand")
    p.op("tensor", lambda e: e.matmul(paux[0:1, 0:128], ctx.t["c_ones"][:, 0:1], scr[:, :], start=True, stop=True),
         reads=[rscr, ctx.r["c_ones"]], writes=[rpaux], tag="cnt_mm")
    cnt_sb = ctx.sb("cnt_sb", [1, 8], F32)
    rcnt = ctx.r["cnt_sb"]
    p.op("vector", lambda e: e.tensor_reduce(cnt_sb[:, :], bass.AP(paux, 0, [[512, 1], [1, 8], [8, 16]]), AX.X, ALU.add),
         reads=[rpaux], writes=[rcnt], tag="cnt_rd")

    for t in range(16):
        p.dma("sync", lambda e, t=t: e.dma_start(out=rw_out[t * 128:(t + 1) * 128, :], in_=rw_t[:, t * 8:(t + 1) * 8]),
              reads=[rrw], writes=[ctx.r["out:rw"]], tag=f"strw{t}")
        p.dma("sync", lambda e, t=t: e.dma_start(out=cand_out[t * 128:(t + 1) * 128, :], in_=scr[:, t * 8:(t + 1) * 8]),
              reads=[rscr], writes=[ctx.r["out:cand"]], tag=f"stcd{t}")
    p.dma("sync", lambda e: e.dma_start(out=cnt_out, in_=cnt_sb[:, :]), reads=[rcnt], writes=[ctx.r["out:cnt"]], tag="stcnt")

    p.build(nc)
    ctx.close()
    nc.reset()
    return ctx


def build_l0_moe():
    ctx = Ctx("l0moe")
    nc, p = ctx.nc, ctx.p
    x_in = ctx.inp("x1", [2048, 1024])
    h_in = ctx.inp("h1T", [1024, 2048], BF16)
    rw_in = ctx.inp("rw", [2048, 8])
    cand_in = ctx.inp("cand", [2048, 8])
    lg_in = ctx.inp("logits", [2048, 9])
    cnts_in = ctx.inp("cnts", [8, 8])
    pmask_in = ctx.inp("pmask", [8, 1])
    dg_in = ctx.inp("dgT", [8, 1024, 512])
    du_in = ctx.inp("duT", [8, 1024, 512])
    dd_in = ctx.inp("ddT", [8, 512, 1024])
    sg_in = ctx.inp("sgT", [1024, 2048])
    su_in = ctx.inp("suT", [1024, 2048])
    sd_in = ctx.inp("sdT", [2048, 1024])
    c_in = ctx.inp("c_row", [1, 1024])
    wmod_in = ctx.inp("wmod", [1024, 6144])
    x2_out = ctx.out("x2", [2048, 1024])

    ctx.psum_pool()
    ctx.consts()
    wstA = ctx.sb("wstA", [128, 1536], F32)
    wstB = ctx.sb("wstB", [128, 1536], F32)
    wst = [(wstA, ctx.r["wstA"]), (wstB, ctx.r["wstB"])]
    mod = emit_mod(ctx, c_in, wmod_in, "mod0", sections=("gb",), wst=wst)
    gb_t, rgb = mod["gb"]
    paux, rpaux = ctx.t["paux"], ctx.r["paux"]
    pwide, rpwide = ctx.t["pwide"], ctx.r["pwide"]
    pmm = [ctx.t[f"pmm{i}"] for i in range(4)]
    rpmm = [ctx.r[f"pmm{i}"] for i in range(4)]

    xsb = ctx.sb("xsb", [128, 16 * 1024], F32)
    rx = ctx.r["xsb"]
    for t in range(16):
        p.dma("sync", lambda e, t=t: e.dma_start(out=xsb[:, t * 1024:(t + 1) * 1024], in_=x_in[t * 128:(t + 1) * 128, :]),
              writes=[rx], tag=f"ldx{t}")
    hT = ctx.sb("hT", [128, 8 * 2048], BF16)
    rhT = ctx.r["hT"]
    for j in range(8):
        p.dma("sync", lambda e, j=j: e.dma_start(out=hT[:, j * 2048:(j + 1) * 2048], in_=h_in[j * 128:(j + 1) * 128, :]),
              writes=[rhT], tag=f"ldh{j}")

    rw_t = ctx.sb("rw_t", [128, 128], F32)
    rrw = ctx.r["rw_t"]
    cnd = ctx.sb("cnd", [128, 128], F32)
    rcnd = ctx.r["cnd"]
    cndb = ctx.sb("cndb", [128, 128], BF16)
    rcndb = ctx.r["cndb"]
    lg9 = ctx.sb("lg9", [128, 144], F32)
    rlg9 = ctx.r["lg9"]
    for t in range(16):
        p.dma("sync", lambda e, t=t: e.dma_start(out=rw_t[:, t * 8:(t + 1) * 8], in_=rw_in[t * 128:(t + 1) * 128, :]),
              writes=[rrw], tag=f"ldrw{t}")
        p.dma("sync", lambda e, t=t: e.dma_start(out=cnd[:, t * 8:(t + 1) * 8], in_=cand_in[t * 128:(t + 1) * 128, :]),
              writes=[rcnd], tag=f"ldcd{t}")
        p.dma("sync", lambda e, t=t: e.dma_start(out=lg9[:, t * 9:(t + 1) * 9], in_=lg_in[t * 128:(t + 1) * 128, :]),
              writes=[rlg9], tag=f"ldlg{t}")
    p.op("vector", lambda e: e.tensor_copy(cndb[:, :], cnd[:, :]), reads=[rcnd], writes=[rcndb], tag="cndb")

    cnts_sb = ctx.sb("cnts_sb", [8, 8], F32)
    rcnts = ctx.r["cnts_sb"]
    pmask_sb = ctx.sb("pmask_sb", [8, 1], F32)
    rpmask = ctx.r["pmask_sb"]
    p.dma("sync", lambda e: e.dma_start(out=cnts_sb[:, :], in_=cnts_in), writes=[rcnts], tag="ldcn")
    p.dma("sync", lambda e: e.dma_start(out=pmask_sb[:, :], in_=pmask_in), writes=[rpmask], tag="ldpm")
    carry = ctx.sb("carry", [1, 8], F32)
    rcarry = ctx.r["carry"]
    p.op("tensor", lambda e: e.matmul(paux[0:1, 0:8], pmask_sb[:, :], cnts_sb[:, :], start=True, stop=True),
         reads=[rpmask, rcnts], writes=[rpaux], tag="ofs_mm")
    p.op("vector", lambda e: e.tensor_copy(carry[:, :], paux[0:1, 0:8]), reads=[rpaux], writes=[rcarry], tag="ofs_ev")

    ut = ctx.sb("ut", [128, 128], BF16)
    rut = ctx.r["ut"]
    p.op("gpsimd", lambda e: e.memset(ut[:, :], 1.0), writes=[rut], tag="ut0")
    p.op("gpsimd", lambda e: e.affine_select(out=ut[:, :], in_=ut[:, :], compare_op=ALU.is_ge,
                                             fill=0.0, base=0, pattern=[[1, 128]], channel_multiplier=-1),
         reads=[rut], writes=[rut], tag="ut1")

    kept = ctx.sb("kept", [128, 128], F32)
    rkept = ctx.r["kept"]
    carryb = ctx.sb("carryb", [128, 8], F32)
    rcarryb = ctx.r["carryb"]
    ttot = ctx.sb("ttot", [1, 8], F32)
    rttot = ctx.r["ttot"]
    cum = ctx.sb("cum", [128, 8], F32)
    rcum = ctx.r["cum"]
    for t in range(16):
        p.op("tensor", lambda e: e.matmul(paux[:, 8:16], ctx.t["c_ones"][0:1, :], carry[:, :], start=True, stop=True),
             reads=[rcarry, ctx.r["c_ones"]], writes=[rpaux], tag=f"cp_cb{t}")
        p.op("vector", lambda e: e.tensor_copy(carryb[:, :], paux[:, 8:16]), reads=[rpaux], writes=[rcarryb], tag=f"cp_cbe{t}")
        p.op("tensor", lambda e, t=t: e.matmul(paux[:, 0:8], ut[:, :], cndb[:, t * 8:(t + 1) * 8], start=True, stop=True),
             reads=[rut, rcndb], writes=[rpaux], tag=f"cp_mm{t}")
        p.op("vector", lambda e: e.tensor_add(cum[:, :], paux[:, 0:8], carryb[:, :]), reads=[rpaux, rcarryb], writes=[rcum], tag=f"cp_cum{t}")
        p.op("vector", lambda e: e.tensor_scalar(cum[:, :], cum[:, :], 2560.0, None, ALU.is_le), reads=[rcum], writes=[rcum], tag=f"cp_le{t}")
        p.op("vector", lambda e, t=t: e.tensor_tensor(kept[:, t * 8:(t + 1) * 8], cum[:, :], cnd[:, t * 8:(t + 1) * 8], ALU.mult),
             reads=[rcum, rcnd], writes=[rkept], tag=f"cp_k{t}")
        p.op("tensor", lambda e, t=t: e.matmul(paux[0:1, 0:8], ctx.t["c_ones_b"][:, 0:1], cndb[:, t * 8:(t + 1) * 8], start=True, stop=True),
             reads=[rcndb, ctx.r["c_ones_b"]], writes=[rpaux], tag=f"cp_tt{t}")
        p.op("vector", lambda e: e.tensor_copy(ttot[:, :], paux[0:1, 0:8]), reads=[rpaux], writes=[rttot], tag=f"cp_tte{t}")
        p.op("vector", lambda e: e.tensor_add(carry[:, :], carry[:, :], ttot[:, :]), reads=[rcarry, rttot], writes=[rcarry], tag=f"cp_ca{t}")

    msk9 = ctx.sb("msk9", [128, 144], F32)
    rmsk9 = ctx.r["msk9"]
    p.op("gpsimd", lambda e: e.memset(msk9[:, :], 1.0), writes=[rmsk9], tag="m9s")
    p.op("vector", lambda e: e.tensor_copy(bass.AP(msk9, 0, [[144, 128], [9, 16], [1, 8]]),
                                           bass.AP(kept, 0, [[128, 128], [8, 16], [1, 8]])),
         reads=[rkept], writes=[rmsk9], tag="m9c")
    glw = ctx.sb("glw", [128, 144], F32)
    rglw = ctx.r["glw"]
    t16b = ctx.sb("t16b", [128, 16], F32)
    rt16b = ctx.r["t16b"]

    def v9(tsr):
        return bass.AP(tsr, 0, [[144, 128], [9, 16], [1, 9]])
    def b169(tsr):
        return bass.AP(tsr, 0, [[16, 128], [1, 16], [0, 9]])
    def v8(tsr):
        return bass.AP(tsr, 0, [[144, 128], [9, 16], [1, 8]])
    p.op("vector", lambda e: e.tensor_scalar(glw[:, :], msk9[:, :], -1.0, 1e30, ALU.add, ALU.mult),
         reads=[rmsk9], writes=[rglw], tag="gw0")
    p.op("vector", lambda e: e.tensor_add(glw[:, :], glw[:, :], lg9[:, :]), reads=[rglw, rlg9], writes=[rglw], tag="gw1")
    p.op("vector", lambda e: e.tensor_reduce(t16b[:, :], v9(glw), AX.X, ALU.max), reads=[rglw], writes=[rt16b], tag="gw2")
    p.op("vector", lambda e: e.tensor_tensor(v9(glw), v9(glw), b169(t16b), ALU.subtract), reads=[rglw, rt16b], writes=[rglw], tag="gw3")
    p.op("scalar", lambda e: e.activation(glw[:, :], glw[:, :], AF.Exp), reads=[rglw], writes=[rglw], tag="gw4")
    p.op("vector", lambda e: e.tensor_reduce(t16b[:, :], v9(glw), AX.X, ALU.add), reads=[rglw], writes=[rt16b], tag="gw5")
    p.op("vector", lambda e: e.reciprocal(t16b[:, :], t16b[:, :]), reads=[rt16b], writes=[rt16b], tag="gw6")
    p.op("vector", lambda e: e.tensor_tensor(v9(glw), v9(glw), b169(t16b), ALU.mult), reads=[rglw, rt16b], writes=[rglw], tag="gw7")
    p.op("vector", lambda e: e.tensor_reduce(t16b[:, :], v8(glw), AX.X, ALU.add), reads=[rglw], writes=[rt16b], tag="gw8")
    p.op("vector", lambda e: e.tensor_tensor(v8(glw), bass.AP(rw_t, 0, [[128, 128], [8, 16], [1, 8]]),
                                             bass.AP(t16b, 0, [[16, 128], [1, 16], [0, 8]]), ALU.mult),
         reads=[rrw, rt16b], writes=[rglw], tag="gw9")
    p.op("vector", lambda e: e.tensor_tensor(v8(glw), v8(glw), bass.AP(kept, 0, [[128, 128], [8, 16], [1, 8]]), ALU.mult),
         reads=[rglw, rkept], writes=[rglw], tag="gw10")

    gbb = ctx.sb("gbb", [128, 1024], BF16)
    rgbb = ctx.r["gbb"]
    p.op("vector", lambda e: e.tensor_copy(gbb[:, :], gb_t[:, :]), reads=[rgb], writes=[rgbb], tag="gbb")

    gbuf = ctx.sb("gbuf", [128, 8 * 512], BF16)
    rgbuf = ctx.r["gbuf"]
    ubuf = ctx.sb("ubuf", [128, 8 * 512], BF16)
    rubuf = ctx.r["ubuf"]
    dbuf = ctx.sb("dbuf", [128, 4 * 1024], BF16)
    rdbuf = ctx.r["dbuf"]
    act = ctx.sb("act", [128, 4 * 2048], BF16)
    ract = ctx.r["act"]
    sgt = ctx.sb("sgt", [128, 512], BF16)
    rsgt = ctx.r["sgt"]

    iters = []
    for e_ in range(8):
        iters.append(dict(I=512, gsrc=dg_in[e_], usrc=du_in[e_], dsrc=dd_in[e_],
                          drow0=0, gcol=e_, nm=f"d{e_}"))
    for hf_ in range(4):
        iters.append(dict(I=512, gsrc=sg_in[:, hf_ * 512:(hf_ + 1) * 512],
                          usrc=su_in[:, hf_ * 512:(hf_ + 1) * 512],
                          dsrc=sd_in[hf_ * 512:(hf_ + 1) * 512, :],
                          drow0=0, gcol=8, nm=f"s{hf_}"))

    for it in iters:
        I, nm = it["I"], it["nm"]
        nj = I // 128
        for c in range(8):
            p.dma("gpsimd", lambda e, c=c, it=it, I=I: e.dma_start(
                out=gbuf[:, c * I:(c + 1) * I], in_=it["gsrc"][c * 128:(c + 1) * 128, :]),
                writes=[rgbuf], tag=f"{nm}_lg{c}")
            p.dma("gpsimd", lambda e, c=c, it=it, I=I: e.dma_start(
                out=ubuf[:, c * I:(c + 1) * I], in_=it["usrc"][c * 128:(c + 1) * 128, :]),
                writes=[rubuf], tag=f"{nm}_lu{c}")
        for j in range(nj):
            p.dma("gpsimd", lambda e, j=j, it=it: e.dma_start(
                out=dbuf[:, j * 1024:(j + 1) * 1024], in_=it["dsrc"][j * 128:(j + 1) * 128, :]),
                writes=[rdbuf], tag=f"{nm}_ld{j}")
            p.op("vector", lambda e, j=j: e.tensor_mul(dbuf[:, j * 1024:(j + 1) * 1024],
                                                       dbuf[:, j * 1024:(j + 1) * 1024], gbb[:, :]),
                 reads=[rdbuf, rgbb], writes=[rdbuf], tag=f"{nm}_fd{j}")
        for j in range(nj):
            for tg in range(4):
                bg, bu = tg % 2, 2 + tg % 2

                def gmm(e, j=j, tg=tg, bg=bg, I=I):
                    inst = None
                    for c in range(8):
                        inst = e.matmul(pmm[bg][:, :], gbuf[:, c * I + j * 128: c * I + (j + 1) * 128],
                                        hT[:, c * 2048 + tg * 512: c * 2048 + (tg + 1) * 512],
                                        start=(c == 0), stop=(c == 7))
                    return inst
                p.op("tensor", gmm, reads=[rgbuf, rhT], writes=[rpmm[bg]], tag=f"{nm}_gm{j}_{tg}")

                def umm(e, j=j, tg=tg, bu=bu, I=I):
                    inst = None
                    for c in range(8):
                        inst = e.matmul(pmm[bu][:, :], ubuf[:, c * I + j * 128: c * I + (j + 1) * 128],
                                        hT[:, c * 2048 + tg * 512: c * 2048 + (tg + 1) * 512],
                                        start=(c == 0), stop=(c == 7))
                    return inst
                p.op("tensor", umm, reads=[rubuf, rhT], writes=[rpmm[bu]], tag=f"{nm}_um{j}_{tg}")
                p.op("scalar", lambda e, bg=bg: e.activation(sgt[:, :], pmm[bg][:, :], AF.Silu),
                     reads=[rpmm[bg]], writes=[rsgt], tag=f"{nm}_si{j}_{tg}")
                p.op("vector", lambda e, j=j, tg=tg, bu=bu: e.tensor_tensor(
                    act[:, j * 2048 + tg * 512: j * 2048 + (tg + 1) * 512], sgt[:, :], pmm[bu][:, :], ALU.mult),
                    reads=[rsgt, rpmm[bu]], writes=[ract], tag=f"{nm}_ac{j}_{tg}")
        gcol = it["gcol"]
        for t in range(16):
            def dmm(e, t=t, nj=nj):
                inst = None
                for hf in range(2):
                    for j in range(nj):
                        inst = e.matmul(pwide[:, hf * 512:(hf + 1) * 512],
                                        act[:, j * 2048 + t * 128: j * 2048 + (t + 1) * 128],
                                        dbuf[:, j * 1024 + hf * 512: j * 1024 + (hf + 1) * 512],
                                        start=(j == 0), stop=(j == nj - 1))
                return inst
            p.op("tensor", dmm, reads=[ract, rdbuf], writes=[rpwide], tag=f"{nm}_dm{t}")
            for hf in range(2):
                p.op("vector", lambda e, t=t, hf=hf, gcol=gcol: e.scalar_tensor_tensor(
                    xsb[:, t * 1024 + hf * 512: t * 1024 + (hf + 1) * 512],
                    pwide[:, hf * 512:(hf + 1) * 512], glw[:, t * 9 + gcol: t * 9 + gcol + 1],
                    xsb[:, t * 1024 + hf * 512: t * 1024 + (hf + 1) * 512], ALU.mult, ALU.add),
                    reads=[rpwide, rglw, rx], writes=[rx], tag=f"{nm}_cb{t}_{hf}")

    for t in range(16):
        p.dma("sync", lambda e, t=t: e.dma_start(out=x2_out[t * 128:(t + 1) * 128, :], in_=xsb[:, t * 1024:(t + 1) * 1024]),
              reads=[rx], writes=[ctx.r["out:x2"]], tag=f"stx2_{t}")

    p.build(nc)
    ctx.close()
    nc.reset()
    return ctx


def emit_mamba(ctx, x_in, hin_src, mod, win_in, winpad_in, dtw_in, dtb_in, alog_in,
               dpar_in, include_y, normw_in=None, wout_in=None, x3_out=None,
               hfin_out=None):
    p, nc = ctx.p, ctx.nc
    paux, rpaux = ctx.t["paux"], ctx.r["paux"]
    pwide, rpwide = ctx.t["pwide"], ctx.r["pwide"]
    pmm = [ctx.t[f"pmm{i}"] for i in range(4)]
    rpmm = [ctx.r[f"pmm{i}"] for i in range(4)]
    ptrb, rptrb = ctx.t["ptrb"], ctx.r["ptrb"]

    dtw = ctx.sb("dtw", [16, 16], F32)
    rdtw = ctx.r["dtw"]
    p.dma("sync", lambda e: e.dma_start(out=dtw[:, :], in_=dtw_in), writes=[rdtw], tag="lddtw")
    dtb = ctx.sb("m_dtb", [16, 1], F32)
    rdtb = ctx.r["m_dtb"]
    p.dma("sync", lambda e: e.dma_start(out=dtb[:, :], in_=dtb_in), writes=[rdtb], tag="lddtb")
    acol = ctx.sb("acol", [16, 1], F32)
    racol = ctx.r["acol"]
    p.dma("sync", lambda e: e.dma_start(out=acol[:, :], in_=alog_in), writes=[racol], tag="ldal")
    p.op("scalar", lambda e: e.activation(acol[:, :], acol[:, :], AF.Exp), reads=[racol], writes=[racol], tag="aexp")
    p.op("vector", lambda e: e.tensor_scalar_mul(acol[:, :], acol[:, :], -1.0), reads=[racol], writes=[racol], tag="aneg")
    if include_y:
        drep = ctx.sb("drep", [128, 16], F32)
        rdrep = ctx.r["drep"]
        dprow = ctx.sb("dprow", [1, 16], F32)
        rdprow = ctx.r["dprow"]
        p.dma("sync", lambda e: e.dma_start(out=dprow[:, :], in_=dpar_in), writes=[rdprow], tag="lddp")
        p.op("tensor", lambda e: e.matmul(paux[:, 0:16], ctx.t["c_ones"][0:1, :], dprow[:, :], start=True, stop=True),
             reads=[rdprow, ctx.r["c_ones"]], writes=[rpaux], tag="dp_bc")
        p.op("vector", lambda e: e.tensor_copy(drep[:, :], paux[:, 0:16]), reads=[rpaux], writes=[rdrep], tag="dp_ev")
        nwb = ctx.sb("nwb", [128, 1024], F32)
        rnwb = ctx.r["nwb"]
        nwrow = ctx.sb("nwrow", [1, 1024], F32)
        rnwrow = ctx.r["nwrow"]
        p.dma("sync", lambda e: e.dma_start(out=nwrow[:, :], in_=normw_in), writes=[rnwrow], tag="ldnw")

        def nwbc(e):
            e.matmul(pwide[:, 0:512], ctx.t["c_ones"][0:1, :], nwrow[:, 0:512], start=True, stop=True)
            return e.matmul(pwide[:, 512:1024], ctx.t["c_ones"][0:1, :], nwrow[:, 512:1024], start=True, stop=True)
        p.op("tensor", nwbc, reads=[rnwrow, ctx.r["c_ones"]], writes=[rpwide], tag="nw_bc")
        p.op("vector", lambda e: e.tensor_copy(nwb[:, :], pwide[:, :]), reads=[rpwide], writes=[rnwb], tag="nw_ev")
        wout = ctx.sb("wout", [128, 8 * 1024], BF16)
        rwout = ctx.r["wout"]
        gab = ctx.sb("gab", [128, 1024], BF16)
        rgab = ctx.r["gab"]
        p.op("vector", lambda e: e.tensor_copy(gab[:, :], ctx.t["mod1_ga"][:, :]), reads=[mod["ga"][1]], writes=[rgab], tag="gab")
        for c in range(8):
            p.dma("gpsimd", lambda e, c=c: e.dma_start(out=wout[:, c * 1024:(c + 1) * 1024], in_=wout_in[c * 128:(c + 1) * 128, :]),
                  writes=[rwout], tag=f"ldwo{c}")
            p.op("vector", lambda e, c=c: e.tensor_mul(wout[:, c * 1024:(c + 1) * 1024], wout[:, c * 1024:(c + 1) * 1024], gab[:, :]),
                 reads=[rwout, rgab], writes=[rwout], tag=f"fwo{c}")

    selh1 = ctx.sb("selh1", [16, 128], F32)
    rselh1 = ctx.r["selh1"]
    selg = ctx.sb("selg", [16, 4 * 128], F32)
    rselg = ctx.r["selg"]
    p.op("gpsimd", lambda e: e.memset(selg[:, :], 1.0), writes=[rselg], tag="selg0")
    for g in range(4):
        for r in range(4):
            p.op("gpsimd", lambda e, g=g, r=r: e.affine_select(
                out=selg[:, g * 128 + r * 32: g * 128 + (r + 1) * 32], in_=selg[:, g * 128 + r * 32: g * 128 + (r + 1) * 32],
                compare_op=ALU.is_equal, fill=0.0, base=-(g * 4 + r), pattern=[[0, 32]], channel_multiplier=1),
                reads=[rselg], writes=[rselg], tag=f"selg{g}_{r}")

    state = ctx.sb("state", [128, 256], F32)
    rstate = ctx.r["state"]
    hin_src(state, rstate)
    stateb = ctx.sb("stateb", [128, 256], BF16)
    rstateb = ctx.r["stateb"]
    p.op("vector", lambda e: e.tensor_copy(stateb[:, :], state[:, :]), reads=[rstate], writes=[rstateb], tag="stb0")
    sscr = ctx.sb("sscr", [128, 256], F32)
    rsscr = ctx.r["sscr"]

    TPP = 1024
    h2T = ctx.sb("m_h2T", [128, 8 * TPP], BF16)
    rh2T = ctx.r["m_h2T"]
    xp = ctx.sb("m_xp", [128, 8 * 1024], BF16)
    rxp = ctx.r["m_xp"]
    btok = ctx.sb("m_btok", [128, 8 * 256], BF16)
    rbtok = ctx.r["m_btok"]
    wsb = [ctx.sb(f"m_ws{i}", [128, 8 * 512], BF16) for i in range(2)]
    rwsb = [ctx.r[f"m_ws{i}"] for i in range(2)]
    dtl = ctx.sb("m_dtl", [16, TPP], F32)
    rdtl = ctx.r["m_dtl"]
    dtT = ctx.sb("m_dtT", [16, TPP], F32)
    rdtT = ctx.r["m_dtT"]
    aT = ctx.sb("m_aT", [16, TPP], F32)
    raT = ctx.r["m_aT"]
    sp1, rsp1 = dtl, rdtl
    a_tok = ctx.sb("m_atok", [128, 8 * 16], F32)
    ratok = ctx.r["m_atok"]
    dt_tok = ctx.sb("m_dttok", [128, 8 * 16], F32)
    rdttok = ctx.r["m_dttok"]
    expa = ctx.sb("m_expa", [128, 16], F32)
    rexpa = ctx.r["m_expa"]
    cdpad = ctx.sb("m_cdpad", [128, 8 * 4], F32)
    rcdpad = ctx.r["m_cdpad"]
    alrow = ctx.sb("m_alrow", [1, 16], F32)
    ralrow = ctx.r["m_alrow"]
    wxf = ctx.sb("m_wxf", [128, 16], F32)
    rwxf = ctx.r["m_wxf"]
    bdec = ctx.sb("m_bdec", [128, 256], BF16)
    rbdec = ctx.r["m_bdec"]
    if include_y:
        z = ctx.sb("m_z", [128, 8 * 1024], BF16)
        rz = ctx.r["m_z"]
        xdt = ctx.sb("m_xdt", [128, 8 * 1024], BF16)
        rxdt = ctx.r["m_xdt"]
        bTp = ctx.sb("m_bTp", [128, 4 * TPP], BF16)
        rbTp = ctx.r["m_bTp"]
        cTp = ctx.sb("m_cTp", [128, 4 * TPP], BF16)
        rcTp = ctx.r["m_cTp"]
        ysb, rysb = h2T, rh2T
        expD = ctx.sb("m_expD", [128, 128], BF16)
        rexpD = ctx.r["m_expD"]
        dscr = ctx.sb("m_dscr", [128, 128], F32)
        rdscr = ctx.r["m_dscr"]
        pt2 = ctx.sb("m_pt2", [128, 128], BF16)
        rpt2 = ctx.r["m_pt2"]
        yw1 = ctx.sb("m_yw1", [128, 64], F32)
        ryw1 = ctx.r["m_yw1"]


        y3T, ry3T = xdt, rxdt

    eps6, reps6 = ctx.const_col(1e-6)
    nh_c, rnh = ctx.const_col(-0.5)

    for pa in range(2):
        tok0 = pa * TPP
        tile0 = pa * 8
        def tsrc(tl, pa=pa):
            t = pa * 8 + tl
            b = t % 2
            bufs = wsb
            return None
        if pa == 0:
            lnx = [ctx.sb(f"m_lnx{i}", [128, 1024], F32) for i in range(2)]
            rlnx = [ctx.r[f"m_lnx{i}"] for i in range(2)]
            xstr, rxstr = lnx, rlnx

        def tile_src(tl, pa=pa):
            t = pa * 8 + tl
            b = tl % 2
            p.dma("sync", lambda e, t=t, b=b: e.dma_start(out=lnx[b][:, :], in_=x_in[t * 128:(t + 1) * 128, :]),
                  writes=[rlnx[b]], tag=f"m{pa}_ldx{t}")
            return lnx[b][:, :], rlnx[b]
        emit_ln_mod(ctx, f"mln{pa}", tile_src, mod["ca1"], mod["sa"], h2T, rh2T, n_tiles=8)

        secs = [("xp", xp, rxp, 0, 1024, BF16)]
        if include_y:
            secs.append(("z", z, rz, 1024, 1024, BF16))
        secs.append(("bt", btok, rbtok, 2048, 256, BF16))
        for nm, dst, rdst, col0, width, dt_ in secs:
            nw5 = (width + 511) // 512
            for w5 in range(nw5):
                wcols = min(512, width - w5 * 512)
                b = w5 % 2
                for c in range(8):
                    p.dma("gpsimd", lambda e, c=c, b=b, col0=col0, w5=w5, wcols=wcols: e.dma_start(
                        out=wsb[b][:, c * 512: c * 512 + wcols],
                        in_=win_in[c * 128:(c + 1) * 128, col0 + w5 * 512: col0 + w5 * 512 + wcols]),
                        writes=[rwsb[b]], tag=f"m{pa}_{nm}w{w5}_{c}")
                for tl in range(8):
                    bank = tl % 2

                    def tmm(e, tl=tl, b=b, bank=bank, wcols=wcols):
                        inst = None
                        for c in range(8):
                            inst = e.matmul(pmm[bank][:, 0:wcols], h2T[:, c * TPP + tl * 128: c * TPP + (tl + 1) * 128],
                                            wsb[b][:, c * 512: c * 512 + wcols], start=(c == 0), stop=(c == 7))
                        return inst
                    p.op("tensor", tmm, reads=[rh2T, rwsb[b]], writes=[rpmm[bank]], tag=f"m{pa}_{nm}m{w5}_{tl}")
                    p.op("scalar", lambda e, tl=tl, bank=bank, w5=w5, wcols=wcols, dst=dst, width=width: e.activation(
                        dst[:, tl * width + w5 * 512: tl * width + w5 * 512 + wcols], pmm[bank][:, 0:wcols], AF.Identity),
                        reads=[rpmm[bank]], writes=[rdst], tag=f"m{pa}_{nm}e{w5}_{tl}")

        if include_y:
            for nm, dst, rdst, pcol0 in (("bp", bTp, rbTp, 0), ("cp", cTp, rcTp, 512)):
                for oc in range(4):
                    b = oc % 2
                    for c in range(8):
                        p.dma("gpsimd", lambda e, c=c, b=b, pcol0=pcol0, oc=oc: e.dma_start(
                            out=wsb[b][:, c * 512: c * 512 + 128],
                            in_=winpad_in[c * 128:(c + 1) * 128, pcol0 + oc * 128: pcol0 + (oc + 1) * 128]),
                            writes=[rwsb[b]], tag=f"m{pa}_{nm}w{oc}_{c}")
                    for tg in range(2):
                        bank = 2 + tg

                        def fmm(e, b=b, tg=tg, bank=bank):
                            inst = None
                            for c in range(8):
                                inst = e.matmul(pmm[bank][:, :], wsb[b][:, c * 512: c * 512 + 128],
                                                h2T[:, c * TPP + tg * 512: c * TPP + (tg + 1) * 512],
                                                start=(c == 0), stop=(c == 7))
                            return inst
                        p.op("tensor", fmm, reads=[rh2T, rwsb[b]], writes=[rpmm[bank]], tag=f"m{pa}_{nm}m{oc}_{tg}")
                        p.op("scalar", lambda e, oc=oc, tg=tg, bank=bank, dst=dst: e.activation(
                            dst[:, oc * TPP + tg * 512: oc * TPP + (tg + 1) * 512], pmm[bank][:, :], AF.Identity),
                            reads=[rpmm[bank]], writes=[rdst], tag=f"m{pa}_{nm}e{oc}_{tg}")

        for tg in range(2):
            b = tg % 2
            p.dma("gpsimd", lambda e, b=b: e.dma_start(out=wsb[b][:, 0:16 * 8].rearrange("p (c n) -> p c n", c=8),
                                                       in_=win_in.rearrange("(c p) n -> p c n", p=128)[:, :, 2560:2576]),
                  writes=[rwsb[b]], tag=f"m{pa}_dtw{tg}")

            def dtmm(e, tg=tg, b=b):
                inst = None
                for c in range(8):
                    inst = e.matmul(paux[0:16, 0:512], wsb[b][:, c * 16:(c + 1) * 16],
                                    h2T[:, c * TPP + tg * 512: c * TPP + (tg + 1) * 512],
                                    start=(c == 0), stop=(c == 7))
                return inst
            p.op("tensor", dtmm, reads=[rh2T, rwsb[b]], writes=[rpaux], tag=f"m{pa}_dtm{tg}")
            p.op("vector", lambda e, tg=tg: e.tensor_copy(dtl[:, tg * 512:(tg + 1) * 512], paux[0:16, 0:512]),
                 reads=[rpaux], writes=[rdtl], tag=f"m{pa}_dte{tg}")
        for tg in range(2):
            p.op("tensor", lambda e, tg=tg: e.matmul(paux[0:16, 0:512], dtw[:, :], dtl[:, tg * 512:(tg + 1) * 512],
                                                     start=True, stop=True),
                 reads=[rdtw, rdtl], writes=[rpaux], tag=f"m{pa}_dtwm{tg}")
            p.op("scalar", lambda e, tg=tg: e.activation(dtT[:, tg * 512:(tg + 1) * 512], paux[0:16, 0:512], AF.Identity, bias=dtb[:, :]),
                 reads=[rpaux, rdtb], writes=[rdtT], tag=f"m{pa}_dtb{tg}")
        p.op("scalar", lambda e: e.activation(sp1[:, :], dtT[:, :], AF.Abs), reads=[rdtT], writes=[rsp1], tag=f"m{pa}_spa")
        nm1, rnm1 = ctx.const_col(-1.0)
        one1, rone1 = ctx.const_col(1.0)
        p.op("scalar", lambda e: e.activation(sp1[:, :], sp1[:, :], AF.Exp, scale=nm1[0:16, :]), reads=[rsp1, rnm1], writes=[rsp1], tag=f"m{pa}_spe")
        p.op("scalar", lambda e: e.activation(sp1[:, :], sp1[:, :], AF.Ln, bias=one1[0:16, :]), reads=[rsp1, rone1], writes=[rsp1], tag=f"m{pa}_spl")
        p.op("vector", lambda e: e.tensor_scalar_max(dtT[:, :], dtT[:, :], 0.0), reads=[rdtT], writes=[rdtT], tag=f"m{pa}_spm")
        p.op("vector", lambda e: e.tensor_add(dtT[:, :], dtT[:, :], sp1[:, :]), reads=[rdtT, rsp1], writes=[rdtT], tag=f"m{pa}_spadd")
        p.op("vector", lambda e: e.tensor_scalar(sp1[:, :], dtT[:, :], acol[:, 0:1], None, ALU.mult),
             reads=[rdtT, racol], writes=[rsp1], tag=f"m{pa}_dA")
        for cc in range(8):
            p.op("vector", lambda e, cc=cc: e.tensor_tensor_scan(aT[:, cc * 128:(cc + 1) * 128], sp1[:, cc * 128:(cc + 1) * 128],
                                                                 sp1[:, cc * 128:(cc + 1) * 128], 0.0, ALU.add, ALU.bypass),
                 reads=[rsp1], writes=[raT], tag=f"m{pa}_cs{cc}")
        def tras(e):
            inst = None
            for cc in range(8):
                inst = e.transpose(paux[:, cc * 16:(cc + 1) * 16], aT[:, cc * 128:(cc + 1) * 128], ctx.t["c_ident"][0:16, 0:16])
            return inst
        p.op("tensor", tras, reads=[raT, ctx.r["c_ident"]], writes=[rpaux], tag=f"m{pa}_atr")
        p.op("vector", lambda e: e.tensor_copy(a_tok[:, :], paux[:, 0:128]), reads=[rpaux], writes=[ratok], tag=f"m{pa}_atre")

        def trds(e):
            inst = None
            for cc in range(8):
                inst = e.transpose(paux[:, cc * 16:(cc + 1) * 16], dtT[:, cc * 128:(cc + 1) * 128], ctx.t["c_ident"][0:16, 0:16])
            return inst
        p.op("tensor", trds, reads=[rdtT, ctx.r["c_ident"]], writes=[rpaux], tag=f"m{pa}_dtr")
        p.op("vector", lambda e: e.tensor_copy(dt_tok[:, :], paux[:, 0:128]), reads=[rpaux], writes=[rdttok], tag=f"m{pa}_dtre")

        if pa == 0:
            cdT_t = ctx.sb("m_cdT", [16, 8], F32)
        cdT = ctx.t["m_cdT"]
        rcdT = ctx.r["m_cdT"]
        for cc in range(8):
            p.op("scalar", lambda e, cc=cc: e.activation(cdT[:, cc:cc + 1], aT[:, cc * 128 + 127: cc * 128 + 128], AF.Exp),
                 reads=[raT], writes=[rcdT], tag=f"m{pa}_cd{cc}")

            def cdmm(e, cc=cc):
                inst = None
                for g in range(4):
                    inst = e.matmul(paux[:, cc * 4 + g: cc * 4 + g + 1], selg[:, g * 128:(g + 1) * 128],
                                    cdT[:, cc:cc + 1], start=True, stop=True)
                return inst
            p.op("tensor", cdmm, reads=[rcdT, rselg], writes=[rpaux], tag=f"m{pa}_cdm{cc}")
        p.op("vector", lambda e: e.tensor_copy(cdpad[:, :], paux[:, 0:32]), reads=[rpaux], writes=[rcdpad], tag=f"m{pa}_cde")

        if include_y:
            for cc in range(8):
                p.op("vector", lambda e, cc=cc: e.tensor_tensor(
                    xdt[:, cc * 1024:(cc + 1) * 1024].rearrange("p (h d) -> p h d", h=16),
                    xp[:, cc * 1024:(cc + 1) * 1024].rearrange("p (h d) -> p h d", h=16),
                    bass.AP(dt_tok, cc * 16, [[128, 128], [1, 16], [0, 64]]), ALU.mult),
                    reads=[rxp, rdttok], writes=[rxdt], tag=f"m{pa}_xdt{cc}")

        for cc in range(8):
            def altr(e, cc=cc):
                return e.transpose(paux[0:1, 0:16], cdT[:, cc:cc + 1], ctx.t["c_ident"][0:16, 0:16])
            def altr2(e, cc=cc):
                return e.transpose(paux[0:1, 0:16], aT[:, cc * 128 + 127: cc * 128 + 128], ctx.t["c_ident"][0:16, 0:16])
            p.op("tensor", altr2, reads=[raT, ctx.r["c_ident"]], writes=[rpaux], tag=f"m{pa}c{cc}_alt")
            p.op("vector", lambda e: e.tensor_copy(alrow[:, :], paux[0:1, 0:16]), reads=[rpaux], writes=[ralrow], tag=f"m{pa}c{cc}_ale")
            p.op("tensor", lambda e: e.matmul(paux[:, 32:48], ctx.t["c_ones"][0:1, :], alrow[:, :], start=True, stop=True),
                 reads=[ralrow, ctx.r["c_ones"]], writes=[rpaux], tag=f"m{pa}c{cc}_alb")
            p.op("vector", lambda e, cc=cc: e.tensor_tensor(wxf[:, :], paux[:, 32:48], a_tok[:, cc * 16:(cc + 1) * 16], ALU.subtract),
                 reads=[rpaux, ratok], writes=[rwxf], tag=f"m{pa}c{cc}_ws")
            p.op("scalar", lambda e: e.activation(wxf[:, :], wxf[:, :], AF.Exp), reads=[rwxf], writes=[rwxf], tag=f"m{pa}c{cc}_we")
            p.op("vector", lambda e, cc=cc: e.tensor_mul(wxf[:, :], wxf[:, :], dt_tok[:, cc * 16:(cc + 1) * 16]),
                 reads=[rwxf, rdttok], writes=[rwxf], tag=f"m{pa}c{cc}_wd")
            p.op("vector", lambda e, cc=cc: e.tensor_tensor(
                bdec[:, :].rearrange("p (h n) -> p h n", h=16),
                btok[:, cc * 256:(cc + 1) * 256].rearrange("p (h n) -> p h n", h=16),
                bass.AP(wxf, 0, [[16, 128], [1, 16], [0, 16]]), ALU.mult),
                reads=[rbtok, rwxf], writes=[rbdec], tag=f"m{pa}c{cc}_bd")
            if include_y:
                p.op("scalar", lambda e, cc=cc: e.activation(expa[:, :], a_tok[:, cc * 16:(cc + 1) * 16], AF.Exp),
                     reads=[ratok], writes=[rexpa], tag=f"m{pa}c{cc}_ea")

            for h in range(16):
                hb = (h % 4) * 32
                hg = h // 4
                if include_y:
                    sb_ = h % 2
                    p.op("gpsimd", lambda e: e.memset(selh1[:, :], 1.0), writes=[rselh1], tag=f"m{pa}c{cc}h{h}_sh0")
                    p.op("gpsimd", lambda e, h=h: e.affine_select(out=selh1[:, :], in_=selh1[:, :], compare_op=ALU.is_equal,
                                                                  fill=0.0, base=-h, pattern=[[0, 128]], channel_multiplier=1),
                         reads=[rselh1], writes=[rselh1], tag=f"m{pa}c{cc}h{h}_sh1")
                    p.op("tensor", lambda e, cc=cc: e.matmul(paux[:, 128:256], selh1[:, :],
                                                             aT[:, cc * 128:(cc + 1) * 128], start=True, stop=True),
                         reads=[raT, rselh1], writes=[rpaux], tag=f"m{pa}c{cc}h{h}_rb")
                    p.op("vector", lambda e, cc=cc, h=h: e.tensor_single_scalar(dscr[:, :], paux[:, 128:256],
                                                                                a_tok[:, cc * 16 + h: cc * 16 + h + 1], ALU.subtract),
                         reads=[rpaux, ratok], writes=[rdscr], tag=f"m{pa}c{cc}h{h}_d")
                    p.op("scalar", lambda e: e.activation(expD[:, :], dscr[:, :], AF.Exp), reads=[rdscr], writes=[rexpD], tag=f"m{pa}c{cc}h{h}_ex")
                    p.op("gpsimd", lambda e: e.affine_select(out=expD[:, :], in_=expD[:, :], compare_op=ALU.is_ge,
                                                             fill=0.0, base=0, pattern=[[1, 128]], channel_multiplier=-1),
                         reads=[rexpD], writes=[rexpD], tag=f"m{pa}c{cc}h{h}_msk")
                    tp = (hb, 0) if hb == 96 else None

                    def smm(e, cc=cc, h=h, hb=hb, sb_=sb_, tp=tp):
                        return e.matmul(pmm[sb_][:, 0:128], bTp[hb:hb + 32, (h // 4) * TPP + cc * 128: (h // 4) * TPP + (cc + 1) * 128],
                                        cTp[hb:hb + 32, (h // 4) * TPP + cc * 128: (h // 4) * TPP + (cc + 1) * 128],
                                        start=True, stop=True, tile_position=tp)
                    p.op("tensor", smm, reads=[rbTp, rcTp], writes=[rpmm[sb_]], tag=f"m{pa}c{cc}h{h}_s")
                    p.op("vector", lambda e, sb_=sb_: e.tensor_tensor(pt2[:, :], pmm[sb_][:, 0:128], expD[:, :], ALU.mult),
                         reads=[rpmm[sb_], rexpD], writes=[rpt2], tag=f"m{pa}c{cc}h{h}_p")
                    p.op("tensor", lambda e, cc=cc, h=h: e.matmul(pmm[2][:, (h % 8) * 64:(h % 8) * 64 + 64], pt2[:, :],
                                                                  xdt[:, cc * 1024 + h * 64: cc * 1024 + (h + 1) * 64],
                                                                  start=True, stop=True),
                         reads=[rpt2, rxdt], writes=[rpmm[2]], tag=f"m{pa}c{cc}h{h}_yi")
                    tpc = (hb, 0) if hb == 96 else None

                    def ycm(e, cc=cc, h=h, hb=hb, hg=hg, tpc=tpc):
                        return e.matmul(pmm[3][:, (h % 8) * 64:(h % 8) * 64 + 64],
                                        cTp[hb:hb + 32, hg * TPP + cc * 128: hg * TPP + (cc + 1) * 128],
                                        stateb[hb:hb + 32, hg * 64:(hg + 1) * 64], start=True, stop=True, tile_position=tpc)
                    p.op("tensor", ycm, reads=[rcTp, rstateb], writes=[rpmm[3]], tag=f"m{pa}c{cc}h{h}_yc")
                    p.op("vector", lambda e, h=h: e.tensor_scalar(yw1[:, :], pmm[3][:, (h % 8) * 64:(h % 8) * 64 + 64],
                                                                  expa[:, h:h + 1], None, ALU.mult),
                         reads=[rpmm[3], rexpa], writes=[ryw1], tag=f"m{pa}c{cc}h{h}_e1")
                    p.op("vector", lambda e, h=h: e.tensor_add(yw1[:, :], yw1[:, :], pmm[2][:, (h % 8) * 64:(h % 8) * 64 + 64]),
                         reads=[ryw1, rpmm[2]], writes=[ryw1], tag=f"m{pa}c{cc}h{h}_e2")
                    p.op("vector", lambda e, cc=cc, h=h: e.scalar_tensor_tensor(
                        ysb[:, cc * 1024 + h * 64: cc * 1024 + (h + 1) * 64],
                        xp[:, cc * 1024 + h * 64: cc * 1024 + (h + 1) * 64], drep[:, h:h + 1], yw1[:, :],
                        ALU.mult, ALU.add),
                        reads=[rxp, rdrep, ryw1], writes=[rysb], tag=f"m{pa}c{cc}h{h}_e3")
                tps = (0, hb) if hb == 96 else None

                def csm(e, cc=cc, h=h, hb=hb, hg=hg, tps=tps):
                    return e.matmul(pwide[hb:hb + 16, hg * 64:(hg + 1) * 64], bdec[:, h * 16:(h + 1) * 16],
                                    xp[:, cc * 1024 + h * 64: cc * 1024 + (h + 1) * 64], start=True, stop=True,
                                    tile_position=tps)
                p.op("tensor", csm, reads=[rbdec, rxp], writes=[rpwide], tag=f"m{pa}c{cc}h{h}_cs")
            p.op("vector", lambda e, cc=cc: e.tensor_tensor(
                sscr[:, :].rearrange("p (g d) -> p g d", g=4),
                state[:, :].rearrange("p (g d) -> p g d", g=4),
                bass.AP(cdpad, cc * 4, [[32, 128], [1, 4], [0, 64]]), ALU.mult),
                reads=[rstate, rcdpad], writes=[rsscr], tag=f"m{pa}c{cc}_su1")
            p.op("vector", lambda e: e.tensor_add(state[:, :], sscr[:, :], pwide[:, 0:256]),
                 reads=[rsscr, rpwide], writes=[rstate], tag=f"m{pa}c{cc}_su2")
            if include_y:
                p.op("vector", lambda e: e.tensor_copy(stateb[:, :], state[:, :]),
                     reads=[rstate], writes=[rstateb], tag=f"m{pa}c{cc}_su3")

        if include_y:
            if pa == 0:
                rms_m = ctx.sb("m_rm", [128, 1], F32)
            rrms = ctx.r["m_rm"]
            sqt, rsqt = ctx.sb_shared("_lns_sq", [128, 1024], F32)
            for tl in range(8):
                t = pa * 8 + tl
                p.op("scalar", lambda e, tl=tl: e.activation(sqt[:, :], ysb[:, tl * 1024:(tl + 1) * 1024], AF.Square, accum_out=rms_m[:, :]),
                     reads=[rysb], writes=[rsqt, rrms], tag=f"m{pa}_rq{tl}")
                p.op("scalar", lambda e: e.activation(rms_m[:, :], rms_m[:, :], AF.Ln, bias=eps6[:, :], scale=1.0 / 1024),
                     reads=[rrms, reps6], writes=[rrms], tag=f"m{pa}_rl{tl}")
                p.op("scalar", lambda e: e.activation(rms_m[:, :], rms_m[:, :], AF.Exp, scale=nh_c[:, :]),
                     reads=[rrms, rnh], writes=[rrms], tag=f"m{pa}_re{tl}")
                p.op("scalar", lambda e, tl=tl: e.activation(sqt[:, :], ysb[:, tl * 1024:(tl + 1) * 1024], AF.Identity, scale=rms_m[:, :]),
                     reads=[rysb, rrms], writes=[rsqt], tag=f"m{pa}_rs{tl}")
                p.op("vector", lambda e: e.tensor_mul(sqt[:, :], sqt[:, :], nwb[:, :]), reads=[rsqt, rnwb], writes=[rsqt], tag=f"m{pa}_rn{tl}")
                p.op("scalar", lambda e, tl=tl: e.activation(z[:, tl * 1024:(tl + 1) * 1024], z[:, tl * 1024:(tl + 1) * 1024], AF.Silu),
                     reads=[rz], writes=[rz], tag=f"m{pa}_zs{tl}")
                p.op("vector", lambda e, tl=tl: e.tensor_tensor(z[:, tl * 1024:(tl + 1) * 1024], sqt[:, :],
                                                                z[:, tl * 1024:(tl + 1) * 1024], ALU.mult),
                     reads=[rsqt, rz], writes=[rz], tag=f"m{pa}_y3{tl}")

                def y3tr(e, tl=tl):
                    inst = None
                    for j in range(8):
                        inst = e.transpose(ptrb[:, j * 128:(j + 1) * 128],
                                           z[:, tl * 1024 + j * 128: tl * 1024 + (j + 1) * 128], ctx.t["c_ident_b"][:, :])
                    return inst
                p.op("tensor", y3tr, reads=[rz, ctx.r["c_ident_b"]], writes=[rptrb], tag=f"m{pa}_ytr{tl}")
                p.op("vector", lambda e, tl=tl: e.tensor_copy(
                    bass.AP(y3T, tl * 128, [[8 * TPP, 128], [TPP, 8], [1, 128]]),
                    ptrb[:, :].rearrange("p (j q) -> p j q", j=8)),
                    reads=[rptrb], writes=[ry3T], tag=f"m{pa}_yte{tl}")
            for tl in range(8):
                t = pa * 8 + tl
                b = tl % 2
                p.dma("sync", lambda e, t=t, b=b: e.dma_start(out=xstr[b][:, :], in_=x_in[t * 128:(t + 1) * 128, :]),
                      writes=[rxstr[b]], tag=f"m{pa}_lx{t}")

                def omm(e, tl=tl):
                    inst = None
                    for hf in range(2):
                        for c in range(8):
                            inst = e.matmul(pwide[:, 512 + hf * 256: 512 + (hf + 1) * 256]
                                            if False else pwide[:, hf * 512:(hf + 1) * 512],
                                            y3T[:, c * TPP + tl * 128: c * TPP + (tl + 1) * 128],
                                            wout[:, c * 1024 + hf * 512: c * 1024 + (hf + 1) * 512],
                                            start=(c == 0), stop=(c == 7))
                    return inst
                p.op("tensor", omm, reads=[ry3T, rwout], writes=[rpwide], tag=f"m{pa}_om{tl}")
                p.op("vector", lambda e, b=b: e.tensor_add(xstr[b][:, :], xstr[b][:, :], pwide[:, :]),
                     reads=[rpwide, rxstr[b]], writes=[rxstr[b]], tag=f"m{pa}_oa{tl}")
                p.dma("sync", lambda e, t=t, b=b: e.dma_start(out=x3_out[t * 128:(t + 1) * 128, :], in_=xstr[b][:, :]),
                      reads=[rxstr[b]], writes=[ctx.r["out:x3"]], tag=f"m{pa}_sx{t}")

    if hfin_out is not None:
        p.dma("sync", lambda e: e.dma_start(out=hfin_out, in_=state[:, :]), reads=[rstate],
              writes=[ctx.r["out:hfin"]], tag="sthf")


def build_l1_mamba(include_y):
    ctx = Ctx("mamba")
    nc, p = ctx.nc, ctx.p
    x_in = ctx.inp("x2", [2048, 1024])
    win_in = ctx.inp("winT", [1024, 2576])
    winpad_in = ctx.inp("winpadT", [1024, 1024])
    dtw_in = ctx.inp("dtwT", [16, 16])
    dtb_in = ctx.inp("dtb", [16, 1])
    alog_in = ctx.inp("alog", [16, 1])
    dpar_in = ctx.inp("dpar", [1, 16])
    c_in = ctx.inp("c_row", [1, 1024])
    wmod_in = ctx.inp("wmod", [1024, 6144])
    if include_y:
        normw_in = ctx.inp("normw", [1, 1024])
        wout_in = ctx.inp("woutT", [1024, 1024])
        hin_in = ctx.inp("hin", [128, 256])
        x3_out = ctx.out("x3", [2048, 1024])
        hfin_out = None
    else:
        normw_in = wout_in = None
        x3_out = None
        hfin_out = ctx.out("hfin", [128, 256])

    ctx.psum_pool()
    ctx.consts()
    wstA = ctx.sb("wstA", [128, 512], F32)
    wstB = ctx.sb("wstB", [128, 512], F32)
    wst = [(wstA, ctx.r["wstA"]), (wstB, ctx.r["wstB"])]
    secs = ("sa", "ca1", "ga") if include_y else ("sa", "ca1")
    mod = emit_mod(ctx, c_in, wmod_in, "mod1", sections=secs, wst=wst)

    if include_y:
        def hin_src(state, rstate):
            p.dma("sync", lambda e: e.dma_start(out=state[:, :], in_=hin_in), writes=[rstate], tag="ldhin")
    else:
        def hin_src(state, rstate):
            p.op("gpsimd", lambda e: e.memset(state[:, :], 0.0), writes=[rstate], tag="st0")

    emit_mamba(ctx, x_in, hin_src, mod, win_in, winpad_in, dtw_in, dtb_in, alog_in,
               dpar_in, include_y, normw_in, wout_in, x3_out, hfin_out)

    p.build(nc)
    ctx.close()
    nc.reset()
    return ctx


def emit_h3_tiles(ctx, tag, x_in, mod, cb):
    p = ctx.p
    lnx = [ctx.sb(f"{tag}_x{i}", [128, 1024], F32) for i in range(2)]
    rlnx = [ctx.r[f"{tag}_x{i}"] for i in range(2)]
    ca1_t, rca1 = mod["cb1"]
    sa_t, rsa = mod["sb"]
    mean, rmean = ctx.sb_shared("_lns_mean", [128, 1], F32)
    sq, rsq = ctx.sb_shared("_lns_sq", [128, 1024], F32)
    ssq, rssq = ctx.sb_shared("_lns_ssq", [128, 1], F32)
    var, rvar = ctx.sb_shared("_lns_var", [128, 1], F32)
    rstd, rrstd = ctx.sb_shared("_lns_rstd", [128, 1], F32)
    nmr, rnmr = ctx.sb_shared("_lns_nmr", [128, 1], F32)
    xn, rxn = ctx.sb_shared("_lns_xn", [128, 1024], F32)
    eps_c, reps = ctx.const_col(1e-5)
    nh_c, rnh = ctx.const_col(-0.5)
    for t in range(16):
        b = t % 2
        p.dma("sync", lambda e, t=t, b=b: e.dma_start(out=lnx[b][:, :], in_=x_in[t * 128:(t + 1) * 128, :]),
              writes=[rlnx[b]], tag=f"{tag}_ld{t}")
        xt = lnx[b][:, :]
        rx = rlnx[b]
        p.op("vector", lambda e, xt=xt: e.tensor_reduce(mean[:, :], xt, AX.X, ALU.add), reads=[rx], writes=[rmean], tag=f"{tag}_m{t}")
        p.op("vector", lambda e: e.tensor_scalar_mul(mean[:, :], mean[:, :], 1.0 / 1024), reads=[rmean], writes=[rmean], tag=f"{tag}_ms{t}")
        p.op("scalar", lambda e, xt=xt: e.activation(sq[:, :], xt, AF.Square, accum_out=ssq[:, :]), reads=[rx], writes=[rsq, rssq], tag=f"{tag}_sq{t}")
        p.op("vector", lambda e: e.scalar_tensor_tensor(var[:, :], mean[:, :], -1.0, mean[:, :], ALU.mult, ALU.mult), reads=[rmean], writes=[rvar], tag=f"{tag}_v1{t}")
        p.op("vector", lambda e: e.scalar_tensor_tensor(var[:, :], ssq[:, :], 1.0 / 1024, var[:, :], ALU.mult, ALU.add), reads=[rssq, rvar], writes=[rvar], tag=f"{tag}_v2{t}")
        p.op("scalar", lambda e: e.activation(rstd[:, :], var[:, :], AF.Ln, bias=eps_c[:, :]), reads=[rvar, reps], writes=[rrstd], tag=f"{tag}_l{t}")
        p.op("scalar", lambda e: e.activation(rstd[:, :], rstd[:, :], AF.Exp, scale=nh_c[:, :]), reads=[rrstd, rnh], writes=[rrstd], tag=f"{tag}_e{t}")
        p.op("vector", lambda e: e.scalar_tensor_tensor(nmr[:, :], mean[:, :], -1.0, rstd[:, :], ALU.mult, ALU.mult), reads=[rmean, rrstd], writes=[rnmr], tag=f"{tag}_n{t}")
        p.op("scalar", lambda e, xt=xt: e.activation(xn[:, :], xt, AF.Identity, bias=nmr[:, :], scale=rstd[:, :]), reads=[rx, rnmr, rrstd], writes=[rxn], tag=f"{tag}_xn{t}")
        p.op("vector", lambda e: e.tensor_mul(xn[:, :], xn[:, :], ca1_t[:, :]), reads=[rxn, rca1], writes=[rxn], tag=f"{tag}_sc{t}")
        p.op("vector", lambda e: e.tensor_add(xn[:, :], xn[:, :], sa_t[:, :]), reads=[rxn, rsa], writes=[rxn], tag=f"{tag}_ad{t}")
        cb(t, xn[:, :], rxn)


def build_l1_max1():
    ctx = Ctx("ffnm1")
    nc, p = ctx.nc, ctx.p
    x_in = ctx.inp("x3", [2048, 1024])
    c_in = ctx.inp("c_row", [1, 1024])
    wmod_in = ctx.inp("wmod", [1024, 6144])
    mx_out = ctx.out("mx1", [1, 1])
    ctx.psum_pool()
    ctx.consts()
    mod = emit_mod(ctx, c_in, wmod_in, "mod1", sections=("sb", "cb1"))
    amax = ctx.sb("amax", [128, 1], F32)
    ramax = ctx.r["amax"]
    tcol = ctx.sb("tcol", [128, 1], F32)
    rtcol = ctx.r["tcol"]
    p.op("gpsimd", lambda e: e.memset(amax[:, :], 0.0), writes=[ramax], tag="am0")

    def cb(t, h3, rh3):
        p.op("vector", lambda e, h3=h3: e.tensor_reduce(tcol[:, :], h3, AX.X, ALU.max, apply_absolute_value=True),
             reads=[rh3], writes=[rtcol], tag=f"am_r{t}")
        p.op("vector", lambda e: e.tensor_max(amax[:, :], amax[:, :], tcol[:, :]), reads=[ramax, rtcol], writes=[ramax], tag=f"am_m{t}")
    emit_h3_tiles(ctx, "m1", x_in, mod, cb)
    paux, rpaux = ctx.t["paux"], ctx.r["paux"]
    p.op("tensor", lambda e: e.transpose(paux[0:1, 0:128], amax[:, :], ctx.t["c_ident"][:, :]),
         reads=[ramax, ctx.r["c_ident"]], writes=[rpaux], tag="am_tr")
    mx = ctx.sb("mx", [1, 1], F32)
    rmx = ctx.r["mx"]
    p.op("vector", lambda e: e.tensor_reduce(mx[:, :], paux[0:1, 0:128], AX.X, ALU.max), reads=[rpaux], writes=[rmx], tag="am_f")
    p.dma("sync", lambda e: e.dma_start(out=mx_out, in_=mx[:, :]), reads=[rmx], writes=[ctx.r["out:mx1"]], tag="stmx")
    p.build(nc)
    ctx.close()
    nc.reset()
    return ctx


def emit_gmax_scale(ctx, tag, maxes_ap, clip_ap, out_names):
    p = ctx.p
    m8 = ctx.sb(tag + "_m8", [8, 1], F32)
    rm8 = ctx.r[tag + "_m8"]
    p.dma("sync", lambda e: e.dma_start(out=m8[:, :], in_=maxes_ap), writes=[rm8], tag=tag + "_ld")
    clip = ctx.sb(tag + "_cl", [1, 1], F32)
    rclip = ctx.r[tag + "_cl"]
    p.dma("sync", lambda e: e.dma_start(out=clip[:, :], in_=clip_ap), writes=[rclip], tag=tag + "_lc")
    paux, rpaux = ctx.t["paux"], ctx.r["paux"]
    p.op("tensor", lambda e: e.transpose(paux[0:1, 0:8], m8[:, :], ctx.t["c_ident"][0:8, 0:8]),
         reads=[rm8, ctx.r["c_ident"]], writes=[rpaux], tag=tag + "_tr")
    s = ctx.sb(tag + "_s", [1, 1], F32)
    rs = ctx.r[tag + "_s"]
    p.op("vector", lambda e: e.tensor_reduce(s[:, :], paux[0:1, 0:8], AX.X, ALU.max), reads=[rpaux], writes=[rs], tag=tag + "_mx")
    p.op("scalar", lambda e: e.activation(clip[:, :], clip[:, :], AF.Sigmoid), reads=[rclip], writes=[rclip], tag=tag + "_sg")
    p.op("vector", lambda e: e.tensor_mul(s[:, :], s[:, :], clip[:, :]), reads=[rs, rclip], writes=[rs], tag=tag + "_m1")
    p.op("vector", lambda e: e.tensor_scalar_mul(s[:, :], s[:, :], 1.0 / 127.0), reads=[rs], writes=[rs], tag=tag + "_m2")
    rqs = ctx.sb(tag + "_rq", [1, 1], F32)
    rrqs = ctx.r[tag + "_rq"]
    p.op("vector", lambda e: e.reciprocal(rqs[:, :], s[:, :]), reads=[rs], writes=[rrqs], tag=tag + "_rc")

    def bc(e):
        e.matmul(paux[:, 0:1], ctx.t["c_ones"][0:1, :], rqs[:, :], start=True, stop=True)
        return e.matmul(paux[:, 1:2], ctx.t["c_ones"][0:1, :], s[:, :], start=True, stop=True)
    p.op("tensor", bc, reads=[rrqs, rs, ctx.r["c_ones"]], writes=[rpaux], tag=tag + "_bc")
    cols = ctx.sb(tag + "_cols", [128, 2], F32)
    rcols = ctx.r[tag + "_cols"]
    p.op("vector", lambda e: e.tensor_copy(cols[:, :], paux[:, 0:2]), reads=[rpaux], writes=[rcols], tag=tag + "_bce")
    return cols, rcols


def build_l1_gateup():
    ctx = Ctx("ffngu")
    nc, p = ctx.nc, ctx.p
    x_in = ctx.inp("x3", [2048, 1024])
    maxes_in = ctx.inp("maxes1", [8, 1])
    clipx_in = ctx.inp("clip_x", [1, 1])
    wg_in = ctx.inp("w_gate", [2816, 1024])
    wu_in = ctx.inp("w_up", [2816, 1024])
    c_in = ctx.inp("c_row", [1, 1024])
    wmod_in = ctx.inp("wmod", [1024, 6144])
    hact_out = ctx.out("hact", [2816, 2048], BF16)
    mx_out = ctx.out("mx2", [1, 1])
    ctx.psum_pool()
    ctx.consts()
    mod = emit_mod(ctx, c_in, wmod_in, "mod1", sections=("sb", "cb1"))
    scx, rscx = emit_gmax_scale(ctx, "sx", maxes_in, clipx_in, None)

    qxT = ctx.sb("qxT", [128, 8 * 2048], BF16)
    rqxT = ctx.r["qxT"]
    qt_t = ctx.sb("qt_t", [128, 1024], F32)
    rqt = ctx.r["qt_t"]
    qb_t = ctx.sb("qb_t", [128, 1024], BF16)
    rqb = ctx.r["qb_t"]
    ptrb, rptrb = ctx.t["ptrb"], ctx.r["ptrb"]

    def cb(t, h3, rh3):
        p.op("vector", lambda e, h3=h3: e.tensor_scalar(qt_t[:, :], h3, scx[:, 0:1], RND_C, ALU.mult, ALU.add),
             reads=[rh3, rscx], writes=[rqt], tag=f"qx1_{t}")
        p.op("vector", lambda e: e.tensor_scalar(qt_t[:, :], qt_t[:, :], RND_C, 127.0, ALU.subtract, ALU.min),
             reads=[rqt], writes=[rqt], tag=f"qx2_{t}")
        p.op("vector", lambda e: e.tensor_scalar_max(qb_t[:, :], qt_t[:, :], -128.0), reads=[rqt], writes=[rqb], tag=f"qx3_{t}")

        def trs(e):
            inst = None
            for j in range(8):
                inst = e.transpose(ptrb[:, j * 128:(j + 1) * 128], qb_t[:, j * 128:(j + 1) * 128], ctx.t["c_ident_b"][:, :])
            return inst
        p.op("tensor", trs, reads=[rqb, ctx.r["c_ident_b"]], writes=[rptrb], tag=f"qxt_{t}")
        p.op("vector", lambda e, t=t: e.tensor_copy(
            bass.AP(qxT, t * 128, [[8 * 2048, 128], [2048, 8], [1, 128]]),
            ptrb[:, :].rearrange("p (j q) -> p j q", j=8)),
            reads=[rptrb], writes=[rqxT], tag=f"qxe_{t}")
    emit_h3_tiles(ctx, "gu", x_in, mod, cb)

    wst = [ctx.sb(f"wq{i}", [128, 1024], F32) for i in range(2)]
    rwst = [ctx.r[f"wq{i}"] for i in range(2)]
    sw = ctx.sb("sw", [128, 2], F32)
    rsw = ctx.r["sw"]
    swq = ctx.sb("swq", [128, 2], F32)
    rswq = ctx.r["swq"]
    gq = ctx.sb("gq", [128, 2 * 1024], BF16)
    rgq = ctx.r["gq"]
    hbuf = ctx.sb("hbuf", [128, 2048], BF16)
    rhbuf = ctx.r["hbuf"]
    sgt = ctx.sb("sgt", [128, 512], F32)
    rsgt = ctx.r["sgt"]
    amax = ctx.sb("amax", [128, 1], F32)
    ramax = ctx.r["amax"]
    tcol = ctx.sb("tcol", [128, 1], F32)
    rtcol = ctx.r["tcol"]
    p.op("gpsimd", lambda e: e.memset(amax[:, :], 0.0), writes=[ramax], tag="am0")
    pmm = [ctx.t[f"pmm{i}"] for i in range(4)]
    rpmm = [ctx.r[f"pmm{i}"] for i in range(4)]
    paux, rpaux = ctx.t["paux"], ctx.r["paux"]
    ptr2, rptr2 = ctx.t["ptrb"], ctx.r["ptrb"]

    for ic in range(22):
        for wi, (w_in, coln) in enumerate(((wg_in, 0), (wu_in, 1))):
            b = wi
            p.dma("sync", lambda e, ic=ic, b=b, w_in=w_in: e.dma_start(out=wst[b][:, :], in_=w_in[ic * 128:(ic + 1) * 128, :]),
                  writes=[rwst[b]], tag=f"g{ic}_ld{wi}")
            p.op("vector", lambda e, b=b, coln=coln: e.tensor_reduce(sw[:, coln:coln + 1], wst[b][:, :], AX.X, ALU.max, apply_absolute_value=True),
                 reads=[rwst[b]], writes=[rsw], tag=f"g{ic}_rm{wi}")
            p.op("vector", lambda e, coln=coln: e.tensor_scalar_mul(sw[:, coln:coln + 1], sw[:, coln:coln + 1], 1.0 / 127.0),
                 reads=[rsw], writes=[rsw], tag=f"g{ic}_sm{wi}")
            p.op("vector", lambda e, coln=coln: e.reciprocal(swq[:, coln:coln + 1], sw[:, coln:coln + 1]),
                 reads=[rsw], writes=[rswq], tag=f"g{ic}_rc{wi}")
            p.op("vector", lambda e, b=b, coln=coln: e.tensor_scalar(wst[b][:, :], wst[b][:, :], swq[:, coln:coln + 1], RND_C, ALU.mult, ALU.add),
                 reads=[rwst[b], rswq], writes=[rwst[b]], tag=f"g{ic}_q1{wi}")
            p.op("vector", lambda e, b=b: e.tensor_scalar(wst[b][:, :], wst[b][:, :], RND_C, 127.0, ALU.subtract, ALU.min),
                 reads=[rwst[b]], writes=[rwst[b]], tag=f"g{ic}_q2{wi}")
            p.op("vector", lambda e, b=b: e.tensor_scalar_max(qt_t[:, :], wst[b][:, :], -128.0),
                 reads=[rwst[b]], writes=[rqt], tag=f"g{ic}_q3{wi}")
            p.op("vector", lambda e: e.tensor_copy(qb_t[:, :], qt_t[:, :]), reads=[rqt], writes=[rqb], tag=f"g{ic}_q4{wi}")

            def trw(e, wi=wi):
                inst = None
                for j in range(8):
                    inst = e.transpose(ptr2[:, j * 128:(j + 1) * 128], qb_t[:, j * 128:(j + 1) * 128], ctx.t["c_ident_b"][:, :])
                return inst
            p.op("tensor", trw, reads=[rqb, ctx.r["c_ident_b"]], writes=[rptr2], tag=f"g{ic}_tr{wi}")
            p.op("vector", lambda e, wi=wi: e.tensor_copy(gq[:, wi * 1024:(wi + 1) * 1024], ptr2[:, :]),
                 reads=[rptr2], writes=[rgq], tag=f"g{ic}_te{wi}")
        p.op("vector", lambda e: e.tensor_scalar(sw[:, 0:2], sw[:, 0:2], scx[:, 1:2], None, ALU.mult),
             reads=[rsw, rscx], writes=[rsw], tag=f"g{ic}_sv")
        for tg in range(4):
            bg, bu = tg % 2, 2 + tg % 2

            def gmm(e, tg=tg, bg=bg):
                inst = None
                for c in range(8):
                    inst = e.matmul(pmm[bg][:, :], gq[:, c * 128:(c + 1) * 128],
                                    qxT[:, c * 2048 + tg * 512:(c) * 2048 + tg * 512 + 512], start=(c == 0), stop=(c == 7))
                return inst
            p.op("tensor", gmm, reads=[rgq, rqxT], writes=[rpmm[bg]], tag=f"g{ic}_gm{tg}")

            def umm(e, tg=tg, bu=bu):
                inst = None
                for c in range(8):
                    inst = e.matmul(pmm[bu][:, :], gq[:, 1024 + c * 128: 1024 + (c + 1) * 128],
                                    qxT[:, c * 2048 + tg * 512: c * 2048 + tg * 512 + 512], start=(c == 0), stop=(c == 7))
                return inst
            p.op("tensor", umm, reads=[rgq, rqxT], writes=[rpmm[bu]], tag=f"g{ic}_um{tg}")
            p.op("scalar", lambda e, bg=bg: e.activation(sgt[:, :], pmm[bg][:, :], AF.Silu, scale=sw[:, 0:1]),
                 reads=[rpmm[bg], rsw], writes=[rsgt], tag=f"g{ic}_si{tg}")
            p.op("vector", lambda e, tg=tg, bu=bu: e.scalar_tensor_tensor(
                hbuf[:, tg * 512:(tg + 1) * 512], pmm[bu][:, :], sw[:, 1:2], sgt[:, :], ALU.mult, ALU.mult),
                reads=[rpmm[bu], rsw, rsgt], writes=[rhbuf], tag=f"g{ic}_hb{tg}")
        p.op("vector", lambda e: e.tensor_reduce(tcol[:, :], hbuf[:, :], AX.X, ALU.max, apply_absolute_value=True),
             reads=[rhbuf], writes=[rtcol], tag=f"g{ic}_am")
        p.op("vector", lambda e: e.tensor_max(amax[:, :], amax[:, :], tcol[:, :]), reads=[ramax, rtcol], writes=[ramax], tag=f"g{ic}_am2")
        p.dma("sync", lambda e, ic=ic: e.dma_start(out=hact_out[ic * 128:(ic + 1) * 128, :], in_=hbuf[:, :]),
              reads=[rhbuf], writes=[ctx.r["out:hact"]], tag=f"g{ic}_st")

    p.op("tensor", lambda e: e.transpose(paux[0:1, 0:128], amax[:, :], ctx.t["c_ident"][:, :]),
         reads=[ramax, ctx.r["c_ident"]], writes=[rpaux], tag="am_tr")
    mx = ctx.sb("mx", [1, 1], F32)
    rmx = ctx.r["mx"]
    p.op("vector", lambda e: e.tensor_reduce(mx[:, :], paux[0:1, 0:128], AX.X, ALU.max), reads=[rpaux], writes=[rmx], tag="am_f")
    p.dma("sync", lambda e: e.dma_start(out=mx_out, in_=mx[:, :]), reads=[rmx], writes=[ctx.r["out:mx2"]], tag="stmx")
    p.build(nc)
    ctx.close()
    nc.reset()
    return ctx


def build_l1_down():
    ctx = Ctx("ffndn")
    nc, p = ctx.nc, ctx.p
    x_in = ctx.inp("x3", [2048, 1024])
    hact_in = ctx.inp("hact", [2816, 2048], BF16)
    maxes_in = ctx.inp("maxes2", [8, 1])
    clipd_in = ctx.inp("clip_down", [1, 1])
    wd_in = ctx.inp("w_down", [1024, 2816])
    c_in = ctx.inp("c_row", [1, 1024])
    wmod_in = ctx.inp("wmod", [1024, 6144])
    x4_out = ctx.out("x4", [2048, 1024])
    ctx.psum_pool()
    ctx.consts()
    mod = emit_mod(ctx, c_in, wmod_in, "mod1", sections=("gb",))
    scd, rscd = emit_gmax_scale(ctx, "sd", maxes_in, clipd_in, None)
    gb_t, rgb = mod["gb"]

    wst = [ctx.sb("wd0", [128, 2816], F32)]
    rwst = [ctx.r["wd0"]]
    wst = [wst[0], wst[0]]
    rwst = [rwst[0], rwst[0]]
    dwT = ctx.sb("dwT", [128, 22 * 1024], BF16)
    rdwT = ctx.r["dwT"]
    swd = ctx.sb("swd", [128, 8], F32)
    rswd = ctx.r["swd"]
    rq_c = ctx.sb("rq_c", [128, 1], F32)
    rrq = ctx.r["rq_c"]
    qn = ctx.sb("qn", [128, 2816], BF16)
    rqn = ctx.r["qn"]
    ptrb, rptrb = ctx.t["ptrb"], ctx.r["ptrb"]
    paux, rpaux = ctx.t["paux"], ctx.r["paux"]
    for hc in range(8):
        b = hc % 2
        p.dma("sync", lambda e, hc=hc, b=b: e.dma_start(out=wst[b][:, :], in_=wd_in[hc * 128:(hc + 1) * 128, :]),
              writes=[rwst[b]], tag=f"d{hc}_ld")
        p.op("vector", lambda e, hc=hc, b=b: e.tensor_reduce(swd[:, hc:hc + 1], wst[b][:, :], AX.X, ALU.max, apply_absolute_value=True),
             reads=[rwst[b]], writes=[rswd], tag=f"d{hc}_rm")
        p.op("vector", lambda e, hc=hc: e.tensor_scalar_mul(swd[:, hc:hc + 1], swd[:, hc:hc + 1], 1.0 / 127.0),
             reads=[rswd], writes=[rswd], tag=f"d{hc}_sm")
        p.op("vector", lambda e, hc=hc: e.reciprocal(rq_c[:, :], swd[:, hc:hc + 1]), reads=[rswd], writes=[rrq], tag=f"d{hc}_rc")
        p.op("vector", lambda e, b=b: e.tensor_scalar(wst[b][:, :], wst[b][:, :], rq_c[:, 0:1], RND_C, ALU.mult, ALU.add),
             reads=[rwst[b], rrq], writes=[rwst[b]], tag=f"d{hc}_q1")
        p.op("vector", lambda e, b=b: e.tensor_scalar(wst[b][:, :], wst[b][:, :], RND_C, 127.0, ALU.subtract, ALU.min),
             reads=[rwst[b]], writes=[rwst[b]], tag=f"d{hc}_q2")
        p.op("vector", lambda e, b=b: e.tensor_scalar_max(qn[:, :], wst[b][:, :], -128.0), reads=[rwst[b]], writes=[rqn], tag=f"d{hc}_q3")

        def trd(e, hc=hc):
            inst = None
            for ic in range(22):
                inst = e.transpose(ptrb[:, (ic % 8) * 128:(ic % 8) * 128 + 128], qn[:, ic * 128:(ic + 1) * 128], ctx.t["c_ident_b"][:, :])
            return inst
        for grp in range(3):
            n_in = min(8, 22 - grp * 8)

            def trg(e, hc=hc, grp=grp, n_in=n_in):
                inst = None
                for k in range(n_in):
                    ic = grp * 8 + k
                    inst = e.transpose(ptrb[:, k * 128:(k + 1) * 128], qn[:, ic * 128:(ic + 1) * 128], ctx.t["c_ident_b"][:, :])
                return inst
            p.op("tensor", trg, reads=[rqn, ctx.r["c_ident_b"]], writes=[rptrb], tag=f"d{hc}_t{grp}")
            p.op("vector", lambda e, hc=hc, grp=grp, n_in=n_in: e.tensor_copy(
                bass.AP(dwT, grp * 8 * 1024 + hc * 128, [[22 * 1024, 128], [1024, n_in], [1, 128]]),
                ptrb[:, 0:n_in * 128].rearrange("p (k q) -> p k q", k=n_in)),
                reads=[rptrb], writes=[rdwT], tag=f"d{hc}_te{grp}")
    swrow = ctx.sb("swrow", [1, 1024], F32)
    rswrow = ctx.r["swrow"]

    for grp in range(2):
        def swtr(e, grp=grp):
            inst = None
            for k in range(4):
                hc = grp * 4 + k
                inst = e.transpose(paux[0:1, k * 128:(k + 1) * 128], swd[:, hc:hc + 1], ctx.t["c_ident"][:, :])
            return inst
        p.op("tensor", swtr, reads=[rswd, ctx.r["c_ident"]], writes=[rpaux], tag=f"sw_tr{grp}")
        p.op("vector", lambda e, grp=grp: e.tensor_copy(swrow[:, grp * 512:(grp + 1) * 512], paux[0:1, 0:512]),
             reads=[rpaux], writes=[rswrow], tag=f"sw_te{grp}")
    swb = ctx.sb("swb", [128, 1024], F32)
    rswb = ctx.r["swb"]

    def swbc(e):
        e.matmul(ctx.t["pwide"][:, 0:512], ctx.t["c_ones"][0:1, :], swrow[:, 0:512], start=True, stop=True)
        return e.matmul(ctx.t["pwide"][:, 512:1024], ctx.t["c_ones"][0:1, :], swrow[:, 512:1024], start=True, stop=True)
    p.op("tensor", swbc, reads=[rswrow, ctx.r["c_ones"]], writes=[ctx.r["pwide"]], tag="sw_bc")
    p.op("vector", lambda e: e.tensor_mul(swb[:, :], ctx.t["pwide"][:, :], gb_t[:, :]), reads=[ctx.r["pwide"], rgb], writes=[rswb], tag="sw_gb")
    for ic in range(22):
        p.op("vector", lambda e, ic=ic: e.tensor_mul(dwT[:, ic * 1024:(ic + 1) * 1024], dwT[:, ic * 1024:(ic + 1) * 1024], swb[:, :]),
             reads=[rdwT, rswb], writes=[rdwT], tag=f"fold{ic}")

    qh = ctx.sb("qh", [128, 22 * 2048], BF16)
    rqh = ctx.r["qh"]
    hstr = [ctx.sb(f"hs{i}", [128, 2048], BF16) for i in range(2)]
    rhstr = [ctx.r[f"hs{i}"] for i in range(2)]
    qtf = ctx.sb("qtf", [128, 2048], F32)
    rqtf = ctx.r["qtf"]
    for ic in range(22):
        b = ic % 2
        p.dma("sync", lambda e, ic=ic, b=b: e.dma_start(out=hstr[b][:, :], in_=hact_in[ic * 128:(ic + 1) * 128, :]),
              writes=[rhstr[b]], tag=f"h{ic}_ld")
        p.op("vector", lambda e, b=b: e.tensor_scalar(qtf[:, :], hstr[b][:, :], scd[:, 0:1], RND_C, ALU.mult, ALU.add),
             reads=[rhstr[b], rscd], writes=[rqtf], tag=f"h{ic}_q1")
        p.op("vector", lambda e: e.tensor_scalar(qtf[:, :], qtf[:, :], RND_C, 127.0, ALU.subtract, ALU.min),
             reads=[rqtf], writes=[rqtf], tag=f"h{ic}_q2")
        p.op("vector", lambda e, ic=ic: e.tensor_scalar_max(qh[:, ic * 2048:(ic + 1) * 2048], qtf[:, :], -128.0),
             reads=[rqtf], writes=[rqh], tag=f"h{ic}_q3")

    xstr = [wst[0][:, 0:1024], wst[0][:, 1024:2048]]
    rxstr = [rwst[0], rwst[0]]
    pwide, rpwide = ctx.t["pwide"], ctx.r["pwide"]
    for t in range(16):
        b = t % 2
        p.dma("sync", lambda e, t=t, b=b: e.dma_start(out=xstr[b], in_=x_in[t * 128:(t + 1) * 128, :]),
              writes=[rxstr[b]], tag=f"o{t}_ld")

        def dmm(e, t=t):
            inst = None
            for hf in range(2):
                for ic in range(22):
                    inst = e.matmul(pwide[:, hf * 512:(hf + 1) * 512],
                                    qh[:, ic * 2048 + t * 128: ic * 2048 + (t + 1) * 128],
                                    dwT[:, ic * 1024 + hf * 512: ic * 1024 + (hf + 1) * 512],
                                    start=(ic == 0), stop=(ic == 21))
            return inst
        p.op("tensor", dmm, reads=[rqh, rdwT], writes=[rpwide], tag=f"o{t}_mm")
        for hf in range(2):
            p.op("vector", lambda e, b=b, hf=hf: e.scalar_tensor_tensor(
                xstr[b][:, hf * 512:(hf + 1) * 512], pwide[:, hf * 512:(hf + 1) * 512], scd[:, 1:2],
                xstr[b][:, hf * 512:(hf + 1) * 512], ALU.mult, ALU.add),
                reads=[rpwide, rscd, rxstr[b]], writes=[rxstr[b]], tag=f"o{t}_cb{hf}")
        p.dma("sync", lambda e, t=t, b=b: e.dma_start(out=x4_out[t * 128:(t + 1) * 128, :], in_=xstr[b]),
              reads=[rxstr[b]], writes=[ctx.r["out:x4"]], tag=f"o{t}_st")

    p.build(nc)
    ctx.close()
    nc.reset()
    return ctx


def shard_rows(x4, core):
    b, h = core // 2, core % 2
    return np.ascontiguousarray(x4[b, h * 2048:(h + 1) * 2048, :])


def kernel(**inputs):
    g = {k: np.ascontiguousarray(np.asarray(v, dtype=np.float32)) for k, v in inputs.items()}
    x = g["L0_attn_x"]
    c0, c1 = g["L0_c"], g["L1_c"]
    wmod0, wmod1 = g["L0_wmod"], g["L1_wmod"]

    wattnT = np.ascontiguousarray(g["L0_attn_w_attn"].T)
    wprojT = np.ascontiguousarray(g["L0_attn_w_proj"].T)
    gatewT = np.ascontiguousarray(g["L0_ffn_gate_w"].T)
    dgT = np.ascontiguousarray(g["L0_ffn_dyn_gate_w"].transpose(0, 2, 1))
    duT = np.ascontiguousarray(g["L0_ffn_dyn_up_w"].transpose(0, 2, 1))
    ddT = np.ascontiguousarray(g["L0_ffn_dyn_down_w"].transpose(0, 2, 1))
    sgT = np.ascontiguousarray(g["L0_ffn_sh_gate_w"][0].T)
    suT = np.ascontiguousarray(g["L0_ffn_sh_up_w"][0].T)
    sdT = np.ascontiguousarray(g["L0_ffn_sh_down_w"][0].T)

    winT = np.ascontiguousarray(g["L1_attn_W_in"].T)
    winpad = np.zeros((1024, 1024), np.float32)
    for h in range(16):
        cb_ = (h % 4) * 32 + (h // 4) * 128
        winpad[:, cb_:cb_ + 16] = winT[:, 2048 + h * 16: 2048 + (h + 1) * 16]
        winpad[:, 512 + cb_:512 + cb_ + 16] = winT[:, 2304 + h * 16: 2304 + (h + 1) * 16]
    winpadT = np.ascontiguousarray(winpad)
    mamba_common = {"winT": winT, "winpadT": winpadT,
                    "dtwT": np.ascontiguousarray(g["L1_attn_dt_W"].T),
                    "dtb": np.ascontiguousarray(g["L1_attn_dt_b"].reshape(16, 1)),
                    "alog": np.ascontiguousarray(g["L1_attn_A_log"].reshape(16, 1)),
                    "dpar": np.ascontiguousarray(g["L1_attn_D_param"].reshape(1, 16)),
                    "wmod": wmod1}

    def crow(c, core):
        return np.ascontiguousarray(c[core // 2:core // 2 + 1, :])

    cores = list(range(8))
    ctx = build_l0_qkv()
    r = run_bass_kernel_spmd(ctx.nc, [{"x": shard_rows(x, c), "c_row": crow(c0, c),
                                       "wmod": wmod0, "wattnT": wattnT} for c in cores],
                             core_ids=cores).results
    QT = [np.asarray(rr["QT"]) for rr in r]
    KT = [np.asarray(rr["KT"]) for rr in r]
    V = [np.asarray(rr["V"]) for rr in r]
    YT = [None] * 8
    for hf in (0, 1):
        ctx = build_l0_flash(hf)
        sub = [c for c in cores if c % 2 == hf]
        im = []
        for c in sub:
            p0 = (c // 2) * 2
            im.append({"QTo": QT[c], "KTf": np.concatenate([KT[p0], KT[p0 + 1]], 1),
                       "Vf": np.concatenate([V[p0], V[p0 + 1]], 0)})
        rr = run_bass_kernel_spmd(ctx.nc, im, core_ids=sub).results
        for i2, c in enumerate(sub):
            YT[c] = np.asarray(rr[i2]["YT"])
    ctx = build_l0_route()
    r3 = run_bass_kernel_spmd(ctx.nc, [{"x": shard_rows(x, c), "YT": YT[c], "wprojT": wprojT,
                                        "c_row": crow(c0, c), "wmod": wmod0, "gatewT": gatewT}
                                       for c in cores], core_ids=cores).results
    cnts = np.ascontiguousarray(np.stack([np.asarray(r3[c]["cnt"])[0] for c in cores], 0))
    ctx = build_l0_moe()
    im4 = []
    for c in cores:
        pm = np.zeros((8, 1), np.float32)
        pm[:c] = 1.0
        im4.append({"x1": np.asarray(r3[c]["x1"]), "h1T": np.asarray(r3[c]["h1T"]),
                    "rw": np.asarray(r3[c]["rw"]), "cand": np.asarray(r3[c]["cand"]),
                    "logits": np.asarray(r3[c]["logits"]), "cnts": cnts, "pmask": pm,
                    "dgT": dgT, "duT": duT, "ddT": ddT, "sgT": sgT, "suT": suT, "sdT": sdT,
                    "c_row": crow(c0, c), "wmod": wmod0})
    r4 = run_bass_kernel_spmd(ctx.nc, im4, core_ids=cores).results
    x2s = [np.asarray(r4[c]["x2"]) for c in cores]
    ctx = build_l1_mamba(False)
    r5 = run_bass_kernel_spmd(ctx.nc, [dict(mamba_common, x2=x2s[c], c_row=crow(c1, c))
                                       for c in cores], core_ids=cores).results
    ctx = build_l1_mamba(True)
    im6 = []
    for c in cores:
        hin = (np.ascontiguousarray(np.asarray(r5[2 * (c // 2)]["hfin"], dtype=np.float32))
               if c % 2 == 1 else np.zeros((128, 256), np.float32))
        im6.append(dict(mamba_common, x2=x2s[c], c_row=crow(c1, c),
                        normw=np.ascontiguousarray(g["L1_attn_norm_w"].reshape(1, 1024)),
                        woutT=np.ascontiguousarray(g["L1_attn_W_out"].T), hin=hin))
    r6 = run_bass_kernel_spmd(ctx.nc, im6, core_ids=cores).results
    x3s = [np.asarray(r6[c]["x3"]) for c in cores]
    ctx = build_l1_max1()
    r7 = run_bass_kernel_spmd(ctx.nc, [{"x3": x3s[c], "c_row": crow(c1, c), "wmod": wmod1}
                                       for c in cores], core_ids=cores).results
    maxes1 = np.ascontiguousarray(np.stack([np.asarray(r7[c]["mx1"])[0] for c in cores], 0))
    ctx = build_l1_gateup()
    r8 = run_bass_kernel_spmd(ctx.nc, [{"x3": x3s[c], "c_row": crow(c1, c), "wmod": wmod1,
                                        "maxes1": maxes1,
                                        "clip_x": np.ascontiguousarray(g["L1_ffn_clip_x"].reshape(1, 1)),
                                        "w_gate": g["L1_ffn_w_gate"], "w_up": g["L1_ffn_w_up"]}
                                       for c in cores], core_ids=cores).results
    maxes2 = np.ascontiguousarray(np.stack([np.asarray(r8[c]["mx2"])[0] for c in cores], 0))
    ctx = build_l1_down()
    r9 = run_bass_kernel_spmd(ctx.nc, [{"x3": x3s[c], "c_row": crow(c1, c), "wmod": wmod1,
                                        "hact": np.asarray(r8[c]["hact"]), "maxes2": maxes2,
                                        "clip_down": np.ascontiguousarray(g["L1_ffn_clip_down"].reshape(1, 1)),
                                        "w_down": g["L1_ffn_w_down"]}
                                       for c in cores], core_ids=cores).results
    out = np.empty((4, 4096, 1024), np.float32)
    for c in cores:
        b, hf = c // 2, c % 2
        out[b, hf * 2048:(hf + 1) * 2048, :] = np.asarray(r9[c]["x4"])
    return out


# version 02 (final)
# speedup vs baseline: 1.0582x; 1.0582x over previous
import contextlib
import numpy as np
import ml_dtypes
import concourse.bass as bass
import concourse.mybir as mybir
from concourse.bass_utils import run_bass_kernel_spmd
from concourse.bass_core_rust import IndirectOffsetOnAxis

F32 = mybir.dt.float32
BF16 = mybir.dt.bfloat16
I32 = mybir.dt.int32
AF = mybir.ActivationFunctionType
ALU = mybir.AluOpType
AX = mybir.AxisListType
BFNP = ml_dtypes.bfloat16

RND_C = 12582912.0

ENGINES = ("tensor", "vector", "scalar", "gpsimd", "sync")
DOMAINS = ENGINES + tuple("q_" + e for e in ENGINES)


class Res:
    __slots__ = ("name", "last_writer", "readers")

    def __init__(self, name):
        self.name = name
        self.last_writer = None
        self.readers = []


class Op:
    __slots__ = ("plan_idx", "engine", "domain", "fn", "reads", "writes", "deps",
                 "tag", "is_dma", "seq", "waits", "inc_needed", "sem_count", "vc",
                 "collective")

    def __init__(self, plan_idx, engine, fn, reads, writes, tag, is_dma, collective):
        self.plan_idx = plan_idx
        self.engine = engine
        self.is_dma = is_dma
        self.collective = collective
        if collective:
            self.domain = f"cc{plan_idx}"
        elif is_dma:
            self.domain = "q_" + engine
        else:
            self.domain = engine
        self.fn = fn
        self.reads = reads
        self.writes = writes
        self.deps = set()
        self.tag = tag
        self.seq = 0
        self.waits = []
        self.inc_needed = False
        self.sem_count = 0
        self.vc = None


class Prog:
    def __init__(self, same_engine_sync=True):
        self.ops = []
        self.same_engine_sync = same_engine_sync
        self._res_count = 0

    def res(self, name=None):
        self._res_count += 1
        return Res(name or f"r{self._res_count}")

    def op(self, engine, fn, reads=(), writes=(), tag="", dma=False, collective=False):
        o = Op(len(self.ops), engine, fn, tuple(reads), tuple(writes), tag, dma,
               collective)
        deps = set()
        for r in o.reads:
            if r.last_writer is not None:
                deps.add(r.last_writer)
        for r in o.writes:
            if r.last_writer is not None:
                deps.add(r.last_writer)
            for rd in r.readers:
                deps.add(rd)
        deps.discard(o)
        o.deps = deps
        for r in o.reads:
            r.readers.append(o)
        for r in o.writes:
            r.last_writer = o
            r.readers = []
        self.ops.append(o)
        return o

    def dma(self, engine, fn, reads=(), writes=(), tag=""):
        return self.op(engine, fn, reads, writes, tag, dma=True)

    def _schedule(self):
        self.domains = list(DOMAINS) + [o.domain for o in self.ops if o.collective]
        seq = {d: 0 for d in self.domains}
        for o in self.ops:
            seq[o.domain] += 1
            o.seq = seq[o.domain]
        vc_eng = {e: {d: 0 for d in self.domains} for e in ENGINES}
        eng_seq = {e: 0 for e in ENGINES}
        for o in self.ops:
            vc = vc_eng[o.engine]
            is_plain = (not o.is_dma) and (not o.collective)
            waits = []
            for d in sorted(o.deps, key=lambda d: d.plan_idx):
                d_plain = (not d.is_dma) and (not d.collective)
                if d_plain and d.engine == o.engine and not self.same_engine_sync:
                    continue
                if vc[d.domain] >= d.seq:
                    continue
                waits.append(d)
            for d in waits:
                d.inc_needed = True
                for dom, v in d.vc.items():
                    if v > vc[dom]:
                        vc[dom] = v
                if d.seq > vc[d.domain]:
                    vc[d.domain] = d.seq
            o.waits = waits
            if is_plain:
                eng_seq[o.engine] = o.seq
            myvc = dict(vc)
            myvc[o.domain] = max(myvc[o.domain], o.seq)
            if is_plain:
                myvc[o.engine] = max(myvc[o.engine], o.seq)
            else:
                myvc[o.engine] = max(myvc[o.engine], eng_seq[o.engine])
            o.vc = myvc
        cnt = {d: 0 for d in self.domains}
        for o in self.ops:
            if o.collective:
                cnt[o.domain] = 1
                o.sem_count = 1
                o.inc_needed = True
            elif o.is_dma:
                cnt[o.domain] += 16
                o.sem_count = cnt[o.domain]
                o.inc_needed = True
            elif o.inc_needed:
                cnt[o.domain] += 1
                o.sem_count = cnt[o.domain]
        self.sem_totals = cnt

    def build(self, nc, verbose=False):
        self._schedule()
        with contextlib.ExitStack() as stack:
            sems = {}
            for d in self.domains:
                if self.sem_totals[d] > 0:
                    sems[d] = stack.enter_context(nc.semaphore(f"sem_{d}"))
            block = stack.enter_context(nc.Block())

            def emit_engine(eng_name, eng):
                for o in self.ops:
                    if o.engine != eng_name:
                        continue
                    if o.is_dma and not o.collective and o.sem_count > 16:
                        eng.wait_ge(sems[o.domain], o.sem_count - 16)
                    for w in o.waits:
                        eng.wait_ge(sems[w.domain], w.sem_count)
                    inst = o.fn(eng)
                    if o.inc_needed:
                        assert inst is not None, (o.tag, "fn must return instruction")
                        if o.collective:
                            inst.then_inc(sems[o.domain])
                        elif o.is_dma:
                            inst.then_inc(sems[o.domain], 16)
                        else:
                            inst.then_inc(sems[o.domain], 1)

            @block.tensor
            def _(eng):
                emit_engine("tensor", eng)

            @block.vector
            def _(eng):
                emit_engine("vector", eng)

            @block.scalar
            def _(eng):
                emit_engine("scalar", eng)

            @block.gpsimd
            def _(eng):
                emit_engine("gpsimd", eng)

            @block.sync
            def _(eng):
                emit_engine("sync", eng)
        if verbose:
            nw = sum(len(o.waits) for o in self.ops)
            print(f"[fw] ops={len(self.ops)} waits={nw} sems={ {d: c for d, c in self.sem_totals.items() if c} }")


class Ctx:
    def __init__(self, name):
        self.nc = bass.Bass("TRN2", target_bir_lowering=False)
        self.p = Prog()
        self.stack = contextlib.ExitStack()
        self.r = {}
        self.t = {}
        self.n_sb_bytes = 0
        self.inputs = []
        self.outputs = []

    def inp(self, name, shape, dtype=F32):
        self.inputs.append(name)
        return self.nc.dram_tensor(name, shape, dtype, kind="ExternalInput").ap()

    def out(self, name, shape, dtype=F32):
        self.outputs.append(name)
        ap = self.nc.dram_tensor(name, shape, dtype, kind="ExternalOutput").ap()
        self.r["out:" + name] = self.p.res("out:" + name)
        return ap

    def dram(self, name, shape, dtype=F32):
        ap = self.nc.dram_tensor(name, shape, dtype, kind="Internal").ap()
        self.r[name] = self.p.res(name)
        return ap

    def sb(self, name, shape, dtype=F32):
        cur = getattr(self, "_phase_cursor", None)
        if cur is not None:
            if name in self.t:
                return self.t[name]
            per_part = int(np.prod(shape[1:])) * mybir.dt.size(dtype)
            off = (cur + 31) // 32 * 32
            self._phase_cursor = off + per_part
            h, _ = self.arena_sb(name, shape, dtype, off)
            return h
        h = self.stack.enter_context(self.nc.sbuf_tensor(name, list(shape), dtype))
        self.t[name] = h
        self.r[name] = self.p.res(name)
        per_part = int(np.prod(shape[1:])) * mybir.dt.size(dtype)
        self.n_sb_bytes += per_part
        return h

    def sb_persist(self, name, shape, dtype=F32):
        cur = getattr(self, "_phase_cursor", None)
        self._phase_cursor = None
        h = self.sb(name, shape, dtype)
        self._phase_cursor = cur
        return h

    def ps(self, name, shape, dtype=F32):
        h = self.stack.enter_context(self.nc.psum_tensor(name, list(shape), dtype))
        self.t[name] = h
        self.r[name] = self.p.res(name)
        return h

    def psum_pool(self):
        self.ps("pmm0", [128, 512], F32)
        self.ps("pmm1", [128, 512], F32)
        self.ps("pmm2", [128, 512], F32)
        self.ps("pmm3", [128, 512], F32)
        self.ps("pwide", [128, 1024], F32)
        self.ps("ptrb", [128, 1024], BF16)
        self.ps("paux", [128, 512], F32)

    def arena_init(self, nbytes):
        self._arena_h = self.stack.enter_context(
            self.nc.sbuf_tensor("phase_arena", [128, nbytes], mybir.dt.uint8))
        self._arena_addr = self.nc.lookup_mloc(self._arena_h).addr
        self._arena_bytes = nbytes
        self._arena_cur = []
        self.n_sb_bytes += nbytes

    def arena_sb(self, name, shape, dtype, offset):
        per_part = int(np.prod(shape[1:])) * mybir.dt.size(dtype)
        assert offset + per_part <= self._arena_bytes, (name, offset, per_part)
        assert offset % 32 == 0, (name, offset)
        h = self.nc.virtual_sbuf_tensor(name, list(shape), dtype)
        self.nc._tensor(h.name, h.shape, h.dtype, type="SB")
        mloc = self.nc.lookup_mloc(h)
        mloc.allocated = True
        mloc.addr = self._arena_addr + offset
        self.t[name] = h
        r = self.p.res(name)
        seen = set()
        for pr, poff, psz in self._arena_cur:
            if poff < offset + per_part and offset < poff + psz:
                for o in ([pr.last_writer] if pr.last_writer else []) + pr.readers:
                    if id(o) not in seen:
                        seen.add(id(o))
                        r.readers.append(o)
        self._arena_cur.append((r, offset, per_part))
        self.r[name] = r
        return h, r

    def phase(self, cursor):
        self._phase_cursor = cursor

    def phase_off(self):
        self._phase_cursor = None

    def sb_shared(self, name, shape, dtype=F32):
        if name in self.t:
            return self.t[name], self.r[name]
        h = self.sb(name, shape, dtype)
        return h, self.r[name]

    def const_col(self, value, dtype=F32):
        key = (value, dtype)
        if not hasattr(self, "_cc"):
            self._cc = {}
        if key not in self._cc:
            nm = f"cc_{len(self._cc)}"
            h = self.sb(nm, [128, 1], dtype)
            r = self.r[nm]
            self.p.op("gpsimd", lambda e: e.memset(h[:, :], value), writes=[r], tag=nm)
            self._cc[key] = (h, r)
        return self._cc[key]

    def close(self):
        self.stack.close()

    def consts(self):
        p, nc = self.p, self.nc
        ident = self.sb("c_ident", [128, 128], F32)
        ident_b = self.sb("c_ident_b", [128, 128], BF16)
        ones = self.sb("c_ones", [128, 128], F32)
        ones_b = self.sb("c_ones_b", [128, 128], BF16)
        for nm, h, fill in (("c_ident", ident, "id"), ("c_ident_b", ident_b, "id"),
                            ("c_ones", ones, "1"), ("c_ones_b", ones_b, "1")):
            r = self.r[nm]
            p.op("gpsimd", lambda e, h=h: e.memset(h[:, :], 1.0), writes=[r], tag=nm)
            if fill == "id":
                p.op("gpsimd", lambda e, h=h: e.affine_select(
                    out=h[:, :], in_=h[:, :], compare_op=ALU.is_equal, fill=0.0,
                    base=0, pattern=[[-1, 128]], channel_multiplier=1),
                    reads=[r], writes=[r], tag=nm + "_sel")


def emit_mod(ctx, c_row_ap, wmod_ap, tag, sections=("sa", "ca1", "ga", "sb", "cb1", "gb"), wst=None, dst=None):
    p, nc = ctx.p, ctx.nc
    crow = ctx.sb(tag + "_crow", [1, 1024], F32)
    rcrow = ctx.r[tag + "_crow"]
    p.dma("sync", lambda e: e.dma_start(out=crow[:, :], in_=c_row_ap), writes=[rcrow], tag=tag + "_ldc")
    p.op("scalar", lambda e: e.activation(crow[:, :], crow[:, :], AF.Silu),
         reads=[rcrow], writes=[rcrow], tag=tag + "_silu")
    silc = ctx.sb(tag + "_silc", [128, 8], F32)
    rsilc = ctx.r[tag + "_silc"]
    paux, rpaux = ctx.t["paux"], ctx.r["paux"]

    def tr_all(e):
        inst = None
        for c in range(8):
            inst = e.transpose(paux[:, c:c + 1], crow[0:1, c * 128:(c + 1) * 128],
                               ctx.t["c_ident"][0:1, 0:1])
        return inst
    p.op("tensor", tr_all, reads=[rcrow, ctx.r["c_ident"]], writes=[rpaux], tag=tag + "_tr")
    p.op("vector", lambda e: e.tensor_copy(silc[:, :], paux[:, 0:8]), reads=[rpaux], writes=[rsilc], tag=tag + "_trev")

    if wst is None:
        wbuf = [ctx.sb(tag + f"_wbuf{i}", [128, 1536], F32) for i in range(2)]
        rwbuf = [ctx.r[tag + f"_wbuf{i}"] for i in range(2)]
    else:
        wbuf = [wst[0][0], wst[1][0]]
        rwbuf = [wst[0][1], wst[1][1]]
    pmod = [ctx.t[f"pmm{i}"] for i in range(4)]
    rpmod = [ctx.r[f"pmm{i}"] for i in range(4)]
    names_all = ["sa", "ca1", "ga", "sb", "cb1", "gb"]
    sec_idx = [i for i, nm in enumerate(names_all) if nm in sections]
    modrow = ctx.sb(tag + "_modrow", [1, 1024 * len(sec_idx)], F32)
    rmodrow = ctx.r[tag + "_modrow"]
    wpiece = (wbuf[0].shape[1] // 512) * 512
    npieces = 6144 // wpiece
    for half in range(npieces):
        for c in range(8):
            b = c % 2
            p.dma("sync", lambda e, c=c, b=b, half=half, wpiece=wpiece: e.dma_start(
                out=wbuf[b][:, 0:wpiece], in_=wmod_ap[c * 128:(c + 1) * 128, half * wpiece:(half + 1) * wpiece]),
                writes=[rwbuf[b]], tag=f"{tag}_ldw{half}_{c}")

            def mms(e, c=c, b=b, half=half, wpiece=wpiece):
                inst = None
                for g6 in range(wpiece // 512):
                    g = half * (wpiece // 512) + g6
                    bank, row = g % 4, (g // 4) * 32
                    inst = e.matmul(pmod[bank][row:row + 1, :], silc[:, c:c + 1],
                                    wbuf[b][:, g6 * 512:(g6 + 1) * 512],
                                    start=(c == 0), stop=(c == 7))
                return inst
            p.op("tensor", mms, reads=[rwbuf[b], rsilc], writes=rpmod, tag=f"{tag}_mm{half}_{c}")

    def evac(e):
        inst = None
        for k, i in enumerate(sec_idx):
            for gg in range(2):
                g = i * 2 + gg
                bank, row = g % 4, (g // 4) * 32
                inst = e.tensor_copy(modrow[:, (k * 2 + gg) * 512:(k * 2 + gg + 1) * 512], pmod[bank][row:row + 1, :])
        return inst
    p.op("vector", evac, reads=rpmod, writes=[rmodrow], tag=tag + "_evac")

    result = {}
    names_i = [(k, names_all[i]) for k, i in enumerate(sec_idx)]
    pbc, rpbc = ctx.t["pwide"], ctx.r["pwide"]
    if "ga" in sections or "gb" in sections:
        tmp, rtmp = ctx.sb_shared("_mod_tmprow", [1, 1024], F32)
        sgn, rsgn = ctx.sb_shared("_mod_sgnrow", [1, 1024], F32)
    for i, nm in names_i:
        full = tag + "_" + nm
        if dst is not None:
            dtile, rdst = dst[nm]
            ctx.t[full] = dtile
        else:
            dtile = ctx.sb(full, [128, 1024], F32)
            rdst = ctx.r[full]
        sec = modrow[:, i * 1024:(i + 1) * 1024]
        if nm in ("ga", "gb"):
            p.op("scalar", lambda e, sec=sec: e.sign(sgn[:, :], sec), reads=[rmodrow], writes=[rsgn], tag=full + "_sg")
            p.op("scalar", lambda e, sec=sec: e.activation(tmp[:, :], sec, AF.Abs),
                 reads=[rmodrow], writes=[rtmp], tag=full + "_abs")
            p.op("vector", lambda e: e.tensor_scalar_max(tmp[:, :], tmp[:, :], 0.25),
                 reads=[rtmp], writes=[rtmp], tag=full + "_mx")
            p.op("vector", lambda e: e.tensor_mul(tmp[:, :], tmp[:, :], sgn[:, :]),
                 reads=[rtmp, rsgn], writes=[rtmp], tag=full + "_ms")
            src_row, rsrc = tmp, rtmp
        else:
            src_row, rsrc = sec, rmodrow

        def bc(e, src_row=src_row):
            e.matmul(pbc[:, 0:512], ctx.t["c_ones"][0:1, :], src_row[:, 0:512], start=True, stop=True)
            return e.matmul(pbc[:, 512:1024], ctx.t["c_ones"][0:1, :], src_row[:, 512:1024], start=True, stop=True)
        p.op("tensor", bc, reads=[rsrc, ctx.r["c_ones"]], writes=[rpbc], tag=full + "_bc")
        add1 = 1.0 if nm in ("ca1", "cb1") else 0.0
        p.op("vector", lambda e, dtile=dtile, add1=add1: e.tensor_scalar_add(dtile[:, :], pbc[:, :], add1),
             reads=[rpbc], writes=[rdst], tag=full + "_ev")
        result[nm] = (dtile, rdst)
    return result


def emit_abs_reduce(ctx, nm, in_ap, rin, flavor=ALU.max):
    p = ctx.p
    col = ctx.sb(nm + "_col", [128, 1], F32)
    rcol = ctx.r[nm + "_col"]
    paux, rpaux = ctx.t["paux"], ctx.r["paux"]
    sc = ctx.sb(nm + "_sc", [1, 1], F32)
    rsc = ctx.r[nm + "_sc"]
    p.op("vector", lambda e: e.tensor_reduce(col[:, :], in_ap, AX.X, flavor, apply_absolute_value=True),
         reads=[rin], writes=[rcol], tag=nm + "_r1")
    p.op("tensor", lambda e: e.transpose(paux[0:1, 0:128], col[:, :], ctx.t["c_ident"][:, :]),
         reads=[rcol, ctx.r["c_ident"]], writes=[rpaux], tag=nm + "_tr")
    p.op("vector", lambda e: e.tensor_reduce(sc[:, :], paux[0:1, 0:128], AX.X, flavor),
         reads=[rpaux], writes=[rsc], tag=nm + "_r2")
    return sc, rsc


def emit_ln_mod(ctx, tag, tile_src, ca1, sa, hT, rhT, n_tiles=16):
    p = ctx.p
    ca1_t, rca1 = ca1
    sa_t, rsa = sa
    mean, rmean = ctx.sb_shared("_lns_mean", [128, 1], F32)
    sq, rsq = ctx.sb_shared("_lns_sq", [128, 1024], F32)
    ssq, rssq = ctx.sb_shared("_lns_ssq", [128, 1], F32)
    var, rvar = ctx.sb_shared("_lns_var", [128, 1], F32)
    rstd, rrstd = ctx.sb_shared("_lns_rstd", [128, 1], F32)
    nmr, rnmr = ctx.sb_shared("_lns_nmr", [128, 1], F32)
    xn, rxn = ctx.sb_shared("_lns_xn", [128, 1024], F32)
    hbf, rhbf = ctx.sb_shared("_lns_hbf", [128, 1024], BF16)
    ptr, rptr = ctx.t["ptrb"], ctx.r["ptrb"]
    for t in range(n_tiles):
        xt, rx = tile_src(t)
        p.op("vector", lambda e, xt=xt: e.tensor_reduce(mean[:, :], xt, AX.X, ALU.add),
             reads=[rx], writes=[rmean], tag=f"{tag}_m{t}")
        p.op("vector", lambda e: e.tensor_scalar_mul(mean[:, :], mean[:, :], 1.0 / 1024),
             reads=[rmean], writes=[rmean], tag=f"{tag}_ms{t}")
        p.op("scalar", lambda e, xt=xt: e.activation(sq[:, :], xt, AF.Square, accum_out=ssq[:, :]),
             reads=[rx], writes=[rsq, rssq], tag=f"{tag}_sq{t}")
        p.op("vector", lambda e: e.scalar_tensor_tensor(var[:, :], mean[:, :], -1.0, mean[:, :], ALU.mult, ALU.mult),
             reads=[rmean], writes=[rvar], tag=f"{tag}_v1{t}")
        p.op("vector", lambda e: e.scalar_tensor_tensor(var[:, :], ssq[:, :], 1.0 / 1024, var[:, :], ALU.mult, ALU.add),
             reads=[rssq, rvar], writes=[rvar], tag=f"{tag}_v2{t}")
        eps_c, reps = ctx.const_col(1e-5)
        nh_c, rnh = ctx.const_col(-0.5)
        p.op("scalar", lambda e: e.activation(rstd[:, :], var[:, :], AF.Ln, bias=eps_c[:, :]),
             reads=[rvar, reps], writes=[rrstd], tag=f"{tag}_ln{t}")
        p.op("scalar", lambda e: e.activation(rstd[:, :], rstd[:, :], AF.Exp, scale=nh_c[:, :]),
             reads=[rrstd, rnh], writes=[rrstd], tag=f"{tag}_ex{t}")
        p.op("vector", lambda e: e.scalar_tensor_tensor(nmr[:, :], mean[:, :], -1.0, rstd[:, :], ALU.mult, ALU.mult),
             reads=[rmean, rrstd], writes=[rnmr], tag=f"{tag}_nm{t}")
        p.op("scalar", lambda e, xt=xt: e.activation(xn[:, :], xt, AF.Identity, bias=nmr[:, :], scale=rstd[:, :]),
             reads=[rx, rnmr, rrstd], writes=[rxn], tag=f"{tag}_xn{t}")
        p.op("vector", lambda e: e.tensor_mul(xn[:, :], xn[:, :], ca1_t[:, :]),
             reads=[rxn, rca1], writes=[rxn], tag=f"{tag}_sc{t}")
        p.op("vector", lambda e: e.tensor_add(hbf[:, :], xn[:, :], sa_t[:, :]),
             reads=[rxn, rsa], writes=[rhbf], tag=f"{tag}_ad{t}")

        def trs(e, t=t):
            inst = None
            for j in range(8):
                inst = e.transpose(ptr[:, j * 128:(j + 1) * 128], hbf[:, j * 128:(j + 1) * 128],
                                   ctx.t["c_ident_b"][:, :])
            return inst
        p.op("tensor", trs, reads=[rhbf, ctx.r["c_ident_b"]], writes=[rptr], tag=f"{tag}_tr{t}")
        tpp = hT.shape[1] // 8
        p.op("vector", lambda e, t=t, tpp=tpp: e.tensor_copy(
            bass.AP(hT, t * 128, [[8 * tpp, 128], [tpp, 8], [1, 128]]),
            ptr[:, :].rearrange("p (j q) -> p j q", j=8)),
            reads=[rptr], writes=[rhT], tag=f"{tag}_te{t}")


def make_x_stream(ctx, tag, x_in, n_tiles=16, nbuf=3):
    bufs = [ctx.sb(f"{tag}_xs{i}", [128, 1024], F32) for i in range(nbuf)]
    rbufs = [ctx.r[f"{tag}_xs{i}"] for i in range(nbuf)]

    def tile_src(t):
        b = t % nbuf
        ctx.p.dma("sync", lambda e, t=t, b=b: e.dma_start(
            out=bufs[b][:, :], in_=x_in[t * 128:(t + 1) * 128, :]),
            writes=[rbufs[b]], tag=f"{tag}_ldx{t}")
        return bufs[b][:, :], rbufs[b]
    return tile_src


def np_bf(x):
    return np.ascontiguousarray(x).astype(BFNP)


def build_l0_qkv(dbg=False):
    ctx = Ctx("l0qkv")
    nc, p = ctx.nc, ctx.p
    x_in = ctx.inp("x", [2048, 1024])
    c_in = ctx.inp("c_row", [1, 1024])
    wmod_in = ctx.inp("wmod", [1024, 6144])
    wat_in = ctx.inp("wattnT", [1024, 3072])
    qt_out = ctx.out("QT", [1024, 2048], BF16)
    kt_out = ctx.out("KT", [1024, 2048], BF16)
    v_out = ctx.out("V", [2048, 1024], BF16)

    ctx.psum_pool()
    ctx.consts()
    wstA = ctx.sb("wstA", [128, 3072], F32)
    wstB = ctx.sb("wstB", [128, 3072], F32)
    rwstA, rwstB = ctx.r["wstA"], ctx.r["wstB"]
    wst = [(wstA, rwstA), (wstB, rwstB)]

    mod = emit_mod(ctx, c_in, wmod_in, "mod0", sections=("sa", "ca1"), wst=wst)

    hT = ctx.sb("hT", [128, 8 * 2048], BF16)
    rhT = ctx.r["hT"]
    tile_src = make_x_stream(ctx, "l1x", x_in)
    emit_ln_mod(ctx, "ln0a", tile_src, mod["ca1"], mod["sa"], hT, rhT)

    if dbg:
        dsa = ctx.out("dbg_sa", [128, 1024])
        p.dma("sync", lambda e: e.dma_start(out=dsa, in_=ctx.t["mod0_sa"][:, :]),
              reads=[mod["sa"][1]], writes=[ctx.r["out:dbg_sa"]], tag="dbg_sa")
        dca = ctx.out("dbg_ca1", [128, 1024])
        p.dma("sync", lambda e: e.dma_start(out=dca, in_=ctx.t["mod0_ca1"][:, :]),
              reads=[mod["ca1"][1]], writes=[ctx.r["out:dbg_ca1"]], tag="dbg_ca")
        didb = ctx.out("dbg_idb", [128, 128], BF16)
        p.dma("sync", lambda e: e.dma_start(out=didb, in_=ctx.t["c_ident_b"][:, :]),
              reads=[ctx.r["c_ident_b"]], writes=[ctx.r["out:dbg_idb"]], tag="dbg_idb")
        dmod = ctx.out("dbg_modrow", [1, 6144])
        rmodrow_d = ctx.r["mod0_modrow"]
        p.dma("sync", lambda e: e.dma_start(out=dmod, in_=ctx.t["mod0_modrow"][:, :]),
              reads=[rmodrow_d], writes=[ctx.r["out:dbg_modrow"]], tag="dbg_mod")
        dhT = ctx.out("dbg_hT", [1024, 2048], BF16)
        for j in range(8):
            p.dma("sync", lambda e, j=j: e.dma_start(out=dhT[j * 128:(j + 1) * 128, :], in_=hT[:, j * 2048:(j + 1) * 2048]),
                  reads=[rhT], writes=[ctx.r["out:dbg_hT"]], tag=f"dbg_hT{j}")

    wbuf, rwbuf = [wstA, wstB], [rwstA, rwstB]
    acc8 = ctx.sb("acc8", [128, 8], F32)
    racc8 = ctx.r["acc8"]
    for c in range(8):
        b = c % 2
        p.dma("sync", lambda e, c=c, b=b: e.dma_start(out=wbuf[b][:, :], in_=wat_in[c * 128:(c + 1) * 128, :]),
              writes=[rwbuf[b]], tag=f"qw_ld{c}")
        p.op("vector", lambda e, c=c, b=b: e.tensor_reduce(acc8[:, c:c + 1], wbuf[b][:, :], AX.X, ALU.add, apply_absolute_value=True),
             reads=[rwbuf[b]], writes=[racc8], tag=f"qw_abs{c}")
    stot, rstot = emit_abs_reduce(ctx, "qws", acc8[:, :], racc8, flavor=ALU.add)
    p.op("vector", lambda e: e.tensor_scalar(stot[:, :], stot[:, :], 1.0 / (3072 * 1024), 1e-8, ALU.mult, ALU.max),
         reads=[rstot], writes=[rstot], tag="qw_sc")
    rinv = ctx.sb("qw_rinv", [1, 1], F32)
    rrinv = ctx.r["qw_rinv"]
    p.op("vector", lambda e: e.reciprocal(rinv[:, :], stot[:, :]), reads=[rstot], writes=[rrinv], tag="qw_rc")
    paux, rpaux = ctx.t["paux"], ctx.r["paux"]

    def bc2(e):
        e.matmul(paux[:, 0:1], ctx.t["c_ones"][0:1, :], rinv[:, :], start=True, stop=True)
        return e.matmul(paux[:, 1:2], ctx.t["c_ones"][0:1, :], stot[:, :], start=True, stop=True)
    p.op("tensor", bc2, reads=[rrinv, rstot, ctx.r["c_ones"]], writes=[rpaux], tag="qw_bc")
    rcol = ctx.sb("qw_rcol", [128, 2], F32)
    rrcol = ctx.r["qw_rcol"]
    p.op("vector", lambda e: e.tensor_copy(rcol[:, :], paux[:, 0:2]), reads=[rpaux], writes=[rrcol], tag="qw_bce")
    scq = ctx.sb("qw_scq", [128, 1], F32)
    rscq = ctx.r["qw_scq"]
    p.op("vector", lambda e: e.tensor_scalar_mul(scq[:, :], rcol[:, 1:2], 0.125),
         reads=[rrcol], writes=[rscq], tag="qw_scq")

    if dbg:
        dac = ctx.out("dbg_acc8", [128, 8])
        p.dma("sync", lambda e: e.dma_start(out=dac, in_=acc8[:, :]),
              reads=[racc8], writes=[ctx.r["out:dbg_acc8"]], tag="dbg_ac")
        dsc = ctx.out("dbg_sc", [1, 2])
        p.dma("sync", lambda e: e.dma_start(out=dsc[0:1, 0:1], in_=stot[:, :]),
              reads=[rstot], writes=[ctx.r["out:dbg_sc"]], tag="dbg_sc1")
        p.dma("sync", lambda e: e.dma_start(out=dsc[0:1, 1:2], in_=rinv[:, :]),
              reads=[rrinv], writes=[ctx.r["out:dbg_sc"]], tag="dbg_sc2")

    qw = ctx.sb("qw", [128, 8 * 3072], BF16)
    rqw = ctx.r["qw"]
    qtmp = ctx.sb("qtmp", [128, 3072], F32)
    rqtmp = ctx.r["qtmp"]
    for c in range(8):
        b = c % 2
        p.dma("sync", lambda e, c=c, b=b: e.dma_start(out=wbuf[b][:, :], in_=wat_in[c * 128:(c + 1) * 128, :]),
              writes=[rwbuf[b]], tag=f"qw_ld2_{c}")
        p.op("vector", lambda e, b=b: e.tensor_scalar(qtmp[:, :], wbuf[b][:, :], rcol[:, 0:1], RND_C, ALU.mult, ALU.add),
             reads=[rwbuf[b], rrcol], writes=[rqtmp], tag=f"qw_q1_{c}")
        p.op("vector", lambda e: e.tensor_scalar(qtmp[:, :], qtmp[:, :], RND_C, 2.0, ALU.subtract, ALU.min),
             reads=[rqtmp], writes=[rqtmp], tag=f"qw_q2_{c}")
        p.op("vector", lambda e, c=c: e.tensor_scalar_max(qw[:, c * 3072:(c + 1) * 3072], qtmp[:, :], -2.0),
             reads=[rqtmp], writes=[rqw], tag=f"qw_q3_{c}")

    pmm = [ctx.t[f"pmm{i}"] for i in range(4)]
    rpmm = [ctx.r[f"pmm{i}"] for i in range(4)]
    stg = [ctx.sb(f"qk_stg{i}", [128, 2048], BF16) for i in range(2)]
    rstg = [ctx.r[f"qk_stg{i}"] for i in range(2)]
    for j in range(16):
        sb_ = j % 2
        for tg in range(4):
            bank = tg % 4

            def mms(e, j=j, tg=tg, bank=bank):
                inst = None
                for c in range(8):
                    inst = e.matmul(pmm[bank][:, :], qw[:, c * 3072 + j * 128: c * 3072 + (j + 1) * 128],
                                    hT[:, c * 2048 + tg * 512: c * 2048 + (tg + 1) * 512],
                                    start=(c == 0), stop=(c == 7))
                return inst
            p.op("tensor", mms, reads=[rqw, rhT], writes=[rpmm[bank]], tag=f"qk_mm{j}_{tg}")
            scale = scq[:, 0:1] if j < 8 else rcol[:, 1:2]
            rscale = rscq if j < 8 else rrcol
            p.op("scalar", lambda e, tg=tg, bank=bank, sb_=sb_, scale=scale: e.activation(
                stg[sb_][:, tg * 512:(tg + 1) * 512], pmm[bank][:, :], AF.Identity, scale=scale),
                reads=[rpmm[bank], rscale], writes=[rstg[sb_]], tag=f"qk_ev{j}_{tg}")
        dst = qt_out if j < 8 else kt_out
        rdst = ctx.r["out:QT"] if j < 8 else ctx.r["out:KT"]
        jj = j % 8
        p.dma("sync", lambda e, jj=jj, sb_=sb_, dst=dst: e.dma_start(
            out=dst[jj * 128:(jj + 1) * 128, :], in_=stg[sb_][:, :]),
            reads=[rstg[sb_]], writes=[rdst], tag=f"qk_st{j}")

    vstg = [ctx.sb(f"v_stg{i}", [128, 1024], BF16) for i in range(2)]
    rvstg = [ctx.r[f"v_stg{i}"] for i in range(2)]
    for t in range(16):
        sb_ = t % 2
        for hf in range(2):
            bank = hf % 4

            def mmv(e, t=t, hf=hf, bank=bank):
                inst = None
                for c in range(8):
                    inst = e.matmul(pmm[bank][:, :], hT[:, c * 2048 + t * 128: c * 2048 + (t + 1) * 128],
                                    qw[:, c * 3072 + 2048 + hf * 512: c * 3072 + 2048 + (hf + 1) * 512],
                                    start=(c == 0), stop=(c == 7))
                return inst
            p.op("tensor", mmv, reads=[rqw, rhT], writes=[rpmm[bank]], tag=f"v_mm{t}_{hf}")
            p.op("scalar", lambda e, hf=hf, bank=bank, sb_=sb_: e.activation(
                vstg[sb_][:, hf * 512:(hf + 1) * 512], pmm[bank][:, :], AF.Identity, scale=rcol[:, 1:2]),
                reads=[rpmm[bank], rrcol], writes=[rvstg[sb_]], tag=f"v_ev{t}_{hf}")
        p.dma("sync", lambda e, t=t, sb_=sb_: e.dma_start(
            out=v_out[t * 128:(t + 1) * 128, :], in_=vstg[sb_][:, :]),
            reads=[rvstg[sb_]], writes=[ctx.r["out:V"]], tag=f"v_st{t}")

    p.build(nc)
    ctx.close()
    nc.reset()
    return ctx


FLASH_HEADS = 16
FLASH_QG = 4
FLASH_NORM = True


def build_l0_flash(half):
    ctx = Ctx("l0flash")
    nc, p = ctx.nc, ctx.p
    qt_in = ctx.inp("QTo", [1024, 2048], BF16)
    kt_in = ctx.inp("KTf", [1024, 4096], BF16)
    v_in = ctx.inp("Vf", [4096, 1024], BF16)
    yt_out = ctx.out("YT", [1024, 2048], BF16)

    ctx.psum_pool()
    ctx.consts()

    qt = ctx.sb("qt", [128, 8 * 2048], BF16)
    rqt = ctx.r["qt"]
    kt = ctx.sb("kt", [128, 8 * 4096], BF16)
    rkt = ctx.r["kt"]
    vsb = ctx.sb("vsb", [128, 32 * 1040], BF16)
    rvsb = ctx.r["vsb"]
    yt = ctx.sb("yt", [128, 8 * 2048], BF16)
    ryt = ctx.r["yt"]

    for j in range(8):
        p.dma("sync", lambda e, j=j: e.dma_start(out=qt[:, j * 2048:(j + 1) * 2048], in_=qt_in[j * 128:(j + 1) * 128, :]),
              writes=[rqt], tag=f"ldq{j}")
        p.dma("sync", lambda e, j=j: e.dma_start(out=kt[:, j * 4096:(j + 1) * 4096], in_=kt_in[j * 128:(j + 1) * 128, :]),
              writes=[rkt], tag=f"ldk{j}")
    p.op("gpsimd", lambda e: e.memset(bass.AP(vsb, 64, [[33280, 128], [65, 512]]), 1.0),
         writes=[rvsb], tag="vones")
    for t in range(32):
        p.dma("sync", lambda e, t=t: e.dma_start(
            out=bass.AP(vsb, t * 1040, [[33280, 128], [65, 16], [1, 64]]),
            in_=v_in[t * 128:(t + 1) * 128, :].rearrange("p (g d) -> p g d", g=16)),
            writes=[rvsb], tag=f"ldv{t}")

    sqs = ctx.sb("sqs", [128, 2048], BF16)
    rsqs = ctx.r["sqs"]
    n2 = ctx.sb("n2", [1, 33], F32)
    rn2 = ctx.r["n2"]
    paux, rpaux = ctx.t["paux"], ctx.r["paux"]
    norm_srcs = ((qt, rqt, 2048, 0, "q"), (kt, rkt, 4096, 16, "k")) if FLASH_NORM else ()
    for src_t, rsrc, width, base_col, tnm in norm_srcs:
        for ch in range(8):
            for piece in range(width // 2048):
                p.op("scalar", lambda e, src_t=src_t, ch=ch, width=width, piece=piece: e.activation(
                    sqs[:, :], src_t[:, ch * width + piece * 2048: ch * width + (piece + 1) * 2048], AF.Square),
                    reads=[rsrc], writes=[rsqs], tag=f"nsq_{tnm}{ch}_{piece}")
                for sub in range(2):
                    h = ch * 2 + sub
                    for p5 in range(4):
                        first = (piece == 0 and p5 == 0)

                        def nmm(e, sub=sub, p5=p5):
                            b2 = sub * 64
                            return e.matmul(paux[0:1, 0:512], ctx.t["c_ones_b"][b2:b2 + 64, 0:1],
                                            sqs[b2:b2 + 64, p5 * 512:(p5 + 1) * 512], start=True, stop=True)
                        p.op("tensor", nmm, reads=[rsqs, ctx.r["c_ones_b"]], writes=[rpaux], tag=f"nmm_{tnm}{h}_{piece}_{p5}")

                        def nred(e, h=h, base_col=base_col, first=first):
                            if first:
                                return e.tensor_reduce(n2[:, base_col + h:base_col + h + 1], paux[0:1, 0:512], AX.X, ALU.max)
                            e.tensor_reduce(n2[:, 32:33], paux[0:1, 0:512], AX.X, ALU.max)
                            return e.tensor_max(n2[:, base_col + h:base_col + h + 1],
                                                n2[:, base_col + h:base_col + h + 1], n2[:, 32:33])
                        p.op("vector", nred, reads=[rpaux, rn2], writes=[rn2], tag=f"nrd_{tnm}{h}_{piece}_{p5}")

    cr = ctx.sb("cr", [1, 16], F32)
    rcr = ctx.r["cr"]
    if not FLASH_NORM:
        p.op("gpsimd", lambda e: e.memset(n2[:, :], 100.0), writes=[rn2], tag="n2fake")
    p.op("scalar", lambda e: e.activation(n2[:, :], n2[:, :], AF.Sqrt), reads=[rn2], writes=[rn2], tag="nsqrt")
    p.op("vector", lambda e: e.tensor_mul(cr[:, :], n2[:, 0:16], n2[:, 16:32]), reads=[rn2], writes=[rcr], tag="cmul")
    p.op("vector", lambda e: e.tensor_scalar_mul(cr[:, :], cr[:, :], -0.125), reads=[rcr], writes=[rcr], tag="cneg")
    bcneg = ctx.sb("bcneg", [128, 16], F32)
    rbcneg = ctx.r["bcneg"]
    p.op("tensor", lambda e: e.matmul(paux[:, 0:16], ctx.t["c_ones"][0:1, :], cr[:, :], start=True, stop=True),
         reads=[rcr, ctx.r["c_ones"]], writes=[rpaux], tag="cbc")
    p.op("vector", lambda e: e.tensor_copy(bcneg[:, :], paux[:, 0:16]), reads=[rpaux], writes=[rbcneg], tag="cbce")

    pS = [ctx.t["pmm0"], ctx.t["pmm1"]]
    rpS = [ctx.r["pmm0"], ctx.r["pmm1"]]
    pAV, rpAV = ctx.t["pmm2"], ctx.r["pmm2"]
    pBC, rpBC = ctx.t["pmm3"], ctx.r["pmm3"]
    pt_sb = [ctx.sb(f"pt{i}", [128, 512], BF16) for i in range(2)]
    rpt_sb = [ctx.r[f"pt{i}"] for i in range(2)]
    ysb = ctx.sb("ysb", [65, 512], F32)
    rysb = ctx.r["ysb"]
    for h in range(FLASH_HEADS):
        ch, b = h // 2, (h % 2) * 64
        for qg in range(FLASH_QG):
            qstart = (half * 16 + qg * 4) * 128
            nkc = half * 16 + (qg + 1) * 4
            for kc in range(nkc):
                sb_ = kc % 2

                def smm(e, h=h, qg=qg, kc=kc, sb_=sb_, ch=ch, b=b):
                    return e.matmul(pS[sb_][:, :],
                                    kt[b:b + 64, ch * 4096 + kc * 128: ch * 4096 + (kc + 1) * 128],
                                    qt[b:b + 64, ch * 2048 + qg * 512: ch * 2048 + (qg + 1) * 512],
                                    start=True, stop=True)
                p.op("tensor", smm, reads=[rkt, rqt], writes=[rpS[sb_]], tag=f"S{h}_{qg}_{kc}")
                p.op("scalar", lambda e, h=h, sb_=sb_: e.activation(
                    pt_sb[sb_][:, :], pS[sb_][:, :], AF.Exp, bias=bcneg[:, h:h + 1]),
                    reads=[rpS[sb_], rbcneg], writes=[rpt_sb[sb_]], tag=f"E{h}_{qg}_{kc}")
                if kc * 128 + 127 > qstart:
                    p.op("gpsimd", lambda e, sb_=sb_, qstart=qstart, kc=kc: e.affine_select(
                        out=pt_sb[sb_][:, :], in_=pt_sb[sb_][:, :], compare_op=ALU.is_ge,
                        fill=0.0, base=qstart - kc * 128, pattern=[[1, 512]], channel_multiplier=-1),
                        reads=[rpt_sb[sb_]], writes=[rpt_sb[sb_]], tag=f"M{h}_{qg}_{kc}")

                def avmm(e, h=h, kc=kc, sb_=sb_, first=(kc == 0), last=(kc == nkc - 1)):
                    return e.matmul(pAV[0:65, :], vsb[:, kc * 1040 + h * 65: kc * 1040 + (h + 1) * 65],
                                    pt_sb[sb_][:, :], start=first, stop=last)
                p.op("tensor", avmm, reads=[rvsb, rpt_sb[sb_]], writes=[rpAV], tag=f"A{h}_{qg}_{kc}")
            p.op("vector", lambda e: e.tensor_copy(ysb[:, :], pAV[0:65, :]), reads=[rpAV], writes=[rysb], tag=f"Y{h}_{qg}")
            p.op("vector", lambda e: e.reciprocal(ysb[64:65, :], ysb[64:65, :]), reads=[rysb], writes=[rysb], tag=f"R{h}_{qg}")
            p.op("tensor", lambda e: e.matmul(pBC[0:64, :], ctx.t["c_ones"][64:65, 0:64], ysb[64:65, :], start=True, stop=True),
                 reads=[rysb, ctx.r["c_ones"]], writes=[rpBC], tag=f"B{h}_{qg}")
            p.op("vector", lambda e, ch=ch, b=b, qg=qg: e.tensor_tensor(
                yt[b:b + 64, ch * 2048 + qg * 512: ch * 2048 + (qg + 1) * 512],
                ysb[0:64, :], pBC[0:64, :], ALU.mult),
                reads=[rysb, rpBC], writes=[ryt], tag=f"W{h}_{qg}")

    for j in range(8):
        p.dma("sync", lambda e, j=j: e.dma_start(out=yt_out[j * 128:(j + 1) * 128, :], in_=yt[:, j * 2048:(j + 1) * 2048]),
              reads=[ryt], writes=[ctx.r["out:YT"]], tag=f"sty{j}")

    p.build(nc)
    ctx.close()
    nc.reset()
    return ctx


def emit_quant_weight(ctx, tag, w_in, n_chunks, ncols, wst, queue="sync"):
    p = ctx.p
    wbuf = [wst[0][0], wst[1][0]]
    rwbuf = [wst[0][1], wst[1][1]]
    accN = ctx.sb(tag + "_acc", [128, n_chunks], F32)
    racc = ctx.r[tag + "_acc"]
    for c in range(n_chunks):
        b = c % 2
        p.dma(queue, lambda e, c=c, b=b: e.dma_start(out=wbuf[b][:, 0:ncols], in_=w_in[c * 128:(c + 1) * 128, :]),
              writes=[rwbuf[b]], tag=f"{tag}_ld{c}")
        p.op("vector", lambda e, c=c, b=b: e.tensor_reduce(accN[:, c:c + 1], wbuf[b][:, 0:ncols], AX.X, ALU.add, apply_absolute_value=True),
             reads=[rwbuf[b]], writes=[racc], tag=f"{tag}_abs{c}")
    stot, rstot = emit_abs_reduce(ctx, tag + "s", accN[:, :], racc, flavor=ALU.add)
    p.op("vector", lambda e: e.tensor_scalar(stot[:, :], stot[:, :], 1.0 / (n_chunks * 128 * ncols), 1e-8, ALU.mult, ALU.max),
         reads=[rstot], writes=[rstot], tag=tag + "_sc")
    rinv = ctx.sb(tag + "_rinv", [1, 1], F32)
    rrinv = ctx.r[tag + "_rinv"]
    p.op("vector", lambda e: e.reciprocal(rinv[:, :], stot[:, :]), reads=[rstot], writes=[rrinv], tag=tag + "_rc")
    paux, rpaux = ctx.t["paux"], ctx.r["paux"]

    def bc2(e):
        e.matmul(paux[:, 0:1], ctx.t["c_ones"][0:1, :], rinv[:, :], start=True, stop=True)
        return e.matmul(paux[:, 1:2], ctx.t["c_ones"][0:1, :], stot[:, :], start=True, stop=True)
    p.op("tensor", bc2, reads=[rrinv, rstot, ctx.r["c_ones"]], writes=[rpaux], tag=tag + "_bc")
    rcol = ctx.sb(tag + "_rcol", [128, 2], F32)
    rrcol = ctx.r[tag + "_rcol"]
    p.op("vector", lambda e: e.tensor_copy(rcol[:, :], paux[:, 0:2]), reads=[rpaux], writes=[rrcol], tag=tag + "_bce")

    qw = ctx.sb(tag + "_qw", [128, n_chunks * ncols], BF16)
    rqw = ctx.r[tag + "_qw"]
    qtmp = ctx.sb(tag + "_qtmp", [128, ncols], F32)
    rqtmp = ctx.r[tag + "_qtmp"]
    for c in range(n_chunks):
        b = c % 2
        p.dma(queue, lambda e, c=c, b=b: e.dma_start(out=wbuf[b][:, 0:ncols], in_=w_in[c * 128:(c + 1) * 128, :]),
              writes=[rwbuf[b]], tag=f"{tag}_ld2_{c}")
        p.op("vector", lambda e, b=b: e.tensor_scalar(qtmp[:, :], wbuf[b][:, 0:ncols], rcol[:, 0:1], RND_C, ALU.mult, ALU.add),
             reads=[rwbuf[b], rrcol], writes=[rqtmp], tag=f"{tag}_q1_{c}")
        p.op("vector", lambda e: e.tensor_scalar(qtmp[:, :], qtmp[:, :], RND_C, 2.0, ALU.subtract, ALU.min),
             reads=[rqtmp], writes=[rqtmp], tag=f"{tag}_q2_{c}")
        p.op("vector", lambda e, c=c: e.tensor_scalar_max(qw[:, c * ncols:(c + 1) * ncols], qtmp[:, :], -2.0),
             reads=[rqtmp], writes=[rqw], tag=f"{tag}_q3_{c}")
    return qw, rqw, rcol, rrcol


def build_l0_route():
    ctx = Ctx("l0route")
    nc, p = ctx.nc, ctx.p
    x_in = ctx.inp("x", [2048, 1024])
    yt_in = ctx.inp("YT", [1024, 2048], BF16)
    wproj_in = ctx.inp("wprojT", [1024, 1024])
    c_in = ctx.inp("c_row", [1, 1024])
    wmod_in = ctx.inp("wmod", [1024, 6144])
    gatew_in = ctx.inp("gatewT", [1024, 9])
    x1_out = ctx.out("x1", [2048, 1024])
    h1t_out = ctx.out("h1T", [1024, 2048], BF16)
    rw_out = ctx.out("rw", [2048, 8])
    cand_out = ctx.out("cand", [2048, 8])
    lg_out = ctx.out("logits", [2048, 9])
    cnt_out = ctx.out("cnt", [1, 8])

    ctx.psum_pool()
    ctx.consts()
    wstA = ctx.sb("wstA", [128, 3072], F32)
    wstB = ctx.sb("wstB", [128, 3072], F32)
    wst = [(wstA, ctx.r["wstA"]), (wstB, ctx.r["wstB"])]
    mod = emit_mod(ctx, c_in, wmod_in, "mod0", sections=("ga", "sb", "cb1"), wst=wst)

    qwp, rqwp, pcol, rpcol = emit_quant_weight(ctx, "qp", wproj_in, 8, 1024, wst)
    gas = ctx.sb("gas", [128, 1024], F32)
    rgas = ctx.r["gas"]
    p.op("vector", lambda e: e.tensor_scalar(gas[:, :], ctx.t["mod0_ga"][:, :], pcol[:, 1:2], None, ALU.mult),
         reads=[mod["ga"][1], rpcol], writes=[rgas], tag="gas")

    yt = ctx.sb("yt", [128, 8 * 2048], BF16)
    ryt = ctx.r["yt"]
    for j in range(8):
        p.dma("sync", lambda e, j=j: e.dma_start(out=yt[:, j * 2048:(j + 1) * 2048], in_=yt_in[j * 128:(j + 1) * 128, :]),
              writes=[ryt], tag=f"ldy{j}")

    xbufs = [ctx.sb(f"xb{i}", [128, 1024], F32) for i in range(3)]
    rxbufs = [ctx.r[f"xb{i}"] for i in range(3)]
    tmpo = ctx.sb("tmpo", [128, 1024], F32)
    rtmpo = ctx.r["tmpo"]
    pwide, rpwide = ctx.t["pwide"], ctx.r["pwide"]

    def tile_src(t):
        b = t % 3
        p.dma("sync", lambda e, t=t, b=b: e.dma_start(out=xbufs[b][:, :], in_=x_in[t * 128:(t + 1) * 128, :]),
              writes=[rxbufs[b]], tag=f"ldx{t}")

        def mms(e, t=t):
            inst = None
            for hf in range(2):
                for c in range(8):
                    inst = e.matmul(pwide[:, hf * 512:(hf + 1) * 512],
                                    yt[:, c * 2048 + t * 128: c * 2048 + (t + 1) * 128],
                                    qwp[:, c * 1024 + hf * 512: c * 1024 + (hf + 1) * 512],
                                    start=(c == 0), stop=(c == 7))
            return inst
        p.op("tensor", mms, reads=[ryt, rqwp], writes=[rpwide], tag=f"op_mm{t}")
        p.op("vector", lambda e: e.tensor_mul(tmpo[:, :], pwide[:, :], gas[:, :]),
             reads=[rpwide, rgas], writes=[rtmpo], tag=f"op_sc{t}")
        p.op("vector", lambda e, b=b: e.tensor_add(xbufs[b][:, :], xbufs[b][:, :], tmpo[:, :]),
             reads=[rtmpo, rxbufs[b]], writes=[rxbufs[b]], tag=f"op_ad{t}")
        p.dma("sync", lambda e, t=t, b=b: e.dma_start(out=x1_out[t * 128:(t + 1) * 128, :], in_=xbufs[b][:, :]),
              reads=[rxbufs[b]], writes=[ctx.r["out:x1"]], tag=f"stx1_{t}")
        return xbufs[b][:, :], rxbufs[b]

    hT = ctx.sb("hT", [128, 8 * 2048], BF16)
    rhT = ctx.r["hT"]
    emit_ln_mod(ctx, "ln0b", tile_src, mod["cb1"], mod["sb"], hT, rhT)
    for j in range(8):
        p.dma("sync", lambda e, j=j: e.dma_start(out=h1t_out[j * 128:(j + 1) * 128, :], in_=hT[:, j * 2048:(j + 1) * 2048]),
              reads=[rhT], writes=[ctx.r["out:h1T"]], tag=f"sth{j}")

    gw_sb = ctx.sb("gw_sb", [128, 8 * 9], BF16)
    rgw_sb = ctx.r["gw_sb"]
    p.dma("gpsimd", lambda e: e.dma_start(out=gw_sb[:, :].rearrange("p (c n) -> p c n", c=8),
                                          in_=gatew_in.rearrange("(c p) n -> p c n", p=128)),
          writes=[rgw_sb], tag="ldgw")
    lg = ctx.sb("lg", [128, 16 * 9], F32)
    rlg = ctx.r["lg"]
    paux, rpaux = ctx.t["paux"], ctx.r["paux"]
    for t in range(16):
        def lmm(e, t=t):
            inst = None
            for c in range(8):
                inst = e.matmul(paux[:, 0:9], hT[:, c * 2048 + t * 128: c * 2048 + (t + 1) * 128],
                                gw_sb[:, c * 9:(c + 1) * 9], start=(c == 0), stop=(c == 7))
            return inst
        p.op("tensor", lmm, reads=[rhT, rgw_sb], writes=[rpaux], tag=f"lg_mm{t}")
        p.op("vector", lambda e, t=t: e.tensor_copy(lg[:, t * 9:(t + 1) * 9], paux[:, 0:9]),
             reads=[rpaux], writes=[rlg], tag=f"lg_ev{t}")
        p.dma("sync", lambda e, t=t: e.dma_start(out=lg_out[t * 128:(t + 1) * 128, :], in_=lg[:, t * 9:(t + 1) * 9]),
              reads=[rlg], writes=[ctx.r["out:logits"]], tag=f"stlg{t}")

    def dynv(tsr, off=0):
        return bass.AP(tsr, off, [[144, 128], [9, 16], [1, 8]])
    def s128(nm):
        h = ctx.sb(nm, [128, 128], F32)
        return h, ctx.r[nm]
    ms, rms = s128("ms")
    absS, rabsS = s128("absS")
    probs, rprobs = s128("probs")
    ex, rex = s128("ex")
    eqs, reqs = s128("eqs")
    scr, rscr = s128("scr")
    scr2, rscr2 = s128("scr2")
    rw_t, rrw = s128("rw_t")
    iота = None
    iot, riot = s128("iot")
    c16 = lambda nm: (ctx.sb(nm, [128, 16], F32), ctx.r[nm])
    thr, rthr = c16("thr")
    selv, rselv = c16("selv")
    lsum, rlsum = c16("lsum")
    csum, rcsum = c16("csum")
    dynk, rdynk = c16("dynk")
    t16, rt16 = c16("t16")
    mults = ctx.sb("mults", [128, 64], F32)
    rmults = ctx.r["mults"]
    selvs = ctx.sb("selvs", [128, 64], F32)
    rselvs = ctx.r["selvs"]

    def v3(tsr, n=8):
        return bass.AP(tsr, 0, [[128, 128], [8, 16], [1, n]])
    def b16(tsr):
        return bass.AP(tsr, 0, [[16, 128], [1, 16], [0, 8]])

    p.op("gpsimd", lambda e: e.iota(iot[:, :].rearrange("p (t e) -> p t e", t=16), [[0, 16], [1, 8]], channel_multiplier=0,
                                    allow_small_or_imprecise_dtypes=True),
         writes=[riot], tag="iota")
    p.op("vector", lambda e: e.tensor_copy(v3(ms), dynv(lg)), reads=[rlg], writes=[rms], tag="mscp")
    p.op("scalar", lambda e: e.activation(absS[:, :], ms[:, :], AF.Abs), reads=[rms], writes=[rabsS], tag="absS")
    p.op("vector", lambda e: e.tensor_reduce(thr[:, :], v3(ms), AX.X, ALU.max), reads=[rms], writes=[rthr], tag="p_max")
    p.op("vector", lambda e: e.tensor_tensor(v3(scr), v3(ms), b16(thr), ALU.subtract), reads=[rms, rthr], writes=[rscr], tag="p_sub")
    p.op("scalar", lambda e: e.activation(probs[:, :], scr[:, :], AF.Exp), reads=[rscr], writes=[rprobs], tag="p_exp")
    p.op("vector", lambda e: e.tensor_reduce(lsum[:, :], v3(probs), AX.X, ALU.add), reads=[rprobs], writes=[rlsum], tag="p_sum")
    p.op("vector", lambda e: e.reciprocal(t16[:, :], lsum[:, :]), reads=[rlsum], writes=[rt16], tag="p_rc")
    p.op("vector", lambda e: e.tensor_tensor(v3(probs), v3(probs), b16(t16), ALU.mult), reads=[rprobs, rt16], writes=[rprobs], tag="p_nm")

    p.op("gpsimd", lambda e: e.memset(rw_t[:, :], 0.0), writes=[rrw], tag="rw0")
    p.op("gpsimd", lambda e: e.memset(csum[:, :], 0.0), writes=[rcsum], tag="cs0")
    p.op("gpsimd", lambda e: e.memset(dynk[:, :], 1.0), writes=[rdynk], tag="dk0")

    for it in range(4):
        p.op("vector", lambda e: e.tensor_reduce(thr[:, :], v3(ms), AX.X, ALU.max), reads=[rms], writes=[rthr], tag=f"i{it}_max")
        p.op("vector", lambda e: e.tensor_tensor(v3(eqs), v3(ms), b16(thr), ALU.is_ge), reads=[rms, rthr], writes=[reqs], tag=f"i{it}_eq")
        p.op("vector", lambda e: e.scalar_tensor_tensor(scr[:, :], iot[:, :], 9.0, eqs[:, :], ALU.subtract, ALU.mult),
             reads=[riot, reqs], writes=[rscr], tag=f"i{it}_s1")
        p.op("vector", lambda e: e.tensor_scalar_add(scr2[:, :], scr[:, :], 9.0),
             reads=[rscr], writes=[rscr2], tag=f"i{it}_s2")
        p.op("vector", lambda e: e.tensor_reduce(selv[:, :], v3(scr2), AX.X, ALU.min), reads=[rscr2], writes=[rselv], tag=f"i{it}_argm")
        p.op("vector", lambda e, it=it: e.tensor_copy(
            bass.AP(selvs, it, [[64, 128], [4, 16]]), selv[:, :]), reads=[rselv], writes=[rselvs], tag=f"i{it}_selst")
        p.op("scalar", lambda e: e.activation(t16[:, :], thr[:, :], AF.Abs), reads=[rthr], writes=[rt16], tag=f"i{it}_athr")
        p.op("vector", lambda e: e.tensor_tensor(v3(scr), v3(absS), b16(t16), ALU.max), reads=[rabsS, rt16], writes=[rscr], tag=f"i{it}_fac")
        p.op("vector", lambda e: e.tensor_tensor(v3(scr2), b16(thr), dynv(lg), ALU.subtract), reads=[rthr, rlg], writes=[rscr2], tag=f"i{it}_d1")
        p.op("vector", lambda e: e.scalar_tensor_tensor(scr[:, :], scr[:, :], -0.2, scr2[:, :], ALU.mult, ALU.add),
             reads=[rscr, rscr2], writes=[rscr], tag=f"i{it}_d2")
        p.op("vector", lambda e: e.tensor_scalar(scr[:, :], scr[:, :], 0.0, None, ALU.is_gt), reads=[rscr], writes=[rscr], tag=f"i{it}_jm")
        p.op("vector", lambda e: e.scalar_tensor_tensor(scr2[:, :], scr[:, :], -1e30, ms[:, :], ALU.mult, ALU.add),
             reads=[rscr, rms], writes=[rscr2], tag=f"i{it}_mk")
        p.op("vector", lambda e: e.tensor_reduce(t16[:, :], v3(scr2), AX.X, ALU.max), reads=[rscr2], writes=[rt16], tag=f"i{it}_gm")
        p.op("vector", lambda e: e.tensor_tensor(v3(scr2), v3(scr2), b16(t16), ALU.subtract), reads=[rscr2, rt16], writes=[rscr2], tag=f"i{it}_gs")
        p.op("scalar", lambda e: e.activation(ex[:, :], scr2[:, :], AF.Exp), reads=[rscr2], writes=[rex], tag=f"i{it}_ge")
        p.op("vector", lambda e: e.tensor_reduce(lsum[:, :], v3(ex), AX.X, ALU.add), reads=[rex], writes=[rlsum], tag=f"i{it}_gl")
        p.op("vector", lambda e: e.tensor_tensor(v3(eqs), v3(iot), b16(selv), ALU.is_equal), reads=[riot, rselv], writes=[reqs], tag=f"i{it}_es")
        p.op("vector", lambda e: e.tensor_tensor(v3(scr), v3(ex), v3(eqs), ALU.mult), reads=[rex, reqs], writes=[rscr], tag=f"i{it}_gn")
        p.op("vector", lambda e: e.tensor_reduce(t16[:, :], v3(scr), AX.X, ALU.add), reads=[rscr], writes=[rt16], tag=f"i{it}_gn2")
        p.op("vector", lambda e: e.reciprocal(lsum[:, :], lsum[:, :]), reads=[rlsum], writes=[rlsum], tag=f"i{it}_gr")
        p.op("vector", lambda e: e.tensor_mul(t16[:, :], t16[:, :], lsum[:, :]), reads=[rt16, rlsum], writes=[rt16], tag=f"i{it}_mu")
        p.op("vector", lambda e, it=it: e.tensor_copy(
            bass.AP(mults, it, [[64, 128], [4, 16]]), t16[:, :]), reads=[rt16], writes=[rmults], tag=f"i{it}_mst")
        p.op("vector", lambda e: e.tensor_tensor(v3(scr), v3(probs), v3(eqs), ALU.mult), reads=[rprobs, reqs], writes=[rscr], tag=f"i{it}_pi")
        p.op("vector", lambda e: e.tensor_reduce(t16[:, :], v3(scr), AX.X, ALU.add), reads=[rscr], writes=[rt16], tag=f"i{it}_pi2")
        p.op("vector", lambda e: e.tensor_add(csum[:, :], csum[:, :], t16[:, :]), reads=[rcsum, rt16], writes=[rcsum], tag=f"i{it}_cs")
        if it < 3:
            p.op("vector", lambda e: e.tensor_scalar(t16[:, :], csum[:, :], 0.75, None, ALU.is_lt), reads=[rcsum], writes=[rt16], tag=f"i{it}_ck")
            p.op("vector", lambda e: e.tensor_add(dynk[:, :], dynk[:, :], t16[:, :]), reads=[rdynk, rt16], writes=[rdynk], tag=f"i{it}_dk")
        p.op("vector", lambda e: e.scalar_tensor_tensor(ms[:, :], eqs[:, :], -1e30, ms[:, :], ALU.mult, ALU.add),
             reads=[reqs, rms], writes=[rms], tag=f"i{it}_msu")

    for it in range(4):
        p.op("vector", lambda e, it=it: e.tensor_scalar(t16[:, :], dynk[:, :], float(-it), 0.0, ALU.add, ALU.is_gt),
             reads=[rdynk], writes=[rt16], tag=f"r{it}_v")
        p.op("vector", lambda e, it=it: e.tensor_mul(t16[:, :], t16[:, :],
             bass.AP(mults, it, [[64, 128], [4, 16]])), reads=[rt16, rmults], writes=[rt16], tag=f"r{it}_m")
        p.op("vector", lambda e, it=it: e.tensor_tensor(v3(eqs), v3(iot),
             bass.AP(selvs, it, [[64, 128], [4, 16], [0, 8]]), ALU.is_equal), reads=[riot, rselvs], writes=[reqs], tag=f"r{it}_e")
        p.op("vector", lambda e: e.tensor_tensor(v3(scr), v3(eqs), b16(t16), ALU.mult), reads=[reqs, rt16], writes=[rscr], tag=f"r{it}_x")
        p.op("vector", lambda e: e.tensor_add(rw_t[:, :], rw_t[:, :], scr[:, :]), reads=[rrw, rscr], writes=[rrw], tag=f"r{it}_a")
    p.op("vector", lambda e: e.tensor_reduce(t16[:, :], v3(rw_t), AX.X, ALU.add), reads=[rrw], writes=[rt16], tag="rw_s")
    p.op("vector", lambda e: e.reciprocal(t16[:, :], t16[:, :]), reads=[rt16], writes=[rt16], tag="rw_r")
    p.op("vector", lambda e: e.tensor_tensor(v3(rw_t), v3(rw_t), b16(t16), ALU.mult), reads=[rrw, rt16], writes=[rrw], tag="rw_n")
    p.op("vector", lambda e: e.tensor_scalar(scr[:, :], rw_t[:, :], 0.0, None, ALU.is_gt), reads=[rrw], writes=[rscr], tag="cand")
    p.op("tensor", lambda e: e.matmul(paux[0:1, 0:128], ctx.t["c_ones"][:, 0:1], scr[:, :], start=True, stop=True),
         reads=[rscr, ctx.r["c_ones"]], writes=[rpaux], tag="cnt_mm")
    cnt_sb = ctx.sb("cnt_sb", [1, 8], F32)
    rcnt = ctx.r["cnt_sb"]
    p.op("vector", lambda e: e.tensor_reduce(cnt_sb[:, :], bass.AP(paux, 0, [[512, 1], [1, 8], [8, 16]]), AX.X, ALU.add),
         reads=[rpaux], writes=[rcnt], tag="cnt_rd")

    for t in range(16):
        p.dma("sync", lambda e, t=t: e.dma_start(out=rw_out[t * 128:(t + 1) * 128, :], in_=rw_t[:, t * 8:(t + 1) * 8]),
              reads=[rrw], writes=[ctx.r["out:rw"]], tag=f"strw{t}")
        p.dma("sync", lambda e, t=t: e.dma_start(out=cand_out[t * 128:(t + 1) * 128, :], in_=scr[:, t * 8:(t + 1) * 8]),
              reads=[rscr], writes=[ctx.r["out:cand"]], tag=f"stcd{t}")
    p.dma("sync", lambda e: e.dma_start(out=cnt_out, in_=cnt_sb[:, :]), reads=[rcnt], writes=[ctx.r["out:cnt"]], tag="stcnt")

    p.build(nc)
    ctx.close()
    nc.reset()
    return ctx


def build_l0_moe():
    ctx = Ctx("l0moe")
    nc, p = ctx.nc, ctx.p
    x_in = ctx.inp("x1", [2048, 1024])
    h_in = ctx.inp("h1T", [1024, 2048], BF16)
    rw_in = ctx.inp("rw", [2048, 8])
    cand_in = ctx.inp("cand", [2048, 8])
    lg_in = ctx.inp("logits", [2048, 9])
    cnts_in = ctx.inp("cnts", [8, 8])
    pmask_in = ctx.inp("pmask", [8, 1])
    dg_in = ctx.inp("dgT", [8, 1024, 512])
    du_in = ctx.inp("duT", [8, 1024, 512])
    dd_in = ctx.inp("ddT", [8, 512, 1024])
    sg_in = ctx.inp("sgT", [1024, 2048])
    su_in = ctx.inp("suT", [1024, 2048])
    sd_in = ctx.inp("sdT", [2048, 1024])
    c_in = ctx.inp("c_row", [1, 1024])
    wmod_in = ctx.inp("wmod", [1024, 6144])
    x2_out = ctx.out("x2", [2048, 1024])

    ctx.psum_pool()
    ctx.consts()
    wstA = ctx.sb("wstA", [128, 1536], F32)
    wstB = ctx.sb("wstB", [128, 1536], F32)
    wst = [(wstA, ctx.r["wstA"]), (wstB, ctx.r["wstB"])]
    mod = emit_mod(ctx, c_in, wmod_in, "mod0", sections=("gb",), wst=wst)
    gb_t, rgb = mod["gb"]
    paux, rpaux = ctx.t["paux"], ctx.r["paux"]
    pwide, rpwide = ctx.t["pwide"], ctx.r["pwide"]
    pmm = [ctx.t[f"pmm{i}"] for i in range(4)]
    rpmm = [ctx.r[f"pmm{i}"] for i in range(4)]

    xsb = ctx.sb("xsb", [128, 16 * 1024], F32)
    rx = ctx.r["xsb"]
    for t in range(16):
        p.dma("sync", lambda e, t=t: e.dma_start(out=xsb[:, t * 1024:(t + 1) * 1024], in_=x_in[t * 128:(t + 1) * 128, :]),
              writes=[rx], tag=f"ldx{t}")
    hT = ctx.sb("hT", [128, 8 * 2048], BF16)
    rhT = ctx.r["hT"]
    for j in range(8):
        p.dma("sync", lambda e, j=j: e.dma_start(out=hT[:, j * 2048:(j + 1) * 2048], in_=h_in[j * 128:(j + 1) * 128, :]),
              writes=[rhT], tag=f"ldh{j}")

    rw_t = ctx.sb("rw_t", [128, 128], F32)
    rrw = ctx.r["rw_t"]
    cnd = ctx.sb("cnd", [128, 128], F32)
    rcnd = ctx.r["cnd"]
    cndb = ctx.sb("cndb", [128, 128], BF16)
    rcndb = ctx.r["cndb"]
    lg9 = ctx.sb("lg9", [128, 144], F32)
    rlg9 = ctx.r["lg9"]
    for t in range(16):
        p.dma("sync", lambda e, t=t: e.dma_start(out=rw_t[:, t * 8:(t + 1) * 8], in_=rw_in[t * 128:(t + 1) * 128, :]),
              writes=[rrw], tag=f"ldrw{t}")
        p.dma("sync", lambda e, t=t: e.dma_start(out=cnd[:, t * 8:(t + 1) * 8], in_=cand_in[t * 128:(t + 1) * 128, :]),
              writes=[rcnd], tag=f"ldcd{t}")
        p.dma("sync", lambda e, t=t: e.dma_start(out=lg9[:, t * 9:(t + 1) * 9], in_=lg_in[t * 128:(t + 1) * 128, :]),
              writes=[rlg9], tag=f"ldlg{t}")
    p.op("vector", lambda e: e.tensor_copy(cndb[:, :], cnd[:, :]), reads=[rcnd], writes=[rcndb], tag="cndb")

    cnts_sb = ctx.sb("cnts_sb", [8, 8], F32)
    rcnts = ctx.r["cnts_sb"]
    pmask_sb = ctx.sb("pmask_sb", [8, 1], F32)
    rpmask = ctx.r["pmask_sb"]
    p.dma("sync", lambda e: e.dma_start(out=cnts_sb[:, :], in_=cnts_in), writes=[rcnts], tag="ldcn")
    p.dma("sync", lambda e: e.dma_start(out=pmask_sb[:, :], in_=pmask_in), writes=[rpmask], tag="ldpm")
    carry = ctx.sb("carry", [1, 8], F32)
    rcarry = ctx.r["carry"]
    p.op("tensor", lambda e: e.matmul(paux[0:1, 0:8], pmask_sb[:, :], cnts_sb[:, :], start=True, stop=True),
         reads=[rpmask, rcnts], writes=[rpaux], tag="ofs_mm")
    p.op("vector", lambda e: e.tensor_copy(carry[:, :], paux[0:1, 0:8]), reads=[rpaux], writes=[rcarry], tag="ofs_ev")

    ut = ctx.sb("ut", [128, 128], BF16)
    rut = ctx.r["ut"]
    p.op("gpsimd", lambda e: e.memset(ut[:, :], 1.0), writes=[rut], tag="ut0")
    p.op("gpsimd", lambda e: e.affine_select(out=ut[:, :], in_=ut[:, :], compare_op=ALU.is_ge,
                                             fill=0.0, base=0, pattern=[[1, 128]], channel_multiplier=-1),
         reads=[rut], writes=[rut], tag="ut1")

    kept = ctx.sb("kept", [128, 128], F32)
    rkept = ctx.r["kept"]
    carryb = ctx.sb("carryb", [128, 8], F32)
    rcarryb = ctx.r["carryb"]
    ttot = ctx.sb("ttot", [1, 8], F32)
    rttot = ctx.r["ttot"]
    cum = ctx.sb("cum", [128, 8], F32)
    rcum = ctx.r["cum"]
    for t in range(16):
        p.op("tensor", lambda e: e.matmul(paux[:, 8:16], ctx.t["c_ones"][0:1, :], carry[:, :], start=True, stop=True),
             reads=[rcarry, ctx.r["c_ones"]], writes=[rpaux], tag=f"cp_cb{t}")
        p.op("vector", lambda e: e.tensor_copy(carryb[:, :], paux[:, 8:16]), reads=[rpaux], writes=[rcarryb], tag=f"cp_cbe{t}")
        p.op("tensor", lambda e, t=t: e.matmul(paux[:, 0:8], ut[:, :], cndb[:, t * 8:(t + 1) * 8], start=True, stop=True),
             reads=[rut, rcndb], writes=[rpaux], tag=f"cp_mm{t}")
        p.op("vector", lambda e: e.tensor_add(cum[:, :], paux[:, 0:8], carryb[:, :]), reads=[rpaux, rcarryb], writes=[rcum], tag=f"cp_cum{t}")
        p.op("vector", lambda e: e.tensor_scalar(cum[:, :], cum[:, :], 2560.0, None, ALU.is_le), reads=[rcum], writes=[rcum], tag=f"cp_le{t}")
        p.op("vector", lambda e, t=t: e.tensor_tensor(kept[:, t * 8:(t + 1) * 8], cum[:, :], cnd[:, t * 8:(t + 1) * 8], ALU.mult),
             reads=[rcum, rcnd], writes=[rkept], tag=f"cp_k{t}")
        p.op("tensor", lambda e, t=t: e.matmul(paux[0:1, 0:8], ctx.t["c_ones_b"][:, 0:1], cndb[:, t * 8:(t + 1) * 8], start=True, stop=True),
             reads=[rcndb, ctx.r["c_ones_b"]], writes=[rpaux], tag=f"cp_tt{t}")
        p.op("vector", lambda e: e.tensor_copy(ttot[:, :], paux[0:1, 0:8]), reads=[rpaux], writes=[rttot], tag=f"cp_tte{t}")
        p.op("vector", lambda e: e.tensor_add(carry[:, :], carry[:, :], ttot[:, :]), reads=[rcarry, rttot], writes=[rcarry], tag=f"cp_ca{t}")

    msk9 = ctx.sb("msk9", [128, 144], F32)
    rmsk9 = ctx.r["msk9"]
    p.op("gpsimd", lambda e: e.memset(msk9[:, :], 1.0), writes=[rmsk9], tag="m9s")
    p.op("vector", lambda e: e.tensor_copy(bass.AP(msk9, 0, [[144, 128], [9, 16], [1, 8]]),
                                           bass.AP(kept, 0, [[128, 128], [8, 16], [1, 8]])),
         reads=[rkept], writes=[rmsk9], tag="m9c")
    glw = ctx.sb("glw", [128, 144], F32)
    rglw = ctx.r["glw"]
    t16b = ctx.sb("t16b", [128, 16], F32)
    rt16b = ctx.r["t16b"]

    def v9(tsr):
        return bass.AP(tsr, 0, [[144, 128], [9, 16], [1, 9]])
    def b169(tsr):
        return bass.AP(tsr, 0, [[16, 128], [1, 16], [0, 9]])
    def v8(tsr):
        return bass.AP(tsr, 0, [[144, 128], [9, 16], [1, 8]])
    p.op("vector", lambda e: e.tensor_scalar(glw[:, :], msk9[:, :], -1.0, 1e30, ALU.add, ALU.mult),
         reads=[rmsk9], writes=[rglw], tag="gw0")
    p.op("vector", lambda e: e.tensor_add(glw[:, :], glw[:, :], lg9[:, :]), reads=[rglw, rlg9], writes=[rglw], tag="gw1")
    p.op("vector", lambda e: e.tensor_reduce(t16b[:, :], v9(glw), AX.X, ALU.max), reads=[rglw], writes=[rt16b], tag="gw2")
    p.op("vector", lambda e: e.tensor_tensor(v9(glw), v9(glw), b169(t16b), ALU.subtract), reads=[rglw, rt16b], writes=[rglw], tag="gw3")
    p.op("scalar", lambda e: e.activation(glw[:, :], glw[:, :], AF.Exp), reads=[rglw], writes=[rglw], tag="gw4")
    p.op("vector", lambda e: e.tensor_reduce(t16b[:, :], v9(glw), AX.X, ALU.add), reads=[rglw], writes=[rt16b], tag="gw5")
    p.op("vector", lambda e: e.reciprocal(t16b[:, :], t16b[:, :]), reads=[rt16b], writes=[rt16b], tag="gw6")
    p.op("vector", lambda e: e.tensor_tensor(v9(glw), v9(glw), b169(t16b), ALU.mult), reads=[rglw, rt16b], writes=[rglw], tag="gw7")
    p.op("vector", lambda e: e.tensor_reduce(t16b[:, :], v8(glw), AX.X, ALU.add), reads=[rglw], writes=[rt16b], tag="gw8")
    p.op("vector", lambda e: e.tensor_tensor(v8(glw), bass.AP(rw_t, 0, [[128, 128], [8, 16], [1, 8]]),
                                             bass.AP(t16b, 0, [[16, 128], [1, 16], [0, 8]]), ALU.mult),
         reads=[rrw, rt16b], writes=[rglw], tag="gw9")
    p.op("vector", lambda e: e.tensor_tensor(v8(glw), v8(glw), bass.AP(kept, 0, [[128, 128], [8, 16], [1, 8]]), ALU.mult),
         reads=[rglw, rkept], writes=[rglw], tag="gw10")

    gbb = ctx.sb("gbb", [128, 1024], BF16)
    rgbb = ctx.r["gbb"]
    p.op("vector", lambda e: e.tensor_copy(gbb[:, :], gb_t[:, :]), reads=[rgb], writes=[rgbb], tag="gbb")

    gbuf = ctx.sb("gbuf", [128, 8 * 512], BF16)
    rgbuf = ctx.r["gbuf"]
    ubuf = ctx.sb("ubuf", [128, 8 * 512], BF16)
    rubuf = ctx.r["ubuf"]
    dbuf = ctx.sb("dbuf", [128, 4 * 1024], BF16)
    rdbuf = ctx.r["dbuf"]
    act = ctx.sb("act", [128, 4 * 2048], BF16)
    ract = ctx.r["act"]
    sgt = ctx.sb("sgt", [128, 512], BF16)
    rsgt = ctx.r["sgt"]

    iters = []
    for e_ in range(8):
        iters.append(dict(I=512, gsrc=dg_in[e_], usrc=du_in[e_], dsrc=dd_in[e_],
                          drow0=0, gcol=e_, nm=f"d{e_}"))
    for hf_ in range(4):
        iters.append(dict(I=512, gsrc=sg_in[:, hf_ * 512:(hf_ + 1) * 512],
                          usrc=su_in[:, hf_ * 512:(hf_ + 1) * 512],
                          dsrc=sd_in[hf_ * 512:(hf_ + 1) * 512, :],
                          drow0=0, gcol=8, nm=f"s{hf_}"))

    for it in iters:
        I, nm = it["I"], it["nm"]
        nj = I // 128
        for c in range(8):
            p.dma("gpsimd", lambda e, c=c, it=it, I=I: e.dma_start(
                out=gbuf[:, c * I:(c + 1) * I], in_=it["gsrc"][c * 128:(c + 1) * 128, :]),
                writes=[rgbuf], tag=f"{nm}_lg{c}")
            p.dma("gpsimd", lambda e, c=c, it=it, I=I: e.dma_start(
                out=ubuf[:, c * I:(c + 1) * I], in_=it["usrc"][c * 128:(c + 1) * 128, :]),
                writes=[rubuf], tag=f"{nm}_lu{c}")
        for j in range(nj):
            p.dma("gpsimd", lambda e, j=j, it=it: e.dma_start(
                out=dbuf[:, j * 1024:(j + 1) * 1024], in_=it["dsrc"][j * 128:(j + 1) * 128, :]),
                writes=[rdbuf], tag=f"{nm}_ld{j}")
            p.op("vector", lambda e, j=j: e.tensor_mul(dbuf[:, j * 1024:(j + 1) * 1024],
                                                       dbuf[:, j * 1024:(j + 1) * 1024], gbb[:, :]),
                 reads=[rdbuf, rgbb], writes=[rdbuf], tag=f"{nm}_fd{j}")
        for j in range(nj):
            for tg in range(4):
                bg, bu = tg % 2, 2 + tg % 2

                def gmm(e, j=j, tg=tg, bg=bg, I=I):
                    inst = None
                    for c in range(8):
                        inst = e.matmul(pmm[bg][:, :], gbuf[:, c * I + j * 128: c * I + (j + 1) * 128],
                                        hT[:, c * 2048 + tg * 512: c * 2048 + (tg + 1) * 512],
                                        start=(c == 0), stop=(c == 7))
                    return inst
                p.op("tensor", gmm, reads=[rgbuf, rhT], writes=[rpmm[bg]], tag=f"{nm}_gm{j}_{tg}")

                def umm(e, j=j, tg=tg, bu=bu, I=I):
                    inst = None
                    for c in range(8):
                        inst = e.matmul(pmm[bu][:, :], ubuf[:, c * I + j * 128: c * I + (j + 1) * 128],
                                        hT[:, c * 2048 + tg * 512: c * 2048 + (tg + 1) * 512],
                                        start=(c == 0), stop=(c == 7))
                    return inst
                p.op("tensor", umm, reads=[rubuf, rhT], writes=[rpmm[bu]], tag=f"{nm}_um{j}_{tg}")
                p.op("scalar", lambda e, bg=bg: e.activation(sgt[:, :], pmm[bg][:, :], AF.Silu),
                     reads=[rpmm[bg]], writes=[rsgt], tag=f"{nm}_si{j}_{tg}")
                p.op("vector", lambda e, j=j, tg=tg, bu=bu: e.tensor_tensor(
                    act[:, j * 2048 + tg * 512: j * 2048 + (tg + 1) * 512], sgt[:, :], pmm[bu][:, :], ALU.mult),
                    reads=[rsgt, rpmm[bu]], writes=[ract], tag=f"{nm}_ac{j}_{tg}")
        gcol = it["gcol"]
        for t in range(16):
            def dmm(e, t=t, nj=nj):
                inst = None
                for hf in range(2):
                    for j in range(nj):
                        inst = e.matmul(pwide[:, hf * 512:(hf + 1) * 512],
                                        act[:, j * 2048 + t * 128: j * 2048 + (t + 1) * 128],
                                        dbuf[:, j * 1024 + hf * 512: j * 1024 + (hf + 1) * 512],
                                        start=(j == 0), stop=(j == nj - 1))
                return inst
            p.op("tensor", dmm, reads=[ract, rdbuf], writes=[rpwide], tag=f"{nm}_dm{t}")
            for hf in range(2):
                p.op("vector", lambda e, t=t, hf=hf, gcol=gcol: e.scalar_tensor_tensor(
                    xsb[:, t * 1024 + hf * 512: t * 1024 + (hf + 1) * 512],
                    pwide[:, hf * 512:(hf + 1) * 512], glw[:, t * 9 + gcol: t * 9 + gcol + 1],
                    xsb[:, t * 1024 + hf * 512: t * 1024 + (hf + 1) * 512], ALU.mult, ALU.add),
                    reads=[rpwide, rglw, rx], writes=[rx], tag=f"{nm}_cb{t}_{hf}")

    for t in range(16):
        p.dma("sync", lambda e, t=t: e.dma_start(out=x2_out[t * 128:(t + 1) * 128, :], in_=xsb[:, t * 1024:(t + 1) * 1024]),
              reads=[rx], writes=[ctx.r["out:x2"]], tag=f"stx2_{t}")

    p.build(nc)
    ctx.close()
    nc.reset()
    return ctx


def emit_mamba(ctx, x_in, hin_src, mod, win_in, winpad_in, dtw_in, dtb_in, alog_in,
               dpar_in, include_y, normw_in=None, wout_in=None, x3_out=None,
               hfin_out=None, x_res=None, rx3out=None, x_dep=None):
    p, nc = ctx.p, ctx.nc
    paux, rpaux = ctx.t["paux"], ctx.r["paux"]
    pwide, rpwide = ctx.t["pwide"], ctx.r["pwide"]
    pmm = [ctx.t[f"pmm{i}"] for i in range(4)]
    rpmm = [ctx.r[f"pmm{i}"] for i in range(4)]
    ptrb, rptrb = ctx.t["ptrb"], ctx.r["ptrb"]

    dtw = ctx.sb("dtw", [16, 16], F32)
    rdtw = ctx.r["dtw"]
    p.dma("sync", lambda e: e.dma_start(out=dtw[:, :], in_=dtw_in), writes=[rdtw], tag="lddtw")
    dtb = ctx.sb("m_dtb", [16, 1], F32)
    rdtb = ctx.r["m_dtb"]
    p.dma("sync", lambda e: e.dma_start(out=dtb[:, :], in_=dtb_in), writes=[rdtb], tag="lddtb")
    acol = ctx.sb("acol", [16, 1], F32)
    racol = ctx.r["acol"]
    p.dma("sync", lambda e: e.dma_start(out=acol[:, :], in_=alog_in), writes=[racol], tag="ldal")
    p.op("scalar", lambda e: e.activation(acol[:, :], acol[:, :], AF.Exp), reads=[racol], writes=[racol], tag="aexp")
    p.op("vector", lambda e: e.tensor_scalar_mul(acol[:, :], acol[:, :], -1.0), reads=[racol], writes=[racol], tag="aneg")
    if include_y:
        drep = ctx.sb("drep", [128, 16], F32)
        rdrep = ctx.r["drep"]
        dprow = ctx.sb("dprow", [1, 16], F32)
        rdprow = ctx.r["dprow"]
        p.dma("sync", lambda e: e.dma_start(out=dprow[:, :], in_=dpar_in), writes=[rdprow], tag="lddp")
        p.op("tensor", lambda e: e.matmul(paux[:, 0:16], ctx.t["c_ones"][0:1, :], dprow[:, :], start=True, stop=True),
             reads=[rdprow, ctx.r["c_ones"]], writes=[rpaux], tag="dp_bc")
        p.op("vector", lambda e: e.tensor_copy(drep[:, :], paux[:, 0:16]), reads=[rpaux], writes=[rdrep], tag="dp_ev")
        nwb = ctx.sb("nwb", [128, 1024], F32)
        rnwb = ctx.r["nwb"]
        nwrow = ctx.sb("nwrow", [1, 1024], F32)
        rnwrow = ctx.r["nwrow"]
        p.dma("sync", lambda e: e.dma_start(out=nwrow[:, :], in_=normw_in), writes=[rnwrow], tag="ldnw")

        def nwbc(e):
            e.matmul(pwide[:, 0:512], ctx.t["c_ones"][0:1, :], nwrow[:, 0:512], start=True, stop=True)
            return e.matmul(pwide[:, 512:1024], ctx.t["c_ones"][0:1, :], nwrow[:, 512:1024], start=True, stop=True)
        p.op("tensor", nwbc, reads=[rnwrow, ctx.r["c_ones"]], writes=[rpwide], tag="nw_bc")
        p.op("vector", lambda e: e.tensor_copy(nwb[:, :], pwide[:, :]), reads=[rpwide], writes=[rnwb], tag="nw_ev")
        wout = ctx.sb("wout", [128, 8 * 1024], BF16)
        rwout = ctx.r["wout"]
        gab = ctx.sb("gab", [128, 1024], BF16)
        rgab = ctx.r["gab"]
        p.op("vector", lambda e: e.tensor_copy(gab[:, :], mod["ga"][0][:, :]), reads=[mod["ga"][1]], writes=[rgab], tag="gab")
        for c in range(8):
            p.dma("gpsimd", lambda e, c=c: e.dma_start(out=wout[:, c * 1024:(c + 1) * 1024], in_=wout_in[c * 128:(c + 1) * 128, :]),
                  writes=[rwout], tag=f"ldwo{c}")
            p.op("vector", lambda e, c=c: e.tensor_mul(wout[:, c * 1024:(c + 1) * 1024], wout[:, c * 1024:(c + 1) * 1024], gab[:, :]),
                 reads=[rwout, rgab], writes=[rwout], tag=f"fwo{c}")

    selh1 = ctx.sb("selh1", [16, 128], F32)
    rselh1 = ctx.r["selh1"]
    selg = ctx.sb("selg", [16, 4 * 128], F32)
    rselg = ctx.r["selg"]
    p.op("gpsimd", lambda e: e.memset(selg[:, :], 1.0), writes=[rselg], tag="selg0")
    for g in range(4):
        for r in range(4):
            p.op("gpsimd", lambda e, g=g, r=r: e.affine_select(
                out=selg[:, g * 128 + r * 32: g * 128 + (r + 1) * 32], in_=selg[:, g * 128 + r * 32: g * 128 + (r + 1) * 32],
                compare_op=ALU.is_equal, fill=0.0, base=-(g * 4 + r), pattern=[[0, 32]], channel_multiplier=1),
                reads=[rselg], writes=[rselg], tag=f"selg{g}_{r}")

    state = ctx.sb("state", [128, 256], F32)
    rstate = ctx.r["state"]
    hin_src(state, rstate)
    stateb = ctx.sb("stateb", [128, 256], BF16)
    rstateb = ctx.r["stateb"]
    p.op("vector", lambda e: e.tensor_copy(stateb[:, :], state[:, :]), reads=[rstate], writes=[rstateb], tag="stb0")
    sscr = ctx.sb("sscr", [128, 256], F32)
    rsscr = ctx.r["sscr"]

    TPP = 1024
    h2T = ctx.sb("m_h2T", [128, 8 * TPP], BF16)
    rh2T = ctx.r["m_h2T"]
    xp = ctx.sb("m_xp", [128, 8 * 1024], BF16)
    rxp = ctx.r["m_xp"]
    btok = ctx.sb("m_btok", [128, 8 * 256], BF16)
    rbtok = ctx.r["m_btok"]
    wsb = [ctx.sb(f"m_ws{i}", [128, 8 * 512], BF16) for i in range(2)]
    rwsb = [ctx.r[f"m_ws{i}"] for i in range(2)]
    dtl = ctx.sb("m_dtl", [16, TPP], F32)
    rdtl = ctx.r["m_dtl"]
    dtT = ctx.sb("m_dtT", [16, TPP], F32)
    rdtT = ctx.r["m_dtT"]
    aT = ctx.sb("m_aT", [16, TPP], F32)
    raT = ctx.r["m_aT"]
    sp1, rsp1 = dtl, rdtl
    a_tok = ctx.sb("m_atok", [128, 8 * 16], F32)
    ratok = ctx.r["m_atok"]
    dt_tok = ctx.sb("m_dttok", [128, 8 * 16], F32)
    rdttok = ctx.r["m_dttok"]
    expa = ctx.sb("m_expa", [128, 16], F32)
    rexpa = ctx.r["m_expa"]
    cdpad = ctx.sb("m_cdpad", [128, 8 * 4], F32)
    rcdpad = ctx.r["m_cdpad"]
    alrow = ctx.sb("m_alrow", [1, 16], F32)
    ralrow = ctx.r["m_alrow"]
    wxf = ctx.sb("m_wxf", [128, 16], F32)
    rwxf = ctx.r["m_wxf"]
    bdec = ctx.sb("m_bdec", [128, 256], BF16)
    rbdec = ctx.r["m_bdec"]
    if include_y:
        z = ctx.sb("m_z", [128, 8 * 1024], BF16)
        rz = ctx.r["m_z"]
        xdt = ctx.sb("m_xdt", [128, 8 * 1024], BF16)
        rxdt = ctx.r["m_xdt"]
        bTp = ctx.sb("m_bTp", [128, 4 * TPP], BF16)
        rbTp = ctx.r["m_bTp"]
        cTp = ctx.sb("m_cTp", [128, 4 * TPP], BF16)
        rcTp = ctx.r["m_cTp"]
        ysb, rysb = h2T, rh2T
        expD = ctx.sb("m_expD", [128, 128], BF16)
        rexpD = ctx.r["m_expD"]
        dscr = ctx.sb("m_dscr", [128, 128], F32)
        rdscr = ctx.r["m_dscr"]
        pt2 = ctx.sb("m_pt2", [128, 128], BF16)
        rpt2 = ctx.r["m_pt2"]
        yw1 = ctx.sb("m_yw1", [128, 64], F32)
        ryw1 = ctx.r["m_yw1"]


        y3T, ry3T = xdt, rxdt

    eps6, reps6 = ctx.const_col(1e-6)
    nh_c, rnh = ctx.const_col(-0.5)

    for pa in range(2):
        tok0 = pa * TPP
        tile0 = pa * 8
        def tsrc(tl, pa=pa):
            t = pa * 8 + tl
            b = t % 2
            bufs = wsb
            return None
        if pa == 0 and x_res is None:
            lnx = [ctx.sb(f"m_lnx{i}", [128, 1024], F32) for i in range(2)]
            rlnx = [ctx.r[f"m_lnx{i}"] for i in range(2)]
            xstr, rxstr = lnx, rlnx

        if x_res is not None:
            xr_t, xr_r = x_res

            def tile_src(tl, pa=pa):
                t = pa * 8 + tl
                return xr_t[:, t * 1024:(t + 1) * 1024], xr_r
        else:
            def tile_src(tl, pa=pa):
                t = pa * 8 + tl
                b = tl % 2
                p.dma("sync", lambda e, t=t, b=b: e.dma_start(out=lnx[b][:, :], in_=x_in[t * 128:(t + 1) * 128, :]),
                      reads=([x_dep] if x_dep is not None else []),
                      writes=[rlnx[b]], tag=f"m{pa}_ldx{t}")
                return lnx[b][:, :], rlnx[b]
        emit_ln_mod(ctx, f"mln{pa}", tile_src, mod["ca1"], mod["sa"], h2T, rh2T, n_tiles=8)

        secs = [("xp", xp, rxp, 0, 1024, BF16)]
        if include_y:
            secs.append(("z", z, rz, 1024, 1024, BF16))
        secs.append(("bt", btok, rbtok, 2048, 256, BF16))
        for nm, dst, rdst, col0, width, dt_ in secs:
            nw5 = (width + 511) // 512
            for w5 in range(nw5):
                wcols = min(512, width - w5 * 512)
                b = w5 % 2
                for c in range(8):
                    p.dma("gpsimd", lambda e, c=c, b=b, col0=col0, w5=w5, wcols=wcols: e.dma_start(
                        out=wsb[b][:, c * 512: c * 512 + wcols],
                        in_=win_in[c * 128:(c + 1) * 128, col0 + w5 * 512: col0 + w5 * 512 + wcols]),
                        writes=[rwsb[b]], tag=f"m{pa}_{nm}w{w5}_{c}")
                for tl in range(8):
                    bank = tl % 2

                    def tmm(e, tl=tl, b=b, bank=bank, wcols=wcols):
                        inst = None
                        for c in range(8):
                            inst = e.matmul(pmm[bank][:, 0:wcols], h2T[:, c * TPP + tl * 128: c * TPP + (tl + 1) * 128],
                                            wsb[b][:, c * 512: c * 512 + wcols], start=(c == 0), stop=(c == 7))
                        return inst
                    p.op("tensor", tmm, reads=[rh2T, rwsb[b]], writes=[rpmm[bank]], tag=f"m{pa}_{nm}m{w5}_{tl}")
                    p.op("scalar", lambda e, tl=tl, bank=bank, w5=w5, wcols=wcols, dst=dst, width=width: e.activation(
                        dst[:, tl * width + w5 * 512: tl * width + w5 * 512 + wcols], pmm[bank][:, 0:wcols], AF.Identity),
                        reads=[rpmm[bank]], writes=[rdst], tag=f"m{pa}_{nm}e{w5}_{tl}")

        if include_y:
            for nm, dst, rdst, pcol0 in (("bp", bTp, rbTp, 0), ("cp", cTp, rcTp, 512)):
                for oc in range(4):
                    b = oc % 2
                    for c in range(8):
                        p.dma("gpsimd", lambda e, c=c, b=b, pcol0=pcol0, oc=oc: e.dma_start(
                            out=wsb[b][:, c * 512: c * 512 + 128],
                            in_=winpad_in[c * 128:(c + 1) * 128, pcol0 + oc * 128: pcol0 + (oc + 1) * 128]),
                            writes=[rwsb[b]], tag=f"m{pa}_{nm}w{oc}_{c}")
                    for tg in range(2):
                        bank = 2 + tg

                        def fmm(e, b=b, tg=tg, bank=bank):
                            inst = None
                            for c in range(8):
                                inst = e.matmul(pmm[bank][:, :], wsb[b][:, c * 512: c * 512 + 128],
                                                h2T[:, c * TPP + tg * 512: c * TPP + (tg + 1) * 512],
                                                start=(c == 0), stop=(c == 7))
                            return inst
                        p.op("tensor", fmm, reads=[rh2T, rwsb[b]], writes=[rpmm[bank]], tag=f"m{pa}_{nm}m{oc}_{tg}")
                        p.op("scalar", lambda e, oc=oc, tg=tg, bank=bank, dst=dst: e.activation(
                            dst[:, oc * TPP + tg * 512: oc * TPP + (tg + 1) * 512], pmm[bank][:, :], AF.Identity),
                            reads=[rpmm[bank]], writes=[rdst], tag=f"m{pa}_{nm}e{oc}_{tg}")

        for tg in range(2):
            b = tg % 2
            p.dma("gpsimd", lambda e, b=b: e.dma_start(out=wsb[b][:, 0:16 * 8].rearrange("p (c n) -> p c n", c=8),
                                                       in_=win_in.rearrange("(c p) n -> p c n", p=128)[:, :, 2560:2576]),
                  writes=[rwsb[b]], tag=f"m{pa}_dtw{tg}")

            def dtmm(e, tg=tg, b=b):
                inst = None
                for c in range(8):
                    inst = e.matmul(paux[0:16, 0:512], wsb[b][:, c * 16:(c + 1) * 16],
                                    h2T[:, c * TPP + tg * 512: c * TPP + (tg + 1) * 512],
                                    start=(c == 0), stop=(c == 7))
                return inst
            p.op("tensor", dtmm, reads=[rh2T, rwsb[b]], writes=[rpaux], tag=f"m{pa}_dtm{tg}")
            p.op("vector", lambda e, tg=tg: e.tensor_copy(dtl[:, tg * 512:(tg + 1) * 512], paux[0:16, 0:512]),
                 reads=[rpaux], writes=[rdtl], tag=f"m{pa}_dte{tg}")
        for tg in range(2):
            p.op("tensor", lambda e, tg=tg: e.matmul(paux[0:16, 0:512], dtw[:, :], dtl[:, tg * 512:(tg + 1) * 512],
                                                     start=True, stop=True),
                 reads=[rdtw, rdtl], writes=[rpaux], tag=f"m{pa}_dtwm{tg}")
            p.op("scalar", lambda e, tg=tg: e.activation(dtT[:, tg * 512:(tg + 1) * 512], paux[0:16, 0:512], AF.Identity, bias=dtb[:, :]),
                 reads=[rpaux, rdtb], writes=[rdtT], tag=f"m{pa}_dtb{tg}")
        p.op("scalar", lambda e: e.activation(sp1[:, :], dtT[:, :], AF.Abs), reads=[rdtT], writes=[rsp1], tag=f"m{pa}_spa")
        nm1, rnm1 = ctx.const_col(-1.0)
        one1, rone1 = ctx.const_col(1.0)
        p.op("scalar", lambda e: e.activation(sp1[:, :], sp1[:, :], AF.Exp, scale=nm1[0:16, :]), reads=[rsp1, rnm1], writes=[rsp1], tag=f"m{pa}_spe")
        p.op("scalar", lambda e: e.activation(sp1[:, :], sp1[:, :], AF.Ln, bias=one1[0:16, :]), reads=[rsp1, rone1], writes=[rsp1], tag=f"m{pa}_spl")
        p.op("vector", lambda e: e.tensor_scalar_max(dtT[:, :], dtT[:, :], 0.0), reads=[rdtT], writes=[rdtT], tag=f"m{pa}_spm")
        p.op("vector", lambda e: e.tensor_add(dtT[:, :], dtT[:, :], sp1[:, :]), reads=[rdtT, rsp1], writes=[rdtT], tag=f"m{pa}_spadd")
        p.op("vector", lambda e: e.tensor_scalar(sp1[:, :], dtT[:, :], acol[:, 0:1], None, ALU.mult),
             reads=[rdtT, racol], writes=[rsp1], tag=f"m{pa}_dA")
        for cc in range(8):
            p.op("vector", lambda e, cc=cc: e.tensor_tensor_scan(aT[:, cc * 128:(cc + 1) * 128], sp1[:, cc * 128:(cc + 1) * 128],
                                                                 sp1[:, cc * 128:(cc + 1) * 128], 0.0, ALU.add, ALU.bypass),
                 reads=[rsp1], writes=[raT], tag=f"m{pa}_cs{cc}")
        def tras(e):
            inst = None
            for cc in range(8):
                inst = e.transpose(paux[:, cc * 16:(cc + 1) * 16], aT[:, cc * 128:(cc + 1) * 128], ctx.t["c_ident"][0:16, 0:16])
            return inst
        p.op("tensor", tras, reads=[raT, ctx.r["c_ident"]], writes=[rpaux], tag=f"m{pa}_atr")
        p.op("vector", lambda e: e.tensor_copy(a_tok[:, :], paux[:, 0:128]), reads=[rpaux], writes=[ratok], tag=f"m{pa}_atre")

        def trds(e):
            inst = None
            for cc in range(8):
                inst = e.transpose(paux[:, cc * 16:(cc + 1) * 16], dtT[:, cc * 128:(cc + 1) * 128], ctx.t["c_ident"][0:16, 0:16])
            return inst
        p.op("tensor", trds, reads=[rdtT, ctx.r["c_ident"]], writes=[rpaux], tag=f"m{pa}_dtr")
        p.op("vector", lambda e: e.tensor_copy(dt_tok[:, :], paux[:, 0:128]), reads=[rpaux], writes=[rdttok], tag=f"m{pa}_dtre")

        if pa == 0:
            cdT_t = ctx.sb("m_cdT", [16, 8], F32)
        cdT = ctx.t["m_cdT"]
        rcdT = ctx.r["m_cdT"]
        for cc in range(8):
            p.op("scalar", lambda e, cc=cc: e.activation(cdT[:, cc:cc + 1], aT[:, cc * 128 + 127: cc * 128 + 128], AF.Exp),
                 reads=[raT], writes=[rcdT], tag=f"m{pa}_cd{cc}")

            def cdmm(e, cc=cc):
                inst = None
                for g in range(4):
                    inst = e.matmul(paux[:, cc * 4 + g: cc * 4 + g + 1], selg[:, g * 128:(g + 1) * 128],
                                    cdT[:, cc:cc + 1], start=True, stop=True)
                return inst
            p.op("tensor", cdmm, reads=[rcdT, rselg], writes=[rpaux], tag=f"m{pa}_cdm{cc}")
        p.op("vector", lambda e: e.tensor_copy(cdpad[:, :], paux[:, 0:32]), reads=[rpaux], writes=[rcdpad], tag=f"m{pa}_cde")

        if include_y:
            for cc in range(8):
                p.op("vector", lambda e, cc=cc: e.tensor_tensor(
                    xdt[:, cc * 1024:(cc + 1) * 1024].rearrange("p (h d) -> p h d", h=16),
                    xp[:, cc * 1024:(cc + 1) * 1024].rearrange("p (h d) -> p h d", h=16),
                    bass.AP(dt_tok, cc * 16, [[128, 128], [1, 16], [0, 64]]), ALU.mult),
                    reads=[rxp, rdttok], writes=[rxdt], tag=f"m{pa}_xdt{cc}")

        for cc in range(8):
            def altr(e, cc=cc):
                return e.transpose(paux[0:1, 0:16], cdT[:, cc:cc + 1], ctx.t["c_ident"][0:16, 0:16])
            def altr2(e, cc=cc):
                return e.transpose(paux[0:1, 0:16], aT[:, cc * 128 + 127: cc * 128 + 128], ctx.t["c_ident"][0:16, 0:16])
            p.op("tensor", altr2, reads=[raT, ctx.r["c_ident"]], writes=[rpaux], tag=f"m{pa}c{cc}_alt")
            p.op("vector", lambda e: e.tensor_copy(alrow[:, :], paux[0:1, 0:16]), reads=[rpaux], writes=[ralrow], tag=f"m{pa}c{cc}_ale")
            p.op("tensor", lambda e: e.matmul(paux[:, 32:48], ctx.t["c_ones"][0:1, :], alrow[:, :], start=True, stop=True),
                 reads=[ralrow, ctx.r["c_ones"]], writes=[rpaux], tag=f"m{pa}c{cc}_alb")
            p.op("vector", lambda e, cc=cc: e.tensor_tensor(wxf[:, :], paux[:, 32:48], a_tok[:, cc * 16:(cc + 1) * 16], ALU.subtract),
                 reads=[rpaux, ratok], writes=[rwxf], tag=f"m{pa}c{cc}_ws")
            p.op("scalar", lambda e: e.activation(wxf[:, :], wxf[:, :], AF.Exp), reads=[rwxf], writes=[rwxf], tag=f"m{pa}c{cc}_we")
            p.op("vector", lambda e, cc=cc: e.tensor_mul(wxf[:, :], wxf[:, :], dt_tok[:, cc * 16:(cc + 1) * 16]),
                 reads=[rwxf, rdttok], writes=[rwxf], tag=f"m{pa}c{cc}_wd")
            p.op("vector", lambda e, cc=cc: e.tensor_tensor(
                bdec[:, :].rearrange("p (h n) -> p h n", h=16),
                btok[:, cc * 256:(cc + 1) * 256].rearrange("p (h n) -> p h n", h=16),
                bass.AP(wxf, 0, [[16, 128], [1, 16], [0, 16]]), ALU.mult),
                reads=[rbtok, rwxf], writes=[rbdec], tag=f"m{pa}c{cc}_bd")
            if include_y:
                p.op("scalar", lambda e, cc=cc: e.activation(expa[:, :], a_tok[:, cc * 16:(cc + 1) * 16], AF.Exp),
                     reads=[ratok], writes=[rexpa], tag=f"m{pa}c{cc}_ea")

            for h in range(16):
                hb = (h % 4) * 32
                hg = h // 4
                if include_y:
                    sb_ = h % 2
                    p.op("gpsimd", lambda e: e.memset(selh1[:, :], 1.0), writes=[rselh1], tag=f"m{pa}c{cc}h{h}_sh0")
                    p.op("gpsimd", lambda e, h=h: e.affine_select(out=selh1[:, :], in_=selh1[:, :], compare_op=ALU.is_equal,
                                                                  fill=0.0, base=-h, pattern=[[0, 128]], channel_multiplier=1),
                         reads=[rselh1], writes=[rselh1], tag=f"m{pa}c{cc}h{h}_sh1")
                    p.op("tensor", lambda e, cc=cc: e.matmul(paux[:, 128:256], selh1[:, :],
                                                             aT[:, cc * 128:(cc + 1) * 128], start=True, stop=True),
                         reads=[raT, rselh1], writes=[rpaux], tag=f"m{pa}c{cc}h{h}_rb")
                    p.op("vector", lambda e, cc=cc, h=h: e.tensor_single_scalar(dscr[:, :], paux[:, 128:256],
                                                                                a_tok[:, cc * 16 + h: cc * 16 + h + 1], ALU.subtract),
                         reads=[rpaux, ratok], writes=[rdscr], tag=f"m{pa}c{cc}h{h}_d")
                    p.op("scalar", lambda e: e.activation(expD[:, :], dscr[:, :], AF.Exp), reads=[rdscr], writes=[rexpD], tag=f"m{pa}c{cc}h{h}_ex")
                    p.op("gpsimd", lambda e: e.affine_select(out=expD[:, :], in_=expD[:, :], compare_op=ALU.is_ge,
                                                             fill=0.0, base=0, pattern=[[1, 128]], channel_multiplier=-1),
                         reads=[rexpD], writes=[rexpD], tag=f"m{pa}c{cc}h{h}_msk")
                    tp = (hb, 0) if hb == 96 else None

                    def smm(e, cc=cc, h=h, hb=hb, sb_=sb_, tp=tp):
                        return e.matmul(pmm[sb_][:, 0:128], bTp[hb:hb + 32, (h // 4) * TPP + cc * 128: (h // 4) * TPP + (cc + 1) * 128],
                                        cTp[hb:hb + 32, (h // 4) * TPP + cc * 128: (h // 4) * TPP + (cc + 1) * 128],
                                        start=True, stop=True, tile_position=tp)
                    p.op("tensor", smm, reads=[rbTp, rcTp], writes=[rpmm[sb_]], tag=f"m{pa}c{cc}h{h}_s")
                    p.op("vector", lambda e, sb_=sb_: e.tensor_tensor(pt2[:, :], pmm[sb_][:, 0:128], expD[:, :], ALU.mult),
                         reads=[rpmm[sb_], rexpD], writes=[rpt2], tag=f"m{pa}c{cc}h{h}_p")
                    p.op("tensor", lambda e, cc=cc, h=h: e.matmul(pmm[2][:, (h % 8) * 64:(h % 8) * 64 + 64], pt2[:, :],
                                                                  xdt[:, cc * 1024 + h * 64: cc * 1024 + (h + 1) * 64],
                                                                  start=True, stop=True),
                         reads=[rpt2, rxdt], writes=[rpmm[2]], tag=f"m{pa}c{cc}h{h}_yi")
                    tpc = (hb, 0) if hb == 96 else None

                    def ycm(e, cc=cc, h=h, hb=hb, hg=hg, tpc=tpc):
                        return e.matmul(pmm[3][:, (h % 8) * 64:(h % 8) * 64 + 64],
                                        cTp[hb:hb + 32, hg * TPP + cc * 128: hg * TPP + (cc + 1) * 128],
                                        stateb[hb:hb + 32, hg * 64:(hg + 1) * 64], start=True, stop=True, tile_position=tpc)
                    p.op("tensor", ycm, reads=[rcTp, rstateb], writes=[rpmm[3]], tag=f"m{pa}c{cc}h{h}_yc")
                    p.op("vector", lambda e, h=h: e.tensor_scalar(yw1[:, :], pmm[3][:, (h % 8) * 64:(h % 8) * 64 + 64],
                                                                  expa[:, h:h + 1], None, ALU.mult),
                         reads=[rpmm[3], rexpa], writes=[ryw1], tag=f"m{pa}c{cc}h{h}_e1")
                    p.op("vector", lambda e, h=h: e.tensor_add(yw1[:, :], yw1[:, :], pmm[2][:, (h % 8) * 64:(h % 8) * 64 + 64]),
                         reads=[ryw1, rpmm[2]], writes=[ryw1], tag=f"m{pa}c{cc}h{h}_e2")
                    p.op("vector", lambda e, cc=cc, h=h: e.scalar_tensor_tensor(
                        ysb[:, cc * 1024 + h * 64: cc * 1024 + (h + 1) * 64],
                        xp[:, cc * 1024 + h * 64: cc * 1024 + (h + 1) * 64], drep[:, h:h + 1], yw1[:, :],
                        ALU.mult, ALU.add),
                        reads=[rxp, rdrep, ryw1], writes=[rysb], tag=f"m{pa}c{cc}h{h}_e3")
                tps = (0, hb) if hb == 96 else None

                def csm(e, cc=cc, h=h, hb=hb, hg=hg, tps=tps):
                    return e.matmul(pwide[hb:hb + 16, hg * 64:(hg + 1) * 64], bdec[:, h * 16:(h + 1) * 16],
                                    xp[:, cc * 1024 + h * 64: cc * 1024 + (h + 1) * 64], start=True, stop=True,
                                    tile_position=tps)
                p.op("tensor", csm, reads=[rbdec, rxp], writes=[rpwide], tag=f"m{pa}c{cc}h{h}_cs")
            p.op("vector", lambda e, cc=cc: e.tensor_tensor(
                sscr[:, :].rearrange("p (g d) -> p g d", g=4),
                state[:, :].rearrange("p (g d) -> p g d", g=4),
                bass.AP(cdpad, cc * 4, [[32, 128], [1, 4], [0, 64]]), ALU.mult),
                reads=[rstate, rcdpad], writes=[rsscr], tag=f"m{pa}c{cc}_su1")
            p.op("vector", lambda e: e.tensor_add(state[:, :], sscr[:, :], pwide[:, 0:256]),
                 reads=[rsscr, rpwide], writes=[rstate], tag=f"m{pa}c{cc}_su2")
            if include_y:
                p.op("vector", lambda e: e.tensor_copy(stateb[:, :], state[:, :]),
                     reads=[rstate], writes=[rstateb], tag=f"m{pa}c{cc}_su3")

        if include_y:
            if pa == 0:
                rms_m = ctx.sb("m_rm", [128, 1], F32)
            rrms = ctx.r["m_rm"]
            sqt, rsqt = ctx.sb_shared("_lns_sq", [128, 1024], F32)
            for tl in range(8):
                t = pa * 8 + tl
                p.op("scalar", lambda e, tl=tl: e.activation(sqt[:, :], ysb[:, tl * 1024:(tl + 1) * 1024], AF.Square, accum_out=rms_m[:, :]),
                     reads=[rysb], writes=[rsqt, rrms], tag=f"m{pa}_rq{tl}")
                p.op("scalar", lambda e: e.activation(rms_m[:, :], rms_m[:, :], AF.Ln, bias=eps6[:, :], scale=1.0 / 1024),
                     reads=[rrms, reps6], writes=[rrms], tag=f"m{pa}_rl{tl}")
                p.op("scalar", lambda e: e.activation(rms_m[:, :], rms_m[:, :], AF.Exp, scale=nh_c[:, :]),
                     reads=[rrms, rnh], writes=[rrms], tag=f"m{pa}_re{tl}")
                p.op("scalar", lambda e, tl=tl: e.activation(sqt[:, :], ysb[:, tl * 1024:(tl + 1) * 1024], AF.Identity, scale=rms_m[:, :]),
                     reads=[rysb, rrms], writes=[rsqt], tag=f"m{pa}_rs{tl}")
                p.op("vector", lambda e: e.tensor_mul(sqt[:, :], sqt[:, :], nwb[:, :]), reads=[rsqt, rnwb], writes=[rsqt], tag=f"m{pa}_rn{tl}")
                p.op("scalar", lambda e, tl=tl: e.activation(z[:, tl * 1024:(tl + 1) * 1024], z[:, tl * 1024:(tl + 1) * 1024], AF.Silu),
                     reads=[rz], writes=[rz], tag=f"m{pa}_zs{tl}")
                p.op("vector", lambda e, tl=tl: e.tensor_tensor(z[:, tl * 1024:(tl + 1) * 1024], sqt[:, :],
                                                                z[:, tl * 1024:(tl + 1) * 1024], ALU.mult),
                     reads=[rsqt, rz], writes=[rz], tag=f"m{pa}_y3{tl}")

                def y3tr(e, tl=tl):
                    inst = None
                    for j in range(8):
                        inst = e.transpose(ptrb[:, j * 128:(j + 1) * 128],
                                           z[:, tl * 1024 + j * 128: tl * 1024 + (j + 1) * 128], ctx.t["c_ident_b"][:, :])
                    return inst
                p.op("tensor", y3tr, reads=[rz, ctx.r["c_ident_b"]], writes=[rptrb], tag=f"m{pa}_ytr{tl}")
                p.op("vector", lambda e, tl=tl: e.tensor_copy(
                    bass.AP(y3T, tl * 128, [[8 * TPP, 128], [TPP, 8], [1, 128]]),
                    ptrb[:, :].rearrange("p (j q) -> p j q", j=8)),
                    reads=[rptrb], writes=[ry3T], tag=f"m{pa}_yte{tl}")
            for tl in range(8):
                t = pa * 8 + tl
                b = tl % 2
                if x_res is None:
                    p.dma("sync", lambda e, t=t, b=b: e.dma_start(out=xstr[b][:, :], in_=x_in[t * 128:(t + 1) * 128, :]),
                          reads=([x_dep] if x_dep is not None else []),
                          writes=[rxstr[b]], tag=f"m{pa}_lx{t}")

                def omm(e, tl=tl):
                    inst = None
                    for hf in range(2):
                        for c in range(8):
                            inst = e.matmul(pwide[:, 512 + hf * 256: 512 + (hf + 1) * 256]
                                            if False else pwide[:, hf * 512:(hf + 1) * 512],
                                            y3T[:, c * TPP + tl * 128: c * TPP + (tl + 1) * 128],
                                            wout[:, c * 1024 + hf * 512: c * 1024 + (hf + 1) * 512],
                                            start=(c == 0), stop=(c == 7))
                    return inst
                p.op("tensor", omm, reads=[ry3T, rwout], writes=[rpwide], tag=f"m{pa}_om{tl}")
                if x_res is not None:
                    xr_t, xr_r = x_res
                    for hf2 in range(2):
                        p.op("vector", lambda e, t=t, hf2=hf2: e.tensor_add(
                            xr_t[:, t * 1024 + hf2 * 512: t * 1024 + (hf2 + 1) * 512],
                            xr_t[:, t * 1024 + hf2 * 512: t * 1024 + (hf2 + 1) * 512],
                            pwide[:, hf2 * 512:(hf2 + 1) * 512]),
                            reads=[rpwide, xr_r], writes=[xr_r], tag=f"m{pa}_oa{t}_{hf2}")
                else:
                    p.op("vector", lambda e, b=b: e.tensor_add(xstr[b][:, :], xstr[b][:, :], pwide[:, :]),
                         reads=[rpwide, rxstr[b]], writes=[rxstr[b]], tag=f"m{pa}_oa{tl}")
                    p.dma("sync", lambda e, t=t, b=b: e.dma_start(out=x3_out[t * 128:(t + 1) * 128, :], in_=xstr[b][:, :]),
                          reads=[rxstr[b]],
                          writes=[rx3out if rx3out is not None else ctx.r["out:x3"]], tag=f"m{pa}_sx{t}")

    if hfin_out is not None:
        rdst_h = rx3out if rx3out is not None else ctx.r["out:hfin"]
        p.dma("sync", lambda e: e.dma_start(out=hfin_out, in_=state[:, :]), reads=[rstate],
              writes=[rdst_h], tag="sthf")


def build_l1_mamba(include_y):
    ctx = Ctx("mamba")
    nc, p = ctx.nc, ctx.p
    x_in = ctx.inp("x2", [2048, 1024])
    win_in = ctx.inp("winT", [1024, 2576])
    winpad_in = ctx.inp("winpadT", [1024, 1024])
    dtw_in = ctx.inp("dtwT", [16, 16])
    dtb_in = ctx.inp("dtb", [16, 1])
    alog_in = ctx.inp("alog", [16, 1])
    dpar_in = ctx.inp("dpar", [1, 16])
    c_in = ctx.inp("c_row", [1, 1024])
    wmod_in = ctx.inp("wmod", [1024, 6144])
    if include_y:
        normw_in = ctx.inp("normw", [1, 1024])
        wout_in = ctx.inp("woutT", [1024, 1024])
        hin_in = ctx.inp("hin", [128, 256])
        x3_out = ctx.out("x3", [2048, 1024])
        hfin_out = None
    else:
        normw_in = wout_in = None
        x3_out = None
        hfin_out = ctx.out("hfin", [128, 256])

    ctx.psum_pool()
    ctx.consts()
    wstA = ctx.sb("wstA", [128, 512], F32)
    wstB = ctx.sb("wstB", [128, 512], F32)
    wst = [(wstA, ctx.r["wstA"]), (wstB, ctx.r["wstB"])]
    secs = ("sa", "ca1", "ga") if include_y else ("sa", "ca1")
    mod = emit_mod(ctx, c_in, wmod_in, "mod1", sections=secs, wst=wst)

    if include_y:
        def hin_src(state, rstate):
            p.dma("sync", lambda e: e.dma_start(out=state[:, :], in_=hin_in), writes=[rstate], tag="ldhin")
    else:
        def hin_src(state, rstate):
            p.op("gpsimd", lambda e: e.memset(state[:, :], 0.0), writes=[rstate], tag="st0")

    emit_mamba(ctx, x_in, hin_src, mod, win_in, winpad_in, dtw_in, dtb_in, alog_in,
               dpar_in, include_y, normw_in, wout_in, x3_out, hfin_out)

    p.build(nc)
    ctx.close()
    nc.reset()
    return ctx


def emit_h3_tiles(ctx, tag, x_in, mod, cb, x_dep=None):
    p = ctx.p
    lnx = [ctx.sb(f"{tag}_x{i}", [128, 1024], F32) for i in range(2)]
    rlnx = [ctx.r[f"{tag}_x{i}"] for i in range(2)]
    ca1_t, rca1 = mod["cb1"]
    sa_t, rsa = mod["sb"]
    mean, rmean = ctx.sb_shared("_lns_mean", [128, 1], F32)
    sq, rsq = ctx.sb_shared("_lns_sq", [128, 1024], F32)
    ssq, rssq = ctx.sb_shared("_lns_ssq", [128, 1], F32)
    var, rvar = ctx.sb_shared("_lns_var", [128, 1], F32)
    rstd, rrstd = ctx.sb_shared("_lns_rstd", [128, 1], F32)
    nmr, rnmr = ctx.sb_shared("_lns_nmr", [128, 1], F32)
    xn, rxn = ctx.sb_shared("_lns_xn", [128, 1024], F32)
    eps_c, reps = ctx.const_col(1e-5)
    nh_c, rnh = ctx.const_col(-0.5)
    for t in range(16):
        b = t % 2
        p.dma("sync", lambda e, t=t, b=b: e.dma_start(out=lnx[b][:, :], in_=x_in[t * 128:(t + 1) * 128, :]),
              reads=([x_dep] if x_dep is not None else []),
              writes=[rlnx[b]], tag=f"{tag}_ld{t}")
        xt = lnx[b][:, :]
        rx = rlnx[b]
        p.op("vector", lambda e, xt=xt: e.tensor_reduce(mean[:, :], xt, AX.X, ALU.add), reads=[rx], writes=[rmean], tag=f"{tag}_m{t}")
        p.op("vector", lambda e: e.tensor_scalar_mul(mean[:, :], mean[:, :], 1.0 / 1024), reads=[rmean], writes=[rmean], tag=f"{tag}_ms{t}")
        p.op("scalar", lambda e, xt=xt: e.activation(sq[:, :], xt, AF.Square, accum_out=ssq[:, :]), reads=[rx], writes=[rsq, rssq], tag=f"{tag}_sq{t}")
        p.op("vector", lambda e: e.scalar_tensor_tensor(var[:, :], mean[:, :], -1.0, mean[:, :], ALU.mult, ALU.mult), reads=[rmean], writes=[rvar], tag=f"{tag}_v1{t}")
        p.op("vector", lambda e: e.scalar_tensor_tensor(var[:, :], ssq[:, :], 1.0 / 1024, var[:, :], ALU.mult, ALU.add), reads=[rssq, rvar], writes=[rvar], tag=f"{tag}_v2{t}")
        p.op("scalar", lambda e: e.activation(rstd[:, :], var[:, :], AF.Ln, bias=eps_c[:, :]), reads=[rvar, reps], writes=[rrstd], tag=f"{tag}_l{t}")
        p.op("scalar", lambda e: e.activation(rstd[:, :], rstd[:, :], AF.Exp, scale=nh_c[:, :]), reads=[rrstd, rnh], writes=[rrstd], tag=f"{tag}_e{t}")
        p.op("vector", lambda e: e.scalar_tensor_tensor(nmr[:, :], mean[:, :], -1.0, rstd[:, :], ALU.mult, ALU.mult), reads=[rmean, rrstd], writes=[rnmr], tag=f"{tag}_n{t}")
        p.op("scalar", lambda e, xt=xt: e.activation(xn[:, :], xt, AF.Identity, bias=nmr[:, :], scale=rstd[:, :]), reads=[rx, rnmr, rrstd], writes=[rxn], tag=f"{tag}_xn{t}")
        p.op("vector", lambda e: e.tensor_mul(xn[:, :], xn[:, :], ca1_t[:, :]), reads=[rxn, rca1], writes=[rxn], tag=f"{tag}_sc{t}")
        p.op("vector", lambda e: e.tensor_add(xn[:, :], xn[:, :], sa_t[:, :]), reads=[rxn, rsa], writes=[rxn], tag=f"{tag}_ad{t}")
        cb(t, xn[:, :], rxn)


def build_l1_max1():
    ctx = Ctx("ffnm1")
    nc, p = ctx.nc, ctx.p
    x_in = ctx.inp("x3", [2048, 1024])
    c_in = ctx.inp("c_row", [1, 1024])
    wmod_in = ctx.inp("wmod", [1024, 6144])
    mx_out = ctx.out("mx1", [1, 1])
    ctx.psum_pool()
    ctx.consts()
    mod = emit_mod(ctx, c_in, wmod_in, "mod1", sections=("sb", "cb1"))
    amax = ctx.sb("amax", [128, 1], F32)
    ramax = ctx.r["amax"]
    tcol = ctx.sb("tcol", [128, 1], F32)
    rtcol = ctx.r["tcol"]
    p.op("gpsimd", lambda e: e.memset(amax[:, :], 0.0), writes=[ramax], tag="am0")

    def cb(t, h3, rh3):
        p.op("vector", lambda e, h3=h3: e.tensor_reduce(tcol[:, :], h3, AX.X, ALU.max, apply_absolute_value=True),
             reads=[rh3], writes=[rtcol], tag=f"am_r{t}")
        p.op("vector", lambda e: e.tensor_max(amax[:, :], amax[:, :], tcol[:, :]), reads=[ramax, rtcol], writes=[ramax], tag=f"am_m{t}")
    emit_h3_tiles(ctx, "m1", x_in, mod, cb)
    paux, rpaux = ctx.t["paux"], ctx.r["paux"]
    p.op("tensor", lambda e: e.transpose(paux[0:1, 0:128], amax[:, :], ctx.t["c_ident"][:, :]),
         reads=[ramax, ctx.r["c_ident"]], writes=[rpaux], tag="am_tr")
    mx = ctx.sb("mx", [1, 1], F32)
    rmx = ctx.r["mx"]
    p.op("vector", lambda e: e.tensor_reduce(mx[:, :], paux[0:1, 0:128], AX.X, ALU.max), reads=[rpaux], writes=[rmx], tag="am_f")
    p.dma("sync", lambda e: e.dma_start(out=mx_out, in_=mx[:, :]), reads=[rmx], writes=[ctx.r["out:mx1"]], tag="stmx")
    p.build(nc)
    ctx.close()
    nc.reset()
    return ctx


def emit_gmax_scale(ctx, tag, maxes_ap, clip_ap, out_names):
    p = ctx.p
    m8 = ctx.sb(tag + "_m8", [8, 1], F32)
    rm8 = ctx.r[tag + "_m8"]
    p.dma("sync", lambda e: e.dma_start(out=m8[:, :], in_=maxes_ap), writes=[rm8], tag=tag + "_ld")
    clip = ctx.sb(tag + "_cl", [1, 1], F32)
    rclip = ctx.r[tag + "_cl"]
    p.dma("sync", lambda e: e.dma_start(out=clip[:, :], in_=clip_ap), writes=[rclip], tag=tag + "_lc")
    paux, rpaux = ctx.t["paux"], ctx.r["paux"]
    p.op("tensor", lambda e: e.transpose(paux[0:1, 0:8], m8[:, :], ctx.t["c_ident"][0:8, 0:8]),
         reads=[rm8, ctx.r["c_ident"]], writes=[rpaux], tag=tag + "_tr")
    s = ctx.sb(tag + "_s", [1, 1], F32)
    rs = ctx.r[tag + "_s"]
    p.op("vector", lambda e: e.tensor_reduce(s[:, :], paux[0:1, 0:8], AX.X, ALU.max), reads=[rpaux], writes=[rs], tag=tag + "_mx")
    p.op("scalar", lambda e: e.activation(clip[:, :], clip[:, :], AF.Sigmoid), reads=[rclip], writes=[rclip], tag=tag + "_sg")
    p.op("vector", lambda e: e.tensor_mul(s[:, :], s[:, :], clip[:, :]), reads=[rs, rclip], writes=[rs], tag=tag + "_m1")
    p.op("vector", lambda e: e.tensor_scalar_mul(s[:, :], s[:, :], 1.0 / 127.0), reads=[rs], writes=[rs], tag=tag + "_m2")
    rqs = ctx.sb(tag + "_rq", [1, 1], F32)
    rrqs = ctx.r[tag + "_rq"]
    p.op("vector", lambda e: e.reciprocal(rqs[:, :], s[:, :]), reads=[rs], writes=[rrqs], tag=tag + "_rc")

    def bc(e):
        e.matmul(paux[:, 0:1], ctx.t["c_ones"][0:1, :], rqs[:, :], start=True, stop=True)
        return e.matmul(paux[:, 1:2], ctx.t["c_ones"][0:1, :], s[:, :], start=True, stop=True)
    p.op("tensor", bc, reads=[rrqs, rs, ctx.r["c_ones"]], writes=[rpaux], tag=tag + "_bc")
    cols = ctx.sb(tag + "_cols", [128, 2], F32)
    rcols = ctx.r[tag + "_cols"]
    p.op("vector", lambda e: e.tensor_copy(cols[:, :], paux[:, 0:2]), reads=[rpaux], writes=[rcols], tag=tag + "_bce")
    return cols, rcols


def build_l1_gateup():
    ctx = Ctx("ffngu")
    nc, p = ctx.nc, ctx.p
    x_in = ctx.inp("x3", [2048, 1024])
    maxes_in = ctx.inp("maxes1", [8, 1])
    clipx_in = ctx.inp("clip_x", [1, 1])
    wg_in = ctx.inp("w_gate", [2816, 1024])
    wu_in = ctx.inp("w_up", [2816, 1024])
    c_in = ctx.inp("c_row", [1, 1024])
    wmod_in = ctx.inp("wmod", [1024, 6144])
    hact_out = ctx.out("hact", [2816, 2048], BF16)
    mx_out = ctx.out("mx2", [1, 1])
    ctx.psum_pool()
    ctx.consts()
    mod = emit_mod(ctx, c_in, wmod_in, "mod1", sections=("sb", "cb1"))
    scx, rscx = emit_gmax_scale(ctx, "sx", maxes_in, clipx_in, None)

    qxT = ctx.sb("qxT", [128, 8 * 2048], BF16)
    rqxT = ctx.r["qxT"]
    qt_t = ctx.sb("qt_t", [128, 1024], F32)
    rqt = ctx.r["qt_t"]
    qb_t = ctx.sb("qb_t", [128, 1024], BF16)
    rqb = ctx.r["qb_t"]
    ptrb, rptrb = ctx.t["ptrb"], ctx.r["ptrb"]

    def cb(t, h3, rh3):
        p.op("vector", lambda e, h3=h3: e.tensor_scalar(qt_t[:, :], h3, scx[:, 0:1], RND_C, ALU.mult, ALU.add),
             reads=[rh3, rscx], writes=[rqt], tag=f"qx1_{t}")
        p.op("vector", lambda e: e.tensor_scalar(qt_t[:, :], qt_t[:, :], RND_C, 127.0, ALU.subtract, ALU.min),
             reads=[rqt], writes=[rqt], tag=f"qx2_{t}")
        p.op("vector", lambda e: e.tensor_scalar_max(qb_t[:, :], qt_t[:, :], -128.0), reads=[rqt], writes=[rqb], tag=f"qx3_{t}")

        def trs(e):
            inst = None
            for j in range(8):
                inst = e.transpose(ptrb[:, j * 128:(j + 1) * 128], qb_t[:, j * 128:(j + 1) * 128], ctx.t["c_ident_b"][:, :])
            return inst
        p.op("tensor", trs, reads=[rqb, ctx.r["c_ident_b"]], writes=[rptrb], tag=f"qxt_{t}")
        p.op("vector", lambda e, t=t: e.tensor_copy(
            bass.AP(qxT, t * 128, [[8 * 2048, 128], [2048, 8], [1, 128]]),
            ptrb[:, :].rearrange("p (j q) -> p j q", j=8)),
            reads=[rptrb], writes=[rqxT], tag=f"qxe_{t}")
    emit_h3_tiles(ctx, "gu", x_in, mod, cb)

    wst = [ctx.sb(f"wq{i}", [128, 1024], F32) for i in range(2)]
    rwst = [ctx.r[f"wq{i}"] for i in range(2)]
    sw = ctx.sb("sw", [128, 2], F32)
    rsw = ctx.r["sw"]
    swq = ctx.sb("swq", [128, 2], F32)
    rswq = ctx.r["swq"]
    gq = ctx.sb("gq", [128, 2 * 1024], BF16)
    rgq = ctx.r["gq"]
    hbuf = ctx.sb("hbuf", [128, 2048], BF16)
    rhbuf = ctx.r["hbuf"]
    sgt = ctx.sb("sgt", [128, 512], F32)
    rsgt = ctx.r["sgt"]
    amax = ctx.sb("amax", [128, 1], F32)
    ramax = ctx.r["amax"]
    tcol = ctx.sb("tcol", [128, 1], F32)
    rtcol = ctx.r["tcol"]
    p.op("gpsimd", lambda e: e.memset(amax[:, :], 0.0), writes=[ramax], tag="am0")
    pmm = [ctx.t[f"pmm{i}"] for i in range(4)]
    rpmm = [ctx.r[f"pmm{i}"] for i in range(4)]
    paux, rpaux = ctx.t["paux"], ctx.r["paux"]
    ptr2, rptr2 = ctx.t["ptrb"], ctx.r["ptrb"]

    for ic in range(22):
        for wi, (w_in, coln) in enumerate(((wg_in, 0), (wu_in, 1))):
            b = wi
            p.dma("sync", lambda e, ic=ic, b=b, w_in=w_in: e.dma_start(out=wst[b][:, :], in_=w_in[ic * 128:(ic + 1) * 128, :]),
                  writes=[rwst[b]], tag=f"g{ic}_ld{wi}")
            p.op("vector", lambda e, b=b, coln=coln: e.tensor_reduce(sw[:, coln:coln + 1], wst[b][:, :], AX.X, ALU.max, apply_absolute_value=True),
                 reads=[rwst[b]], writes=[rsw], tag=f"g{ic}_rm{wi}")
            p.op("vector", lambda e, coln=coln: e.tensor_scalar_mul(sw[:, coln:coln + 1], sw[:, coln:coln + 1], 1.0 / 127.0),
                 reads=[rsw], writes=[rsw], tag=f"g{ic}_sm{wi}")
            p.op("vector", lambda e, coln=coln: e.reciprocal(swq[:, coln:coln + 1], sw[:, coln:coln + 1]),
                 reads=[rsw], writes=[rswq], tag=f"g{ic}_rc{wi}")
            p.op("vector", lambda e, b=b, coln=coln: e.tensor_scalar(wst[b][:, :], wst[b][:, :], swq[:, coln:coln + 1], RND_C, ALU.mult, ALU.add),
                 reads=[rwst[b], rswq], writes=[rwst[b]], tag=f"g{ic}_q1{wi}")
            p.op("vector", lambda e, b=b: e.tensor_scalar(wst[b][:, :], wst[b][:, :], RND_C, 127.0, ALU.subtract, ALU.min),
                 reads=[rwst[b]], writes=[rwst[b]], tag=f"g{ic}_q2{wi}")
            p.op("vector", lambda e, b=b: e.tensor_scalar_max(qt_t[:, :], wst[b][:, :], -128.0),
                 reads=[rwst[b]], writes=[rqt], tag=f"g{ic}_q3{wi}")
            p.op("vector", lambda e: e.tensor_copy(qb_t[:, :], qt_t[:, :]), reads=[rqt], writes=[rqb], tag=f"g{ic}_q4{wi}")

            def trw(e, wi=wi):
                inst = None
                for j in range(8):
                    inst = e.transpose(ptr2[:, j * 128:(j + 1) * 128], qb_t[:, j * 128:(j + 1) * 128], ctx.t["c_ident_b"][:, :])
                return inst
            p.op("tensor", trw, reads=[rqb, ctx.r["c_ident_b"]], writes=[rptr2], tag=f"g{ic}_tr{wi}")
            p.op("vector", lambda e, wi=wi: e.tensor_copy(gq[:, wi * 1024:(wi + 1) * 1024], ptr2[:, :]),
                 reads=[rptr2], writes=[rgq], tag=f"g{ic}_te{wi}")
        p.op("vector", lambda e: e.tensor_scalar(sw[:, 0:2], sw[:, 0:2], scx[:, 1:2], None, ALU.mult),
             reads=[rsw, rscx], writes=[rsw], tag=f"g{ic}_sv")
        for tg in range(4):
            bg, bu = tg % 2, 2 + tg % 2

            def gmm(e, tg=tg, bg=bg):
                inst = None
                for c in range(8):
                    inst = e.matmul(pmm[bg][:, :], gq[:, c * 128:(c + 1) * 128],
                                    qxT[:, c * 2048 + tg * 512:(c) * 2048 + tg * 512 + 512], start=(c == 0), stop=(c == 7))
                return inst
            p.op("tensor", gmm, reads=[rgq, rqxT], writes=[rpmm[bg]], tag=f"g{ic}_gm{tg}")

            def umm(e, tg=tg, bu=bu):
                inst = None
                for c in range(8):
                    inst = e.matmul(pmm[bu][:, :], gq[:, 1024 + c * 128: 1024 + (c + 1) * 128],
                                    qxT[:, c * 2048 + tg * 512: c * 2048 + tg * 512 + 512], start=(c == 0), stop=(c == 7))
                return inst
            p.op("tensor", umm, reads=[rgq, rqxT], writes=[rpmm[bu]], tag=f"g{ic}_um{tg}")
            p.op("scalar", lambda e, bg=bg: e.activation(sgt[:, :], pmm[bg][:, :], AF.Silu, scale=sw[:, 0:1]),
                 reads=[rpmm[bg], rsw], writes=[rsgt], tag=f"g{ic}_si{tg}")
            p.op("vector", lambda e, tg=tg, bu=bu: e.scalar_tensor_tensor(
                hbuf[:, tg * 512:(tg + 1) * 512], pmm[bu][:, :], sw[:, 1:2], sgt[:, :], ALU.mult, ALU.mult),
                reads=[rpmm[bu], rsw, rsgt], writes=[rhbuf], tag=f"g{ic}_hb{tg}")
        p.op("vector", lambda e: e.tensor_reduce(tcol[:, :], hbuf[:, :], AX.X, ALU.max, apply_absolute_value=True),
             reads=[rhbuf], writes=[rtcol], tag=f"g{ic}_am")
        p.op("vector", lambda e: e.tensor_max(amax[:, :], amax[:, :], tcol[:, :]), reads=[ramax, rtcol], writes=[ramax], tag=f"g{ic}_am2")
        p.dma("sync", lambda e, ic=ic: e.dma_start(out=hact_out[ic * 128:(ic + 1) * 128, :], in_=hbuf[:, :]),
              reads=[rhbuf], writes=[ctx.r["out:hact"]], tag=f"g{ic}_st")

    p.op("tensor", lambda e: e.transpose(paux[0:1, 0:128], amax[:, :], ctx.t["c_ident"][:, :]),
         reads=[ramax, ctx.r["c_ident"]], writes=[rpaux], tag="am_tr")
    mx = ctx.sb("mx", [1, 1], F32)
    rmx = ctx.r["mx"]
    p.op("vector", lambda e: e.tensor_reduce(mx[:, :], paux[0:1, 0:128], AX.X, ALU.max), reads=[rpaux], writes=[rmx], tag="am_f")
    p.dma("sync", lambda e: e.dma_start(out=mx_out, in_=mx[:, :]), reads=[rmx], writes=[ctx.r["out:mx2"]], tag="stmx")
    p.build(nc)
    ctx.close()
    nc.reset()
    return ctx


def build_l1_down():
    ctx = Ctx("ffndn")
    nc, p = ctx.nc, ctx.p
    x_in = ctx.inp("x3", [2048, 1024])
    hact_in = ctx.inp("hact", [2816, 2048], BF16)
    maxes_in = ctx.inp("maxes2", [8, 1])
    clipd_in = ctx.inp("clip_down", [1, 1])
    wd_in = ctx.inp("w_down", [1024, 2816])
    c_in = ctx.inp("c_row", [1, 1024])
    wmod_in = ctx.inp("wmod", [1024, 6144])
    x4_out = ctx.out("x4", [2048, 1024])
    ctx.psum_pool()
    ctx.consts()
    mod = emit_mod(ctx, c_in, wmod_in, "mod1", sections=("gb",))
    scd, rscd = emit_gmax_scale(ctx, "sd", maxes_in, clipd_in, None)
    gb_t, rgb = mod["gb"]

    wst = [ctx.sb("wd0", [128, 2816], F32)]
    rwst = [ctx.r["wd0"]]
    wst = [wst[0], wst[0]]
    rwst = [rwst[0], rwst[0]]
    dwT = ctx.sb("dwT", [128, 22 * 1024], BF16)
    rdwT = ctx.r["dwT"]
    swd = ctx.sb("swd", [128, 8], F32)
    rswd = ctx.r["swd"]
    rq_c = ctx.sb("rq_c", [128, 1], F32)
    rrq = ctx.r["rq_c"]
    qn = ctx.sb("qn", [128, 2816], BF16)
    rqn = ctx.r["qn"]
    ptrb, rptrb = ctx.t["ptrb"], ctx.r["ptrb"]
    paux, rpaux = ctx.t["paux"], ctx.r["paux"]
    for hc in range(8):
        b = hc % 2
        p.dma("sync", lambda e, hc=hc, b=b: e.dma_start(out=wst[b][:, :], in_=wd_in[hc * 128:(hc + 1) * 128, :]),
              writes=[rwst[b]], tag=f"d{hc}_ld")
        p.op("vector", lambda e, hc=hc, b=b: e.tensor_reduce(swd[:, hc:hc + 1], wst[b][:, :], AX.X, ALU.max, apply_absolute_value=True),
             reads=[rwst[b]], writes=[rswd], tag=f"d{hc}_rm")
        p.op("vector", lambda e, hc=hc: e.tensor_scalar_mul(swd[:, hc:hc + 1], swd[:, hc:hc + 1], 1.0 / 127.0),
             reads=[rswd], writes=[rswd], tag=f"d{hc}_sm")
        p.op("vector", lambda e, hc=hc: e.reciprocal(rq_c[:, :], swd[:, hc:hc + 1]), reads=[rswd], writes=[rrq], tag=f"d{hc}_rc")
        p.op("vector", lambda e, b=b: e.tensor_scalar(wst[b][:, :], wst[b][:, :], rq_c[:, 0:1], RND_C, ALU.mult, ALU.add),
             reads=[rwst[b], rrq], writes=[rwst[b]], tag=f"d{hc}_q1")
        p.op("vector", lambda e, b=b: e.tensor_scalar(wst[b][:, :], wst[b][:, :], RND_C, 127.0, ALU.subtract, ALU.min),
             reads=[rwst[b]], writes=[rwst[b]], tag=f"d{hc}_q2")
        p.op("vector", lambda e, b=b: e.tensor_scalar_max(qn[:, :], wst[b][:, :], -128.0), reads=[rwst[b]], writes=[rqn], tag=f"d{hc}_q3")

        def trd(e, hc=hc):
            inst = None
            for ic in range(22):
                inst = e.transpose(ptrb[:, (ic % 8) * 128:(ic % 8) * 128 + 128], qn[:, ic * 128:(ic + 1) * 128], ctx.t["c_ident_b"][:, :])
            return inst
        for grp in range(3):
            n_in = min(8, 22 - grp * 8)

            def trg(e, hc=hc, grp=grp, n_in=n_in):
                inst = None
                for k in range(n_in):
                    ic = grp * 8 + k
                    inst = e.transpose(ptrb[:, k * 128:(k + 1) * 128], qn[:, ic * 128:(ic + 1) * 128], ctx.t["c_ident_b"][:, :])
                return inst
            p.op("tensor", trg, reads=[rqn, ctx.r["c_ident_b"]], writes=[rptrb], tag=f"d{hc}_t{grp}")
            p.op("vector", lambda e, hc=hc, grp=grp, n_in=n_in: e.tensor_copy(
                bass.AP(dwT, grp * 8 * 1024 + hc * 128, [[22 * 1024, 128], [1024, n_in], [1, 128]]),
                ptrb[:, 0:n_in * 128].rearrange("p (k q) -> p k q", k=n_in)),
                reads=[rptrb], writes=[rdwT], tag=f"d{hc}_te{grp}")
    swrow = ctx.sb("swrow", [1, 1024], F32)
    rswrow = ctx.r["swrow"]

    for grp in range(2):
        def swtr(e, grp=grp):
            inst = None
            for k in range(4):
                hc = grp * 4 + k
                inst = e.transpose(paux[0:1, k * 128:(k + 1) * 128], swd[:, hc:hc + 1], ctx.t["c_ident"][:, :])
            return inst
        p.op("tensor", swtr, reads=[rswd, ctx.r["c_ident"]], writes=[rpaux], tag=f"sw_tr{grp}")
        p.op("vector", lambda e, grp=grp: e.tensor_copy(swrow[:, grp * 512:(grp + 1) * 512], paux[0:1, 0:512]),
             reads=[rpaux], writes=[rswrow], tag=f"sw_te{grp}")
    swb = ctx.sb("swb", [128, 1024], F32)
    rswb = ctx.r["swb"]

    def swbc(e):
        e.matmul(ctx.t["pwide"][:, 0:512], ctx.t["c_ones"][0:1, :], swrow[:, 0:512], start=True, stop=True)
        return e.matmul(ctx.t["pwide"][:, 512:1024], ctx.t["c_ones"][0:1, :], swrow[:, 512:1024], start=True, stop=True)
    p.op("tensor", swbc, reads=[rswrow, ctx.r["c_ones"]], writes=[ctx.r["pwide"]], tag="sw_bc")
    p.op("vector", lambda e: e.tensor_mul(swb[:, :], ctx.t["pwide"][:, :], gb_t[:, :]), reads=[ctx.r["pwide"], rgb], writes=[rswb], tag="sw_gb")
    for ic in range(22):
        p.op("vector", lambda e, ic=ic: e.tensor_mul(dwT[:, ic * 1024:(ic + 1) * 1024], dwT[:, ic * 1024:(ic + 1) * 1024], swb[:, :]),
             reads=[rdwT, rswb], writes=[rdwT], tag=f"fold{ic}")

    qh = ctx.sb("qh", [128, 22 * 2048], BF16)
    rqh = ctx.r["qh"]
    hstr = [ctx.sb(f"hs{i}", [128, 2048], BF16) for i in range(2)]
    rhstr = [ctx.r[f"hs{i}"] for i in range(2)]
    qtf = ctx.sb("qtf", [128, 2048], F32)
    rqtf = ctx.r["qtf"]
    for ic in range(22):
        b = ic % 2
        p.dma("sync", lambda e, ic=ic, b=b: e.dma_start(out=hstr[b][:, :], in_=hact_in[ic * 128:(ic + 1) * 128, :]),
              writes=[rhstr[b]], tag=f"h{ic}_ld")
        p.op("vector", lambda e, b=b: e.tensor_scalar(qtf[:, :], hstr[b][:, :], scd[:, 0:1], RND_C, ALU.mult, ALU.add),
             reads=[rhstr[b], rscd], writes=[rqtf], tag=f"h{ic}_q1")
        p.op("vector", lambda e: e.tensor_scalar(qtf[:, :], qtf[:, :], RND_C, 127.0, ALU.subtract, ALU.min),
             reads=[rqtf], writes=[rqtf], tag=f"h{ic}_q2")
        p.op("vector", lambda e, ic=ic: e.tensor_scalar_max(qh[:, ic * 2048:(ic + 1) * 2048], qtf[:, :], -128.0),
             reads=[rqtf], writes=[rqh], tag=f"h{ic}_q3")

    xstr = [wst[0][:, 0:1024], wst[0][:, 1024:2048]]
    rxstr = [rwst[0], rwst[0]]
    pwide, rpwide = ctx.t["pwide"], ctx.r["pwide"]
    for t in range(16):
        b = t % 2
        p.dma("sync", lambda e, t=t, b=b: e.dma_start(out=xstr[b], in_=x_in[t * 128:(t + 1) * 128, :]),
              writes=[rxstr[b]], tag=f"o{t}_ld")

        def dmm(e, t=t):
            inst = None
            for hf in range(2):
                for ic in range(22):
                    inst = e.matmul(pwide[:, hf * 512:(hf + 1) * 512],
                                    qh[:, ic * 2048 + t * 128: ic * 2048 + (t + 1) * 128],
                                    dwT[:, ic * 1024 + hf * 512: ic * 1024 + (hf + 1) * 512],
                                    start=(ic == 0), stop=(ic == 21))
            return inst
        p.op("tensor", dmm, reads=[rqh, rdwT], writes=[rpwide], tag=f"o{t}_mm")
        for hf in range(2):
            p.op("vector", lambda e, b=b, hf=hf: e.scalar_tensor_tensor(
                xstr[b][:, hf * 512:(hf + 1) * 512], pwide[:, hf * 512:(hf + 1) * 512], scd[:, 1:2],
                xstr[b][:, hf * 512:(hf + 1) * 512], ALU.mult, ALU.add),
                reads=[rpwide, rscd, rxstr[b]], writes=[rxstr[b]], tag=f"o{t}_cb{hf}")
        p.dma("sync", lambda e, t=t, b=b: e.dma_start(out=x4_out[t * 128:(t + 1) * 128, :], in_=xstr[b]),
              reads=[rxstr[b]], writes=[ctx.r["out:x4"]], tag=f"o{t}_st")

    p.build(nc)
    ctx.close()
    nc.reset()
    return ctx





FUSED_STAGE = 99
FUSED_SKIP = set()


def build_fused():
    ctx = Ctx("fused")
    nc, p = ctx.nc, ctx.p
    x_in = ctx.inp("x", [2048, 1024])
    c0_in = ctx.inp("c0_row", [1, 1024])
    c1_in = ctx.inp("c1_row", [1, 1024])
    wmod0_in = ctx.inp("wmod0", [1024, 6144])
    wmod1_in = ctx.inp("wmod1", [1024, 6144])
    wat_in = ctx.inp("wattnT", [1024, 3072])
    wproj_in = ctx.inp("wprojT", [1024, 1024])
    gatew_in = ctx.inp("gatewT", [1024, 9])
    dg_in = ctx.inp("dgT", [8, 1024, 512])
    du_in = ctx.inp("duT", [8, 1024, 512])
    dd_in = ctx.inp("ddT", [8, 512, 1024])
    sg_in = ctx.inp("sgT", [1024, 2048])
    su_in = ctx.inp("suT", [1024, 2048])
    sd_in = ctx.inp("sdT", [2048, 1024])
    win_in = ctx.inp("winT", [1024, 2576])
    winpad_in = ctx.inp("winpadT", [1024, 1024])
    dtw_in = ctx.inp("dtwT", [16, 16])
    dtb_in = ctx.inp("dtb", [16, 1])
    alog_in = ctx.inp("alog", [16, 1])
    dpar_in = ctx.inp("dpar", [1, 16])
    normw_in = ctx.inp("normw", [1, 1024])
    wout_in = ctx.inp("woutT", [1024, 1024])
    clipx_in = ctx.inp("clip_x", [1, 1])
    clipd_in = ctx.inp("clip_down", [1, 1])
    wg_in = ctx.inp("w_gate", [2816, 1024])
    wu_in = ctx.inp("w_up", [2816, 1024])
    wd_in = ctx.inp("w_down", [1024, 2816])
    pmask_in = ctx.inp("pmask", [8, 1])
    sflag_in = ctx.inp("sflag", [128, 1])
    gidx_in = ctx.inp("gidx", [128, 24], I32)
    yidx_in = ctx.inp("yidx", [128, 32], I32)
    x4_out = ctx.out("x4", [2048, 1024])

    qkv_ai = ctx.dram("qkv_ai", [3072, 2048], BF16)
    qkv_ao = ctx.dram("qkv_ao", [6144, 2048], BF16)
    y_ai = ctx.dram("y_ai", [4096, 512], BF16)
    y_ao = ctx.dram("y_ao", [8192, 512], BF16)
    cnt_ai = ctx.dram("cnt_ai", [1, 8], F32)
    cnt_ao = ctx.dram("cnt_ao", [8, 8], F32)
    st_ai = ctx.dram("st_ai", [128, 256], F32)
    st_ao = ctx.dram("st_ao", [256, 256], F32)
    mx1_ai = ctx.dram("mx1_ai", [1, 1], F32)
    mx1_ao = ctx.dram("mx1_ao", [1, 1], F32)
    mx2_ai = ctx.dram("mx2_ai", [1, 1], F32)
    mx2_ao = ctx.dram("mx2_ao", [1, 1], F32)
    hact_d = ctx.dram("hact_d", [2816, 2048], BF16)
    x3_d = ctx.dram("x3_d", [2048, 1024], F32)

    ctx.psum_pool()
    ctx.consts()
    for nm, sh, dt_ in (("_lns_mean", [128, 1], F32), ("_lns_sq", [128, 1024], F32),
                        ("_lns_ssq", [128, 1], F32), ("_lns_var", [128, 1], F32),
                        ("_lns_rstd", [128, 1], F32), ("_lns_nmr", [128, 1], F32),
                        ("_lns_xn", [128, 1024], F32), ("_lns_hbf", [128, 1024], BF16),
                        ("_mod_tmprow", [1, 1024], F32), ("_mod_sgnrow", [1, 1024], F32)):
        ctx.sb_shared(nm, sh, dt_)
    for v in (1e-5, -0.5, 1e-6, -1.0, 1.0):
        ctx.const_col(v)
    modt = {}
    for nm in ("sa", "ca1", "ga", "sb", "cb1", "gb"):
        h = ctx.sb("mA_" + nm, [128, 1024], F32)
        modt[nm] = (h, ctx.r["mA_" + nm])

    AR = 167936
    ctx.arena_init(AR)
    XO = 65536
    fx, rfx = ctx.arena_sb("fx", [128, 16384], F32, 0)
    for t in range(16):
        p.dma("sync", lambda e, t=t: e.dma_start(out=fx[:, t * 1024:(t + 1) * 1024], in_=x_in[t * 128:(t + 1) * 128, :]),
              writes=[rfx], tag=f"f_ldx{t}")

    ctx.phase(XO)
    hT = ctx.sb("p1_hT", [128, 8 * 2048], BF16)
    rhT = ctx.r["p1_hT"]
    MODO = XO + 33 * 1024
    ctx.phase(MODO)
    wstA = ctx.sb("p1_wA", [128, 1024], F32)
    wstB = ctx.sb("p1_wB", [128, 1024], F32)
    wst = [(wstA, ctx.r["p1_wA"]), (wstB, ctx.r["p1_wB"])]
    emit_mod(ctx, c0_in, wmod0_in, "fm0", wst=wst, dst=modt)
    ctx.phase(MODO + 8192)

    def tile_src1(t):
        return fx[:, t * 1024:(t + 1) * 1024], rfx
    emit_ln_mod(ctx, "fln0a", tile_src1, modt["ca1"], modt["sa"], hT, rhT)

    acc8 = ctx.sb("p1_acc", [128, 24], F32)
    racc8 = ctx.r["p1_acc"]
    for c in range(8):
        for hf in range(3):
            b = (c * 3 + hf) % 2
            p.dma("sync", lambda e, c=c, hf=hf, b=b: e.dma_start(out=wst[b][0][:, 0:1024], in_=wat_in[c * 128:(c + 1) * 128, hf * 1024:(hf + 1) * 1024]),
                  writes=[wst[b][1]], tag=f"f_qwl{c}_{hf}")
            p.op("vector", lambda e, c=c, hf=hf, b=b: e.tensor_reduce(acc8[:, c * 3 + hf:c * 3 + hf + 1], wst[b][0][:, 0:1024], AX.X, ALU.add, apply_absolute_value=True),
                 reads=[wst[b][1]], writes=[racc8], tag=f"f_qwa{c}_{hf}")
    stot, rstot = emit_abs_reduce(ctx, "f_qws", acc8[:, :], racc8, flavor=ALU.add)
    p.op("vector", lambda e: e.tensor_scalar(stot[:, :], stot[:, :], 1.0 / (3072 * 1024), 1e-8, ALU.mult, ALU.max),
         reads=[rstot], writes=[rstot], tag="f_qwsc")
    rinv = ctx.sb("f_qwri", [1, 1], F32)
    rrinv = ctx.r["f_qwri"]
    p.op("vector", lambda e: e.reciprocal(rinv[:, :], stot[:, :]), reads=[rstot], writes=[rrinv], tag="f_qwrc")
    paux, rpaux = ctx.t["paux"], ctx.r["paux"]

    def bc2(e):
        e.matmul(paux[:, 0:1], ctx.t["c_ones"][0:1, :], rinv[:, :], start=True, stop=True)
        return e.matmul(paux[:, 1:2], ctx.t["c_ones"][0:1, :], stot[:, :], start=True, stop=True)
    p.op("tensor", bc2, reads=[rrinv, rstot, ctx.r["c_ones"]], writes=[rpaux], tag="f_qwbc")
    rcol = ctx.sb("f_qwcol", [128, 2], F32)
    rrcol = ctx.r["f_qwcol"]
    p.op("vector", lambda e: e.tensor_copy(rcol[:, :], paux[:, 0:2]), reads=[rpaux], writes=[rrcol], tag="f_qwbe")
    scq = ctx.sb("f_scq", [128, 1], F32)
    rscq = ctx.r["f_scq"]
    p.op("vector", lambda e: e.tensor_scalar_mul(scq[:, :], rcol[:, 1:2], 0.125), reads=[rrcol], writes=[rscq], tag="f_scq")

    qw = ctx.sb("p1_qw", [128, 8 * 3072], BF16)
    rqw = ctx.r["p1_qw"]
    qtmp = ctx.sb("p1_qtmp", [128, 1024], F32)
    rqtmp = ctx.r["p1_qtmp"]
    for c in range(8):
        for hf in range(3):
            b = (c * 3 + hf) % 2
            p.dma("sync", lambda e, c=c, hf=hf, b=b: e.dma_start(out=wst[b][0][:, 0:1024], in_=wat_in[c * 128:(c + 1) * 128, hf * 1024:(hf + 1) * 1024]),
                  writes=[wst[b][1]], tag=f"f_qwl2{c}_{hf}")
            p.op("vector", lambda e, b=b: e.tensor_scalar(qtmp[:, :], wst[b][0][:, 0:1024], rcol[:, 0:1], RND_C, ALU.mult, ALU.add),
                 reads=[wst[b][1], rrcol], writes=[rqtmp], tag=f"f_q1{c}_{hf}")
            p.op("vector", lambda e: e.tensor_scalar(qtmp[:, :], qtmp[:, :], RND_C, 2.0, ALU.subtract, ALU.min),
                 reads=[rqtmp], writes=[rqtmp], tag=f"f_q2{c}_{hf}")
            p.op("vector", lambda e, c=c, hf=hf: e.tensor_scalar_max(qw[:, c * 3072 + hf * 1024: c * 3072 + (hf + 1) * 1024], qtmp[:, :], -2.0),
                 reads=[rqtmp], writes=[rqw], tag=f"f_q3{c}_{hf}")

    pmm = [ctx.t[f"pmm{i}"] for i in range(4)]
    rpmm = [ctx.r[f"pmm{i}"] for i in range(4)]
    stg0 = ctx.sb("p1_stg0", [128, 2048], BF16)
    stg = [stg0, stg0]
    rstg = [ctx.r["p1_stg0"], ctx.r["p1_stg0"]]
    assert ctx._phase_cursor <= AR, ("phase1 over", ctx._phase_cursor)
    rqai = ctx.r["qkv_ai"]
    for j in range(24):
        sb_ = j % 2
        for tg in range(4):
            bank = tg % 4

            def mms(e, j=j, tg=tg, bank=bank):
                inst = None
                for c in range(8):
                    inst = e.matmul(pmm[bank][:, :], qw[:, c * 3072 + j * 128: c * 3072 + (j + 1) * 128],
                                    hT[:, c * 2048 + tg * 512: c * 2048 + (tg + 1) * 512],
                                    start=(c == 0), stop=(c == 7))
                return inst
            p.op("tensor", mms, reads=[rqw, rhT], writes=[rpmm[bank]], tag=f"f_qk{j}_{tg}")
            scale = scq[:, 0:1] if j < 8 else rcol[:, 1:2]
            rscale = rscq if j < 8 else rrcol
            p.op("scalar", lambda e, tg=tg, bank=bank, sb_=sb_, scale=scale: e.activation(
                stg[sb_][:, tg * 512:(tg + 1) * 512], pmm[bank][:, :], AF.Identity, scale=scale),
                reads=[rpmm[bank], rscale], writes=[rstg[sb_]], tag=f"f_qe{j}_{tg}")
        p.dma("sync", lambda e, sb_=sb_, j=j: e.dma_start(
            out=qkv_ai[j * 128:(j + 1) * 128, :], in_=stg[sb_][:, :]),
            reads=[rstg[sb_]], writes=[rqai], tag=f"f_qs{j}")

    if FUSED_STAGE == 1:
        dsa = ctx.out("dbg_sa", [128, 1024])
        p.dma("sync", lambda e: e.dma_start(out=dsa, in_=modt["sa"][0][:, :]),
              reads=[modt["sa"][1]], writes=[ctx.r["out:dbg_sa"]], tag="dbg_sa")
        dht = ctx.out("dbg_hT", [128, 2048], BF16)
        p.dma("sync", lambda e: e.dma_start(out=dht, in_=hT[:, 0:2048]),
              reads=[rhT], writes=[ctx.r["out:dbg_hT"]], tag="dbg_ht")
        dqw = ctx.out("dbg_qw", [128, 3072], BF16)
        p.dma("sync", lambda e: e.dma_start(out=dqw, in_=qw[:, 0:3072]),
              reads=[rqw], writes=[ctx.r["out:dbg_qw"]], tag="dbg_qw")
        dbq = ctx.out("dbg_qai", [3072, 2048], BF16)
        sc_dbg = stg0
        rdbg1 = rstg[0]
        for t in range(24):
            p.dma("sync", lambda e, t=t: e.dma_start(out=sc_dbg[:, :], in_=qkv_ai[t * 128:(t + 1) * 128, :]),
                  reads=[rqai], writes=[rdbg1], tag=f"dbg1_l{t}")
            p.dma("sync", lambda e, t=t: e.dma_start(out=dbq[t * 128:(t + 1) * 128, :], in_=sc_dbg[:, :]),
                  reads=[rdbg1], writes=[ctx.r["out:dbg_qai"]], tag=f"dbg1_s{t}")
        p.build(nc, verbose=True)
        ctx.close()
        nc.reset()
        return ctx

    p.op("gpsimd", lambda e: e.collective_compute(
        "AllGather", ALU.bypass, replica_groups=[[0, 1], [2, 3], [4, 5], [6, 7]],
        ins=[qkv_ai], outs=[qkv_ao]), reads=[rqai], writes=[ctx.r["qkv_ao"]],
        collective=True, tag="f_ag_qkv")
    rqao = ctx.r["qkv_ao"]

    if FUSED_STAGE == 15:
        dbq = ctx.out("dbg_qao", [512, 2048], BF16)
        sc_dbg = stg0
        rdbg1 = rstg[0]
        for t in range(4):
            p.dma("sync", lambda e, t=t: e.dma_start(out=sc_dbg[:, :], in_=qkv_ao[3072 + t * 128: 3072 + (t + 1) * 128, :]),
                  reads=[rqao], writes=[rdbg1], tag=f"dbg2_l{t}")
            p.dma("sync", lambda e, t=t: e.dma_start(out=dbq[t * 128:(t + 1) * 128, :], in_=sc_dbg[:, :]),
                  reads=[rdbg1], writes=[ctx.r["out:dbg_qao"]], tag=f"dbg2_s{t}")
        p.build(nc, verbose=True)
        ctx.close()
        nc.reset()
        return ctx

    ctx.phase(XO)
    gidx = ctx.sb("p2_gidx", [128, 24], I32)
    rgidx = ctx.r["p2_gidx"]
    p.dma("sync", lambda e: e.dma_start(out=gidx[:, :], in_=gidx_in), writes=[rgidx], tag="f2_lgi")
    qt8 = ctx.sb("p2_qt", [128, 4 * 4096], BF16)
    rqt8 = ctx.r["p2_qt"]
    kt8 = ctx.sb("p2_kt", [128, 4 * 4096], BF16)
    rkt8 = ctx.r["p2_kt"]
    for ch in range(4):
        for half in range(2):
            p.dma("gpsimd", lambda e, ch=ch, half=half: e.indirect_dma_start(
                out=qt8[:, ch * 4096 + half * 2048: ch * 4096 + (half + 1) * 2048],
                out_offset=None, in_=qkv_ao,
                in_offset=IndirectOffsetOnAxis(gidx[:, ch * 2 + half: ch * 2 + half + 1], 0)),
                reads=[rqao, rgidx], writes=[rqt8], tag=f"f2_lq{ch}_{half}")
            p.dma("gpsimd", lambda e, ch=ch, half=half: e.indirect_dma_start(
                out=kt8[:, ch * 4096 + half * 2048: ch * 4096 + (half + 1) * 2048],
                out_offset=None, in_=qkv_ao,
                in_offset=IndirectOffsetOnAxis(gidx[:, 8 + ch * 2 + half: 8 + ch * 2 + half + 1], 0)),
                reads=[rqao, rgidx], writes=[rkt8], tag=f"f2_lk{ch}_{half}")
    vtb = [ctx.sb(f"p2_vtb{i}", [128, 4096], BF16) for i in range(2)]
    rvtb = [ctx.r[f"p2_vtb{i}"] for i in range(2)]

    if FUSED_STAGE == 17:
        dbq = ctx.out("dbg_qt8", [128, 4096], BF16)
        p.dma("sync", lambda e: e.dma_start(out=dbq, in_=qt8[:, 0:4096]),
              reads=[rqt8], writes=[ctx.r["out:dbg_qt8"]], tag="dbg3")
        p.build(nc, verbose=True)
        ctx.close()
        nc.reset()
        return ctx

    rmc = ctx.sb("p2_rmc", [128, 4], F32)
    rrmc = ctx.r["p2_rmc"]
    tcol2 = ctx.sb("p2_tc2", [128, 1], F32)
    rtcol2 = ctx.r["p2_tc2"]
    rmneg = [ctx.sb(f"p2_rmn{i}", [1, 512], BF16) for i in range(2)]
    rrmneg = [ctx.r[f"p2_rmn{i}"] for i in range(2)]

    vsb0 = ctx.sb("p2_vs0", [128, 32 * 130], BF16)
    vsb = [vsb0, vsb0]
    rvsb = [ctx.r["p2_vs0"], ctx.r["p2_vs0"]]
    p.op("gpsimd", lambda e: e.memset(bass.AP(vsb0, 64, [[4160, 128], [65, 64]]), 1.0),
         writes=[rvsb[0]], tag="f2_vo0")
    pt_sb = [ctx.sb(f"p2_pt{i}", [128, 512], BF16) for i in range(2)]
    rpt_sb = [ctx.r[f"p2_pt{i}"] for i in range(2)]
    ysb = ctx.sb("p2_ysb", [65, 512], F32)
    rysb = ctx.r["p2_ysb"]
    ywb = [ctx.sb(f"p2_ywb{i}", [64, 512], BF16) for i in range(2)]
    rywb = [ctx.r[f"p2_ywb{i}"] for i in range(2)]
    ytk = [ctx.sb(f"p2_ytk{i}", [128, 256], BF16) for i in range(2)]
    rytk = [ctx.r[f"p2_ytk{i}"] for i in range(2)]
    ryai = ctx.r["y_ai"]
    pS = [pmm[0], pmm[1]]
    rpS = [rpmm[0], rpmm[1]]
    pAV, rpAV = pmm[2], rpmm[2]
    pBC, rpBC = pmm[3], rpmm[3]
    ptrb, rptrb = ctx.t["ptrb"], ctx.r["ptrb"]
    HP_N = 1 if FUSED_STAGE == 19 else 4
    for hp in range(HP_N):
        vb = 0
        vtb_b = hp % 2
        for half in range(2):
            p.dma("gpsimd", lambda e, hp=hp, half=half, vtb_b=vtb_b: e.indirect_dma_start(
                out=vtb[vtb_b][:, half * 2048:(half + 1) * 2048],
                out_offset=None, in_=qkv_ao,
                in_offset=IndirectOffsetOnAxis(gidx[:, 16 + hp * 2 + half: 16 + hp * 2 + half + 1], 0)),
                reads=[rqao, rgidx], writes=[rvtb[vtb_b]], tag=f"f2_lv{hp}_{half}")
        for kt_ in range(32):
            def vtr(e, kt_=kt_, vtb_b=vtb_b):
                return e.transpose(ptrb[:, 0:128], vtb[vtb_b][:, kt_ * 128:(kt_ + 1) * 128],
                                   ctx.t["c_ident_b"][:, :])
            p.op("tensor", vtr, reads=[rvtb[vtb_b], ctx.r["c_ident_b"]], writes=[rptrb], tag=f"f2_vt{hp}_{kt_}")
            p.op("vector", lambda e, kt_=kt_: e.tensor_copy(
                bass.AP(vsb0, kt_ * 130, [[4160, 128], [65, 2], [1, 64]]),
                ptrb[:, 0:128].rearrange("p (hh d) -> p hh d", hh=2)),
                reads=[rptrb], writes=[rvsb[0]], tag=f"f2_ve{hp}_{kt_}")
        pwide2, rpwide2 = ctx.t["pwide"], ctx.r["pwide"]
        for hh in range(2):
            h = hp * 2 + hh
            ch, b = h // 2, (h % 2) * 64
            for qg in range(8):
                qstart = qg * 512
                nkc = (qg + 1) * 4
                rb = qg % 2
                for qb in range(4):
                    for kc5 in range(qg + 1):
                        def rmm(e, ch=ch, b=b, qg=qg, qb=qb, kc5=kc5):
                            return e.matmul(pwide2[:, 0:512],
                                            qt8[b:b + 64, ch * 4096 + qg * 512 + qb * 128: ch * 4096 + qg * 512 + (qb + 1) * 128],
                                            kt8[b:b + 64, ch * 4096 + kc5 * 512: ch * 4096 + (kc5 + 1) * 512],
                                            start=True, stop=True)
                        p.op("tensor", rmm, reads=[rqt8, rkt8], writes=[rpwide2], tag=f"f2rm{h}_{qg}_{qb}_{kc5}")
                        if kc5 == 0:
                            p.op("vector", lambda e, qb=qb: e.tensor_reduce(rmc[:, qb:qb + 1], pwide2[:, 0:512], AX.X, ALU.max),
                                 reads=[rpwide2], writes=[rrmc], tag=f"f2rr{h}_{qg}_{qb}_{kc5}")
                        else:
                            p.op("vector", lambda e: e.tensor_reduce(tcol2[:, :], pwide2[:, 0:512], AX.X, ALU.max),
                                 reads=[rpwide2], writes=[rtcol2], tag=f"f2rr{h}_{qg}_{qb}_{kc5}")
                            p.op("vector", lambda e, qb=qb: e.tensor_max(rmc[:, qb:qb + 1], rmc[:, qb:qb + 1], tcol2[:, :]),
                                 reads=[rtcol2, rrmc], writes=[rrmc], tag=f"f2rx{h}_{qg}_{qb}_{kc5}")

                def rtr(e):
                    inst = None
                    for qb in range(4):
                        inst = e.transpose(paux[0:1, qb * 128:(qb + 1) * 128], rmc[:, qb:qb + 1], ctx.t["c_ident"][:, :])
                    return inst
                p.op("tensor", rtr, reads=[rrmc, ctx.r["c_ident"]], writes=[rpaux], tag=f"f2rt{h}_{qg}")
                p.op("vector", lambda e, rb=rb: e.tensor_scalar_mul(rmneg[rb][:, :], paux[0:1, 0:512], -1.0),
                     reads=[rpaux], writes=[rrmneg[rb]], tag=f"f2rn{h}_{qg}")
                for kc in range(nkc):
                    sb2 = kc % 2
                    if "s" in FUSED_SKIP:
                        continue

                    def smm(e, h=h, qg=qg, kc=kc, sb2=sb2, ch=ch, b=b, rb=rb):
                        e.matmul(pS[sb2][:, :],
                                 kt8[b:b + 64, ch * 4096 + kc * 128: ch * 4096 + (kc + 1) * 128],
                                 qt8[b:b + 64, ch * 4096 + qg * 512: ch * 4096 + (qg + 1) * 512],
                                 start=True, stop=False)
                        return e.matmul(pS[sb2][:, :], ctx.t["c_ones_b"][0:1, :], rmneg[rb][:, :],
                                        start=False, stop=True)
                    p.op("tensor", smm, reads=[rkt8, rqt8, rrmneg[rb], ctx.r["c_ones_b"]],
                         writes=[rpS[sb2]], tag=f"f2S{h}_{qg}_{kc}")
                    p.op("scalar", lambda e, sb2=sb2: e.activation(
                        pt_sb[sb2][:, :], pS[sb2][:, :], AF.Exp, bias=0.0),
                        reads=[rpS[sb2]], writes=[rpt_sb[sb2]], tag=f"f2E{h}_{qg}_{kc}")
                    if kc * 128 + 127 > qstart:
                        p.op("gpsimd", lambda e, sb2=sb2, qstart=qstart, kc=kc: e.affine_select(
                            out=pt_sb[sb2][:, :], in_=pt_sb[sb2][:, :], compare_op=ALU.is_ge,
                            fill=0.0, base=qstart - kc * 128, pattern=[[1, 512]], channel_multiplier=-1),
                            reads=[rpt_sb[sb2]], writes=[rpt_sb[sb2]], tag=f"f2M{h}_{qg}_{kc}")

                    if "av" in FUSED_SKIP:
                        continue

                    def avmm(e, h=h, kc=kc, sb2=sb2, vb=vb, first=(kc == 0), last=(kc == nkc - 1)):
                        return e.matmul(pAV[0:65, :], vsb[vb][:, kc * 130 + (h % 2) * 65: kc * 130 + (h % 2) * 65 + 65],
                                        pt_sb[sb2][:, :], start=first, stop=last)
                    p.op("tensor", avmm, reads=[rvsb[vb], rpt_sb[sb2]], writes=[rpAV], tag=f"f2A{h}_{qg}_{kc}")
                if "av" in FUSED_SKIP or "yrbw" in FUSED_SKIP:
                    continue
                p.op("vector", lambda e: e.tensor_copy(ysb[:, :], pAV[0:65, :]), reads=[rpAV], writes=[rysb], tag=f"f2Y{h}_{qg}")
                p.op("vector", lambda e: e.reciprocal(ysb[64:65, :], ysb[64:65, :]), reads=[rysb], writes=[rysb], tag=f"f2R{h}_{qg}")
                p.op("tensor", lambda e: e.matmul(pBC[0:64, :], ctx.t["c_ones"][64:65, 0:64], ysb[64:65, :], start=True, stop=True),
                     reads=[rysb, ctx.r["c_ones"]], writes=[rpBC], tag=f"f2B{h}_{qg}")
                wb = qg % 2
                p.op("vector", lambda e, wb=wb: e.tensor_tensor(
                    ywb[wb][:, :], ysb[0:64, :], pBC[0:64, :], ALU.mult),
                    reads=[rysb, rpBC], writes=[rywb[wb]], tag=f"f2W{h}_{qg}")

                if "epi" in FUSED_SKIP:
                    continue

                def wtr(e, wb=wb):
                    inst = None
                    for sub in range(4):
                        inst = e.transpose(ptrb[:, sub * 64:(sub + 1) * 64],
                                           ywb[wb][:, sub * 128:(sub + 1) * 128],
                                           ctx.t["c_ident_b"][0:64, 0:64])
                    return inst
                p.op("tensor", wtr, reads=[rywb[wb], ctx.r["c_ident_b"]], writes=[rptrb], tag=f"f2T{h}_{qg}")
                p.op("vector", lambda e, wb=wb: e.tensor_copy(ytk[wb][:, :], ptrb[:, 0:256]),
                     reads=[rptrb], writes=[rytk[wb]], tag=f"f2U{h}_{qg}")
                if "ydma" in FUSED_SKIP:
                    continue
                p.dma("sync", lambda e, h=h, qg=qg, wb=wb: e.dma_start(
                    out=bass.AP(y_ai.tensor, qg * 512 * 512 + h * 64, [[512, 128], [65536, 4], [1, 64]]),
                    in_=ytk[wb][:, :].rearrange("p (s d) -> p s d", s=4)),
                    reads=[rytk[wb]], writes=[ryai], tag=f"f2_ys{h}_{qg}")

    if FUSED_STAGE == 19:
        dby = ctx.out("dbg_yai", [4096, 512], BF16)
        sc_dbg = ctx.sb("dbg_sc3", [128, 512], BF16)
        rdbg = ctx.r["dbg_sc3"]
        for t in range(32):
            p.dma("sync", lambda e, t=t: e.dma_start(out=sc_dbg[:, :], in_=y_ai[t * 128:(t + 1) * 128, :]),
                  reads=[ryai], writes=[rdbg], tag=f"dbg5_l{t}")
            p.dma("sync", lambda e, t=t: e.dma_start(out=dby[t * 128:(t + 1) * 128, :], in_=sc_dbg[:, :]),
                  reads=[rdbg], writes=[ctx.r["out:dbg_yai"]], tag=f"dbg5_s{t}")
        p.build(nc, verbose=True)
        ctx.close()
        nc.reset()
        return ctx

    p.op("gpsimd", lambda e: e.collective_compute(
        "AllGather", ALU.bypass, replica_groups=[[0, 1], [2, 3], [4, 5], [6, 7]],
        ins=[y_ai], outs=[y_ao]), reads=[ryai], writes=[ctx.r["y_ao"]],
        collective=True, tag="f_ag_y")
    ryao = ctx.r["y_ao"]

    if FUSED_STAGE == 2:
        dby = ctx.out("dbg_yai", [4096, 512], BF16)
        sc_dbg = ctx.sb("dbg_sc", [128, 512], BF16)
        rdbg = ctx.r["dbg_sc"]
        for t in range(32):
            p.dma("sync", lambda e, t=t: e.dma_start(out=sc_dbg[:, :], in_=y_ai[t * 128:(t + 1) * 128, :]),
                  reads=[ryai], writes=[rdbg], tag=f"dbg_l{t}")
            p.dma("sync", lambda e, t=t: e.dma_start(out=dby[t * 128:(t + 1) * 128, :], in_=sc_dbg[:, :]),
                  reads=[rdbg], writes=[ctx.r["out:dbg_yai"]], tag=f"dbg_s{t}")
        p.build(nc, verbose=True)
        ctx.close()
        nc.reset()
        return ctx

    ctx.phase(XO)
    yidx = ctx.sb("p3_yidx", [128, 32], I32)
    ryidx = ctx.r["p3_yidx"]
    p.dma("sync", lambda e: e.dma_start(out=yidx[:, :], in_=yidx_in), writes=[ryidx], tag="f3_lyi")
    O3A = ctx._phase_cursor
    ygb = [ctx.sb(f"p3_ygb{i}", [128, 1024], BF16) for i in range(2)]
    rygb = [ctx.r[f"p3_ygb{i}"] for i in range(2)]
    ytt = [ctx.sb(f"p3_ytt{i}", [128, 1024], BF16) for i in range(2)]
    rytt = [ctx.r[f"p3_ytt{i}"] for i in range(2)]
    wstA3 = ctx.sb("p3_wA", [128, 1024], F32)
    wstB3 = ctx.sb("p3_wB", [128, 1024], F32)
    wst3 = [(wstA3, ctx.r["p3_wA"]), (wstB3, ctx.r["p3_wB"])]
    qwp, rqwp, pcol, rpcol = emit_quant_weight(ctx, "f3qp", wproj_in, 8, 1024, wst3)
    gas = ctx.sb("p3_gas", [128, 1024], F32)
    rgas = ctx.r["p3_gas"]
    p.op("vector", lambda e: e.tensor_scalar(gas[:, :], modt["ga"][0][:, :], pcol[:, 1:2], None, ALU.mult),
         reads=[modt["ga"][1], rpcol], writes=[rgas], tag="f3_gas")
    pwide, rpwide = ctx.t["pwide"], ctx.r["pwide"]
    tmpo = ctx.sb("p3_tmpo", [128, 1024], F32)
    rtmpo = ctx.r["p3_tmpo"]

    if FUSED_STAGE == 27:
        dbyg = ctx.out("dbg_ygb", [2048, 1024], BF16)
        dbtm = ctx.out("dbg_tmpo", [2048, 1024])
        dbqw = ctx.out("dbg_qwp", [128, 8192], BF16)
        dbyt = ctx.out("dbg_ytt", [2048, 1024], BF16)
        dbgas = ctx.out("dbg_gas", [128, 1024])
        p.dma("sync", lambda e: e.dma_start(out=dbqw, in_=qwp[:, :]),
              reads=[rqwp], writes=[ctx.r["out:dbg_qwp"]], tag="dbqw")
        p.dma("sync", lambda e: e.dma_start(out=dbgas, in_=gas[:, :]),
              reads=[rgas], writes=[ctx.r["out:dbg_gas"]], tag="dbgas")

    def tile_src3(t):
        bb = t % 2
        for hhalf in range(2):
            p.dma("gpsimd", lambda e, t=t, hhalf=hhalf, bb=bb: e.indirect_dma_start(
                out=ygb[bb][:, hhalf * 512:(hhalf + 1) * 512], out_offset=None, in_=y_ao,
                in_offset=IndirectOffsetOnAxis(yidx[:, t * 2 + hhalf: t * 2 + hhalf + 1], 0)),
                reads=[ryao, ryidx], writes=[rygb[bb]], tag=f"f3_yg{t}_{hhalf}")
        if FUSED_STAGE == 27:
            p.dma("sync", lambda e, t=t, bb=bb: e.dma_start(out=dbyg[t * 128:(t + 1) * 128, :], in_=ygb[bb][:, :]),
                  reads=[rygb[bb]], writes=[ctx.r["out:dbg_ygb"]], tag=f"dbyg{t}")

        def ytr(e, bb=bb):
            inst = None
            for j in range(8):
                inst = e.transpose(ptrb[:, j * 128:(j + 1) * 128], ygb[bb][:, j * 128:(j + 1) * 128],
                                   ctx.t["c_ident_b"][:, :])
            return inst
        p.op("tensor", ytr, reads=[rygb[bb], ctx.r["c_ident_b"]], writes=[rptrb], tag=f"f3_ytr{t}")
        p.op("vector", lambda e, bb=bb: e.tensor_copy(ytt[bb][:, :], ptrb[:, :]),
             reads=[rptrb], writes=[rytt[bb]], tag=f"f3_yte{t}")
        if FUSED_STAGE == 27:
            p.dma("sync", lambda e, t=t, bb=bb: e.dma_start(out=dbyt[t * 128:(t + 1) * 128, :], in_=ytt[bb][:, :]),
                  reads=[rytt[bb]], writes=[ctx.r["out:dbg_ytt"]], tag=f"dbyt{t}")

        def mms3(e, bb=bb):
            inst = None
            for hf in range(2):
                for c in range(8):
                    inst = e.matmul(pwide[:, hf * 512:(hf + 1) * 512],
                                    ytt[bb][:, c * 128:(c + 1) * 128],
                                    qwp[:, c * 1024 + hf * 512: c * 1024 + (hf + 1) * 512],
                                    start=(c == 0), stop=(c == 7))
            return inst
        p.op("tensor", mms3, reads=[rytt[bb], rqwp], writes=[rpwide], tag=f"f3_om{t}")
        p.op("vector", lambda e: e.tensor_mul(tmpo[:, :], pwide[:, :], gas[:, :]),
             reads=[rpwide, rgas], writes=[rtmpo], tag=f"f3_os{t}")
        if FUSED_STAGE == 27:
            p.dma("sync", lambda e, t=t: e.dma_start(out=dbtm[t * 128:(t + 1) * 128, :], in_=tmpo[:, :]),
                  reads=[rtmpo], writes=[ctx.r["out:dbg_tmpo"]], tag=f"dbtm{t}")
        p.op("vector", lambda e, t=t: e.tensor_add(fx[:, t * 1024:(t + 1) * 1024], fx[:, t * 1024:(t + 1) * 1024], tmpo[:, :]),
             reads=[rtmpo, rfx], writes=[rfx], tag=f"f3_oa{t}")
        return fx[:, t * 1024:(t + 1) * 1024], rfx

    hT3 = ctx.sb("p3_hT", [128, 8 * 2048], BF16)
    rhT3 = ctx.r["p3_hT"]
    glw = ctx.sb("p3_glw", [128, 144], F32)
    rglw = ctx.r["p3_glw"]
    emit_ln_mod(ctx, "fln0b", tile_src3, modt["cb1"], modt["sb"], hT3, rhT3)

    if FUSED_STAGE in (25, 27):
        if FUSED_STAGE == 25:
            dbx = ctx.out("dbg_x1", [2048, 1024])
            for t in range(16):
                p.dma("sync", lambda e, t=t: e.dma_start(out=dbx[t * 128:(t + 1) * 128, :], in_=fx[:, t * 1024:(t + 1) * 1024]),
                      reads=[rfx], writes=[ctx.r["out:dbg_x1"]], tag=f"dbgx1_{t}")
        p.build(nc, verbose=True)
        ctx.close()
        nc.reset()
        return ctx

    ctx.phase(O3A)
    gw_sb = ctx.sb("p3_gwsb", [128, 8 * 9], BF16)
    rgw_sb = ctx.r["p3_gwsb"]
    p.dma("gpsimd", lambda e: e.dma_start(out=gw_sb[:, :].rearrange("p (c n) -> p c n", c=8),
                                          in_=gatew_in.rearrange("(c p) n -> p c n", p=128)),
          writes=[rgw_sb], tag="f3_lgw")
    lg = ctx.sb("p3_lg", [128, 144], F32)
    rlg = ctx.r["p3_lg"]
    for t in range(16):
        def lmm(e, t=t):
            inst = None
            for c in range(8):
                inst = e.matmul(paux[:, 0:9], hT3[:, c * 2048 + t * 128: c * 2048 + (t + 1) * 128],
                                gw_sb[:, c * 9:(c + 1) * 9], start=(c == 0), stop=(c == 7))
            return inst
        p.op("tensor", lmm, reads=[rhT3, rgw_sb], writes=[rpaux], tag=f"f3_lm{t}")
        p.op("vector", lambda e, t=t: e.tensor_copy(lg[:, t * 9:(t + 1) * 9], paux[:, 0:9]),
             reads=[rpaux], writes=[rlg], tag=f"f3_le{t}")

    def dynv(tsr, off=0):
        return bass.AP(tsr, off, [[144, 128], [9, 16], [1, 8]])
    def s128(nm):
        h = ctx.sb(nm, [128, 128], F32)
        return h, ctx.r[nm]
    ms, rms = s128("p3_ms")
    absS, rabsS = s128("p3_absS")
    probs, rprobs = s128("p3_probs")
    ex, rex = s128("p3_ex")
    eqs, reqs = s128("p3_eqs")
    scr, rscr = s128("p3_scr")
    scr2, rscr2 = s128("p3_scr2")
    rw_t, rrw = s128("p3_rw")
    iot, riot = s128("p3_iot")
    c16 = lambda nm: (ctx.sb(nm, [128, 16], F32), ctx.r[nm])
    thr, rthr = c16("p3_thr")
    selv, rselv = c16("p3_selv")
    lsum, rlsum = c16("p3_lsum")
    csum, rcsum = c16("p3_csum")
    dynk, rdynk = c16("p3_dynk")
    t16, rt16 = c16("p3_t16")
    mults = ctx.sb("p3_mults", [128, 64], F32)
    rmults = ctx.r["p3_mults"]
    selvs = ctx.sb("p3_selvs", [128, 64], F32)
    rselvs = ctx.r["p3_selvs"]

    def v3(tsr, n=8):
        return bass.AP(tsr, 0, [[128, 128], [8, 16], [1, n]])
    def b16(tsr):
        return bass.AP(tsr, 0, [[16, 128], [1, 16], [0, 8]])

    p.op("gpsimd", lambda e: e.iota(iot[:, :].rearrange("p (t e) -> p t e", t=16), [[0, 16], [1, 8]], channel_multiplier=0,
                                    allow_small_or_imprecise_dtypes=True),
         writes=[riot], tag="f3_iota")
    p.op("vector", lambda e: e.tensor_copy(v3(ms), dynv(lg)), reads=[rlg], writes=[rms], tag="f3_mscp")
    p.op("scalar", lambda e: e.activation(absS[:, :], ms[:, :], AF.Abs), reads=[rms], writes=[rabsS], tag="f3_abs")
    p.op("vector", lambda e: e.tensor_reduce(thr[:, :], v3(ms), AX.X, ALU.max), reads=[rms], writes=[rthr], tag="f3_pm")
    p.op("vector", lambda e: e.tensor_tensor(v3(scr), v3(ms), b16(thr), ALU.subtract), reads=[rms, rthr], writes=[rscr], tag="f3_ps")
    p.op("scalar", lambda e: e.activation(probs[:, :], scr[:, :], AF.Exp), reads=[rscr], writes=[rprobs], tag="f3_pe")
    p.op("vector", lambda e: e.tensor_reduce(lsum[:, :], v3(probs), AX.X, ALU.add), reads=[rprobs], writes=[rlsum], tag="f3_pl")
    p.op("vector", lambda e: e.reciprocal(t16[:, :], lsum[:, :]), reads=[rlsum], writes=[rt16], tag="f3_pr")
    p.op("vector", lambda e: e.tensor_tensor(v3(probs), v3(probs), b16(t16), ALU.mult), reads=[rprobs, rt16], writes=[rprobs], tag="f3_pn")
    p.op("gpsimd", lambda e: e.memset(rw_t[:, :], 0.0), writes=[rrw], tag="f3_rw0")
    p.op("gpsimd", lambda e: e.memset(csum[:, :], 0.0), writes=[rcsum], tag="f3_cs0")
    p.op("gpsimd", lambda e: e.memset(dynk[:, :], 1.0), writes=[rdynk], tag="f3_dk0")
    for it in range(4):
        p.op("vector", lambda e: e.tensor_reduce(thr[:, :], v3(ms), AX.X, ALU.max), reads=[rms], writes=[rthr], tag=f"f3i{it}_max")
        p.op("vector", lambda e: e.tensor_tensor(v3(eqs), v3(ms), b16(thr), ALU.is_ge), reads=[rms, rthr], writes=[reqs], tag=f"f3i{it}_eq")
        p.op("vector", lambda e: e.scalar_tensor_tensor(scr[:, :], iot[:, :], 9.0, eqs[:, :], ALU.subtract, ALU.mult),
             reads=[riot, reqs], writes=[rscr], tag=f"f3i{it}_s1")
        p.op("vector", lambda e: e.tensor_scalar_add(scr2[:, :], scr[:, :], 9.0), reads=[rscr], writes=[rscr2], tag=f"f3i{it}_s2")
        p.op("vector", lambda e: e.tensor_reduce(selv[:, :], v3(scr2), AX.X, ALU.min), reads=[rscr2], writes=[rselv], tag=f"f3i{it}_am")
        p.op("vector", lambda e, it=it: e.tensor_copy(bass.AP(selvs, it, [[64, 128], [4, 16]]), selv[:, :]),
             reads=[rselv], writes=[rselvs], tag=f"f3i{it}_ss")
        p.op("scalar", lambda e: e.activation(t16[:, :], thr[:, :], AF.Abs), reads=[rthr], writes=[rt16], tag=f"f3i{it}_at")
        p.op("vector", lambda e: e.tensor_tensor(v3(scr), v3(absS), b16(t16), ALU.max), reads=[rabsS, rt16], writes=[rscr], tag=f"f3i{it}_fc")
        p.op("vector", lambda e: e.tensor_tensor(v3(scr2), b16(thr), dynv(lg), ALU.subtract), reads=[rthr, rlg], writes=[rscr2], tag=f"f3i{it}_d1")
        p.op("vector", lambda e: e.scalar_tensor_tensor(scr[:, :], scr[:, :], -0.2, scr2[:, :], ALU.mult, ALU.add),
             reads=[rscr, rscr2], writes=[rscr], tag=f"f3i{it}_d2")
        p.op("vector", lambda e: e.tensor_scalar(scr[:, :], scr[:, :], 0.0, None, ALU.is_gt), reads=[rscr], writes=[rscr], tag=f"f3i{it}_jm")
        p.op("vector", lambda e: e.scalar_tensor_tensor(scr2[:, :], scr[:, :], -1e30, ms[:, :], ALU.mult, ALU.add),
             reads=[rscr, rms], writes=[rscr2], tag=f"f3i{it}_mk")
        p.op("vector", lambda e: e.tensor_reduce(t16[:, :], v3(scr2), AX.X, ALU.max), reads=[rscr2], writes=[rt16], tag=f"f3i{it}_gm")
        p.op("vector", lambda e: e.tensor_tensor(v3(scr2), v3(scr2), b16(t16), ALU.subtract), reads=[rscr2, rt16], writes=[rscr2], tag=f"f3i{it}_gs")
        p.op("scalar", lambda e: e.activation(ex[:, :], scr2[:, :], AF.Exp), reads=[rscr2], writes=[rex], tag=f"f3i{it}_ge")
        p.op("vector", lambda e: e.tensor_reduce(lsum[:, :], v3(ex), AX.X, ALU.add), reads=[rex], writes=[rlsum], tag=f"f3i{it}_gl")
        p.op("vector", lambda e: e.tensor_tensor(v3(eqs), v3(iot), b16(selv), ALU.is_equal), reads=[riot, rselv], writes=[reqs], tag=f"f3i{it}_es")
        p.op("vector", lambda e: e.tensor_tensor(v3(scr), v3(ex), v3(eqs), ALU.mult), reads=[rex, reqs], writes=[rscr], tag=f"f3i{it}_gn")
        p.op("vector", lambda e: e.tensor_reduce(t16[:, :], v3(scr), AX.X, ALU.add), reads=[rscr], writes=[rt16], tag=f"f3i{it}_g2")
        p.op("vector", lambda e: e.reciprocal(lsum[:, :], lsum[:, :]), reads=[rlsum], writes=[rlsum], tag=f"f3i{it}_gr")
        p.op("vector", lambda e: e.tensor_mul(t16[:, :], t16[:, :], lsum[:, :]), reads=[rt16, rlsum], writes=[rt16], tag=f"f3i{it}_mu")
        p.op("vector", lambda e, it=it: e.tensor_copy(bass.AP(mults, it, [[64, 128], [4, 16]]), t16[:, :]),
             reads=[rt16], writes=[rmults], tag=f"f3i{it}_ms")
        p.op("vector", lambda e: e.tensor_tensor(v3(scr), v3(probs), v3(eqs), ALU.mult), reads=[rprobs, reqs], writes=[rscr], tag=f"f3i{it}_pi")
        p.op("vector", lambda e: e.tensor_reduce(t16[:, :], v3(scr), AX.X, ALU.add), reads=[rscr], writes=[rt16], tag=f"f3i{it}_p2")
        p.op("vector", lambda e: e.tensor_add(csum[:, :], csum[:, :], t16[:, :]), reads=[rcsum, rt16], writes=[rcsum], tag=f"f3i{it}_cs")
        if it < 3:
            p.op("vector", lambda e: e.tensor_scalar(t16[:, :], csum[:, :], 0.75, None, ALU.is_lt), reads=[rcsum], writes=[rt16], tag=f"f3i{it}_ck")
            p.op("vector", lambda e: e.tensor_add(dynk[:, :], dynk[:, :], t16[:, :]), reads=[rdynk, rt16], writes=[rdynk], tag=f"f3i{it}_dk")
        p.op("vector", lambda e: e.scalar_tensor_tensor(ms[:, :], eqs[:, :], -1e30, ms[:, :], ALU.mult, ALU.add),
             reads=[reqs, rms], writes=[rms], tag=f"f3i{it}_mu2")
    for it in range(4):
        p.op("vector", lambda e, it=it: e.tensor_scalar(t16[:, :], dynk[:, :], float(-it), 0.0, ALU.add, ALU.is_gt),
             reads=[rdynk], writes=[rt16], tag=f"f3r{it}_v")
        p.op("vector", lambda e, it=it: e.tensor_mul(t16[:, :], t16[:, :], bass.AP(mults, it, [[64, 128], [4, 16]])),
             reads=[rt16, rmults], writes=[rt16], tag=f"f3r{it}_m")
        p.op("vector", lambda e, it=it: e.tensor_tensor(v3(eqs), v3(iot), bass.AP(selvs, it, [[64, 128], [4, 16], [0, 8]]), ALU.is_equal),
             reads=[riot, rselvs], writes=[reqs], tag=f"f3r{it}_e")
        p.op("vector", lambda e: e.tensor_tensor(v3(scr), v3(eqs), b16(t16), ALU.mult), reads=[reqs, rt16], writes=[rscr], tag=f"f3r{it}_x")
        p.op("vector", lambda e: e.tensor_add(rw_t[:, :], rw_t[:, :], scr[:, :]), reads=[rrw, rscr], writes=[rrw], tag=f"f3r{it}_a")
    p.op("vector", lambda e: e.tensor_reduce(t16[:, :], v3(rw_t), AX.X, ALU.add), reads=[rrw], writes=[rt16], tag="f3_rs")
    p.op("vector", lambda e: e.reciprocal(t16[:, :], t16[:, :]), reads=[rt16], writes=[rt16], tag="f3_rr")
    p.op("vector", lambda e: e.tensor_tensor(v3(rw_t), v3(rw_t), b16(t16), ALU.mult), reads=[rrw, rt16], writes=[rrw], tag="f3_rn")
    cnd, rcnd = s128("p3_cnd")
    p.op("vector", lambda e: e.tensor_scalar(cnd[:, :], rw_t[:, :], 0.0, None, ALU.is_gt), reads=[rrw], writes=[rcnd], tag="f3_cand")
    cndb = ctx.sb("p3_cndb", [128, 128], BF16)
    rcndb = ctx.r["p3_cndb"]
    p.op("vector", lambda e: e.tensor_copy(cndb[:, :], cnd[:, :]), reads=[rcnd], writes=[rcndb], tag="f3_cndb")
    p.op("tensor", lambda e: e.matmul(paux[0:1, 0:128], ctx.t["c_ones"][:, 0:1], cnd[:, :], start=True, stop=True),
         reads=[rcnd, ctx.r["c_ones"]], writes=[rpaux], tag="f3_cm")
    cnt_sb = ctx.sb("p3_cnt", [1, 8], F32)
    rcnt = ctx.r["p3_cnt"]
    p.op("vector", lambda e: e.tensor_reduce(cnt_sb[:, :], bass.AP(paux, 0, [[512, 1], [1, 8], [8, 16]]), AX.X, ALU.add),
         reads=[rpaux], writes=[rcnt], tag="f3_cr")
    p.dma("sync", lambda e: e.dma_start(out=cnt_ai, in_=cnt_sb[:, :]), reads=[rcnt], writes=[ctx.r["cnt_ai"]], tag="f3_cw")
    p.op("gpsimd", lambda e: e.collective_compute(
        "AllGather", ALU.bypass, replica_groups=[list(range(8))],
        ins=[cnt_ai], outs=[cnt_ao]), reads=[ctx.r["cnt_ai"]], writes=[ctx.r["cnt_ao"]],
        collective=True, tag="f_ag_cnt")
    cnts_sb = ctx.sb("p3_cnts", [8, 8], F32)
    rcnts = ctx.r["p3_cnts"]
    pmask_sb = ctx.sb("p3_pm", [8, 1], F32)
    rpmask = ctx.r["p3_pm"]
    p.dma("sync", lambda e: e.dma_start(out=cnts_sb[:, :], in_=cnt_ao), reads=[ctx.r["cnt_ao"]], writes=[rcnts], tag="f3_lcn")
    p.dma("sync", lambda e: e.dma_start(out=pmask_sb[:, :], in_=pmask_in), writes=[rpmask], tag="f3_lpm")
    carry = ctx.sb("p3_carry", [1, 8], F32)
    rcarry = ctx.r["p3_carry"]
    p.op("tensor", lambda e: e.matmul(paux[0:1, 0:8], pmask_sb[:, :], cnts_sb[:, :], start=True, stop=True),
         reads=[rpmask, rcnts], writes=[rpaux], tag="f3_om2")
    p.op("vector", lambda e: e.tensor_copy(carry[:, :], paux[0:1, 0:8]), reads=[rpaux], writes=[rcarry], tag="f3_oe")
    ut = ctx.sb("p3_ut", [128, 128], BF16)
    rut = ctx.r["p3_ut"]
    p.op("gpsimd", lambda e: e.memset(ut[:, :], 1.0), writes=[rut], tag="f3_ut0")
    p.op("gpsimd", lambda e: e.affine_select(out=ut[:, :], in_=ut[:, :], compare_op=ALU.is_ge,
                                             fill=0.0, base=0, pattern=[[1, 128]], channel_multiplier=-1),
         reads=[rut], writes=[rut], tag="f3_ut1")
    kept, rkept = s128("p3_kept")
    carryb = ctx.sb("p3_cb", [128, 8], F32)
    rcarryb = ctx.r["p3_cb"]
    ttot = ctx.sb("p3_tt", [1, 8], F32)
    rttot = ctx.r["p3_tt"]
    cum = ctx.sb("p3_cum", [128, 8], F32)
    rcum = ctx.r["p3_cum"]
    for t in range(16):
        p.op("tensor", lambda e: e.matmul(paux[:, 8:16], ctx.t["c_ones"][0:1, :], carry[:, :], start=True, stop=True),
             reads=[rcarry, ctx.r["c_ones"]], writes=[rpaux], tag=f"f3cp_cb{t}")
        p.op("vector", lambda e: e.tensor_copy(carryb[:, :], paux[:, 8:16]), reads=[rpaux], writes=[rcarryb], tag=f"f3cp_ce{t}")
        p.op("tensor", lambda e, t=t: e.matmul(paux[:, 0:8], ut[:, :], cndb[:, t * 8:(t + 1) * 8], start=True, stop=True),
             reads=[rut, rcndb], writes=[rpaux], tag=f"f3cp_mm{t}")
        p.op("vector", lambda e: e.tensor_add(cum[:, :], paux[:, 0:8], carryb[:, :]), reads=[rpaux, rcarryb], writes=[rcum], tag=f"f3cp_cu{t}")
        p.op("vector", lambda e: e.tensor_scalar(cum[:, :], cum[:, :], 2560.0, None, ALU.is_le), reads=[rcum], writes=[rcum], tag=f"f3cp_le{t}")
        p.op("vector", lambda e, t=t: e.tensor_tensor(kept[:, t * 8:(t + 1) * 8], cum[:, :], cnd[:, t * 8:(t + 1) * 8], ALU.mult),
             reads=[rcum, rcnd], writes=[rkept], tag=f"f3cp_k{t}")
        p.op("tensor", lambda e, t=t: e.matmul(paux[0:1, 0:8], ctx.t["c_ones_b"][:, 0:1], cndb[:, t * 8:(t + 1) * 8], start=True, stop=True),
             reads=[rcndb, ctx.r["c_ones_b"]], writes=[rpaux], tag=f"f3cp_tt{t}")
        p.op("vector", lambda e: e.tensor_copy(ttot[:, :], paux[0:1, 0:8]), reads=[rpaux], writes=[rttot], tag=f"f3cp_te{t}")
        p.op("vector", lambda e: e.tensor_add(carry[:, :], carry[:, :], ttot[:, :]), reads=[rcarry, rttot], writes=[rcarry], tag=f"f3cp_ca{t}")
    msk9 = ctx.sb("p3_msk9", [128, 144], F32)
    rmsk9 = ctx.r["p3_msk9"]
    p.op("gpsimd", lambda e: e.memset(msk9[:, :], 1.0), writes=[rmsk9], tag="f3_m9")
    p.op("vector", lambda e: e.tensor_copy(bass.AP(msk9, 0, [[144, 128], [9, 16], [1, 8]]),
                                           bass.AP(kept, 0, [[128, 128], [8, 16], [1, 8]])),
         reads=[rkept], writes=[rmsk9], tag="f3_m9c")
    t16b = ctx.sb("p3_t16b", [128, 16], F32)
    rt16b = ctx.r["p3_t16b"]

    def v9(tsr):
        return bass.AP(tsr, 0, [[144, 128], [9, 16], [1, 9]])
    def b169(tsr):
        return bass.AP(tsr, 0, [[16, 128], [1, 16], [0, 9]])
    def v8g(tsr):
        return bass.AP(tsr, 0, [[144, 128], [9, 16], [1, 8]])
    p.op("vector", lambda e: e.tensor_scalar(glw[:, :], msk9[:, :], -1.0, 1e30, ALU.add, ALU.mult),
         reads=[rmsk9], writes=[rglw], tag="f3g0")
    p.op("vector", lambda e: e.tensor_add(glw[:, :], glw[:, :], lg[:, :]), reads=[rglw, rlg], writes=[rglw], tag="f3g1")
    p.op("vector", lambda e: e.tensor_reduce(t16b[:, :], v9(glw), AX.X, ALU.max), reads=[rglw], writes=[rt16b], tag="f3g2")
    p.op("vector", lambda e: e.tensor_tensor(v9(glw), v9(glw), b169(t16b), ALU.subtract), reads=[rglw, rt16b], writes=[rglw], tag="f3g3")
    p.op("scalar", lambda e: e.activation(glw[:, :], glw[:, :], AF.Exp), reads=[rglw], writes=[rglw], tag="f3g4")
    p.op("vector", lambda e: e.tensor_reduce(t16b[:, :], v9(glw), AX.X, ALU.add), reads=[rglw], writes=[rt16b], tag="f3g5")
    p.op("vector", lambda e: e.reciprocal(t16b[:, :], t16b[:, :]), reads=[rt16b], writes=[rt16b], tag="f3g6")
    p.op("vector", lambda e: e.tensor_tensor(v9(glw), v9(glw), b169(t16b), ALU.mult), reads=[rglw, rt16b], writes=[rglw], tag="f3g7")
    p.op("vector", lambda e: e.tensor_reduce(t16b[:, :], v8g(glw), AX.X, ALU.add), reads=[rglw], writes=[rt16b], tag="f3g8")
    p.op("vector", lambda e: e.tensor_tensor(v8g(glw), bass.AP(rw_t, 0, [[128, 128], [8, 16], [1, 8]]),
                                             bass.AP(t16b, 0, [[16, 128], [1, 16], [0, 8]]), ALU.mult),
         reads=[rrw, rt16b], writes=[rglw], tag="f3g9")
    p.op("vector", lambda e: e.tensor_tensor(v8g(glw), v8g(glw), bass.AP(kept, 0, [[128, 128], [8, 16], [1, 8]]), ALU.mult),
         reads=[rglw, rkept], writes=[rglw], tag="f3g10")

    if FUSED_STAGE == 26:
        for nm2, hh2, rr2, w2 in (("dbg_lg", lg, rlg, 144), ("dbg_rw", rw_t, rrw, 128),
                                  ("dbg_kept", kept, rkept, 128), ("dbg_glw", glw, rglw, 144)):
            dbo = ctx.out(nm2, [128, w2])
            p.dma("sync", lambda e, dbo=dbo, hh2=hh2: e.dma_start(out=dbo, in_=hh2[:, :]),
                  reads=[rr2], writes=[ctx.r["out:" + nm2]], tag=nm2)
        p.build(nc, verbose=True)
        ctx.close()
        nc.reset()
        return ctx

    ctx.phase(O3A)
    gbuf = ctx.sb("p3_gbuf", [128, 8 * 512], BF16)
    rgbuf = ctx.r["p3_gbuf"]
    ubuf = ctx.sb("p3_ubuf", [128, 8 * 512], BF16)
    rubuf = ctx.r["p3_ubuf"]
    dbuf = ctx.sb("p3_dbuf", [128, 4 * 1024], BF16)
    rdbuf = ctx.r["p3_dbuf"]
    act = ctx.sb("p3_act", [128, 4 * 2048], BF16)
    ract = ctx.r["p3_act"]
    sgt = ctx.sb("p3_sgt", [128, 512], BF16)
    rsgt = ctx.r["p3_sgt"]
    pmm3 = [ctx.t[f"pmm{i}"] for i in range(4)]
    rpmm3 = [ctx.r[f"pmm{i}"] for i in range(4)]
    iters = []
    for e_ in range(8):
        iters.append(dict(I=512, gsrc=dg_in[e_], usrc=du_in[e_], dsrc=dd_in[e_], gcol=e_, nm=f"fd{e_}"))
    for hf_ in range(4):
        iters.append(dict(I=512, gsrc=sg_in[:, hf_ * 512:(hf_ + 1) * 512],
                          usrc=su_in[:, hf_ * 512:(hf_ + 1) * 512],
                          dsrc=sd_in[hf_ * 512:(hf_ + 1) * 512, :], gcol=8, nm=f"fs{hf_}"))
    for itr in iters:
        I, nm = itr["I"], itr["nm"]
        nj = I // 128
        for c in range(8):
            p.dma("gpsimd", lambda e, c=c, itr=itr, I=I: e.dma_start(
                out=gbuf[:, c * I:(c + 1) * I], in_=itr["gsrc"][c * 128:(c + 1) * 128, :]),
                writes=[rgbuf], tag=f"{nm}_lg{c}")
            p.dma("gpsimd", lambda e, c=c, itr=itr, I=I: e.dma_start(
                out=ubuf[:, c * I:(c + 1) * I], in_=itr["usrc"][c * 128:(c + 1) * 128, :]),
                writes=[rubuf], tag=f"{nm}_lu{c}")
        for j in range(nj):
            p.dma("gpsimd", lambda e, j=j, itr=itr: e.dma_start(
                out=dbuf[:, j * 1024:(j + 1) * 1024], in_=itr["dsrc"][j * 128:(j + 1) * 128, :]),
                writes=[rdbuf], tag=f"{nm}_ld{j}")
            p.op("vector", lambda e, j=j: e.tensor_mul(dbuf[:, j * 1024:(j + 1) * 1024],
                                                       dbuf[:, j * 1024:(j + 1) * 1024], modt["gb"][0][:, :]),
                 reads=[rdbuf, modt["gb"][1]], writes=[rdbuf], tag=f"{nm}_fd{j}")
        for j in range(nj):
            for tg in range(4):
                bg, bu = tg % 2, 2 + tg % 2

                def gmm(e, j=j, tg=tg, bg=bg, I=I):
                    inst = None
                    for c in range(8):
                        inst = e.matmul(pmm3[bg][:, :], gbuf[:, c * I + j * 128: c * I + (j + 1) * 128],
                                        hT3[:, c * 2048 + tg * 512: c * 2048 + (tg + 1) * 512],
                                        start=(c == 0), stop=(c == 7))
                    return inst
                p.op("tensor", gmm, reads=[rgbuf, rhT3], writes=[rpmm3[bg]], tag=f"{nm}_gm{j}_{tg}")

                def umm(e, j=j, tg=tg, bu=bu, I=I):
                    inst = None
                    for c in range(8):
                        inst = e.matmul(pmm3[bu][:, :], ubuf[:, c * I + j * 128: c * I + (j + 1) * 128],
                                        hT3[:, c * 2048 + tg * 512: c * 2048 + (tg + 1) * 512],
                                        start=(c == 0), stop=(c == 7))
                    return inst
                p.op("tensor", umm, reads=[rubuf, rhT3], writes=[rpmm3[bu]], tag=f"{nm}_um{j}_{tg}")
                p.op("scalar", lambda e, bg=bg: e.activation(sgt[:, :], pmm3[bg][:, :], AF.Silu),
                     reads=[rpmm3[bg]], writes=[rsgt], tag=f"{nm}_si{j}_{tg}")
                p.op("vector", lambda e, j=j, tg=tg, bu=bu: e.tensor_tensor(
                    act[:, j * 2048 + tg * 512: j * 2048 + (tg + 1) * 512], sgt[:, :], pmm3[bu][:, :], ALU.mult),
                    reads=[rsgt, rpmm3[bu]], writes=[ract], tag=f"{nm}_ac{j}_{tg}")
        gcol = itr["gcol"]
        for t in range(16):
            def dmm(e, t=t, nj=nj):
                inst = None
                for hf in range(2):
                    for j in range(nj):
                        inst = e.matmul(pwide[:, hf * 512:(hf + 1) * 512],
                                        act[:, j * 2048 + t * 128: j * 2048 + (t + 1) * 128],
                                        dbuf[:, j * 1024 + hf * 512: j * 1024 + (hf + 1) * 512],
                                        start=(j == 0), stop=(j == nj - 1))
                return inst
            p.op("tensor", dmm, reads=[ract, rdbuf], writes=[rpwide], tag=f"{nm}_dm{t}")
            for hf in range(2):
                p.op("vector", lambda e, t=t, hf=hf, gcol=gcol: e.scalar_tensor_tensor(
                    fx[:, t * 1024 + hf * 512: t * 1024 + (hf + 1) * 512],
                    pwide[:, hf * 512:(hf + 1) * 512], glw[:, t * 9 + gcol: t * 9 + gcol + 1],
                    fx[:, t * 1024 + hf * 512: t * 1024 + (hf + 1) * 512], ALU.mult, ALU.add),
                    reads=[rpwide, rglw, rfx], writes=[rfx], tag=f"{nm}_cb{t}_{hf}")

    if FUSED_STAGE == 3:
        dbx = ctx.out("dbg_x2", [2048, 1024])
        for t in range(16):
            p.dma("sync", lambda e, t=t: e.dma_start(out=dbx[t * 128:(t + 1) * 128, :], in_=fx[:, t * 1024:(t + 1) * 1024]),
                  reads=[rfx], writes=[ctx.r["out:dbg_x2"]], tag=f"dbgx2_{t}")
        p.build(nc, verbose=True)
        ctx.close()
        nc.reset()
        return ctx

    x2_d = ctx.dram("x2_d", [2048, 1024], F32)
    rx2d = ctx.r["x2_d"]
    for t in range(16):
        p.dma("sync", lambda e, t=t: e.dma_start(out=x2_d[t * 128:(t + 1) * 128, :], in_=fx[:, t * 1024:(t + 1) * 1024]),
              reads=[rfx], writes=[rx2d], tag=f"f4_sx{t}")
    ctx.phase(XO)
    wstA4 = ctx.sb("p4_wA", [128, 1024], F32)
    wstB4 = ctx.sb("p4_wB", [128, 1024], F32)
    wst4 = [(wstA4, ctx.r["p4_wA"]), (wstB4, ctx.r["p4_wB"])]
    emit_mod(ctx, c1_in, wmod1_in, "fm1", wst=wst4, dst=modt)
    ctx.phase(0)
    sflag_sb = ctx.sb("p4_sflag", [128, 1], F32)
    rsflag = ctx.r["p4_sflag"]
    p.dma("sync", lambda e: e.dma_start(out=sflag_sb[:, :], in_=sflag_in), writes=[rsflag], tag="f4_lsf")

    def hin0(state, rstate):
        p.op("gpsimd", lambda e: e.memset(state[:, :], 0.0), writes=[rstate], tag="f4_st0")
    emit_mamba(ctx, x2_d, hin0, modt, win_in, winpad_in, dtw_in, dtb_in, alog_in,
               dpar_in, False, hfin_out=st_ai, rx3out=ctx.r["st_ai"], x_dep=rx2d)
    p.op("gpsimd", lambda e: e.collective_compute(
        "AllGather", ALU.bypass, replica_groups=[[0, 1], [2, 3], [4, 5], [6, 7]],
        ins=[st_ai], outs=[st_ao]), reads=[ctx.r["st_ai"]], writes=[ctx.r["st_ao"]],
        collective=True, tag="f_ag_st")

    def hin1(state, rstate):
        p.dma("sync", lambda e: e.dma_start(out=state[:, :], in_=st_ao[0:128, :]),
              reads=[ctx.r["st_ao"]], writes=[rstate], tag="f4_lst")
        p.op("vector", lambda e: e.tensor_scalar(state[:, :], state[:, :], sflag_sb[:, 0:1], None, ALU.mult),
             reads=[rstate, rsflag], writes=[rstate], tag="f4_stf")
    emit_mamba(ctx, x2_d, hin1, modt, win_in, winpad_in, dtw_in, dtb_in, alog_in,
               dpar_in, True, normw_in, wout_in, x3_d, rx3out=ctx.r["x3_d"], x_dep=rx2d)
    rx3d = ctx.r["x3_d"]

    if FUSED_STAGE == 4:
        dbx = ctx.out("dbg_x3", [2048, 1024])
        st3 = ctx.sb("p4_st3", [128, 1024], F32)
        rst3 = ctx.r["p4_st3"]
        for t in range(16):
            p.dma("sync", lambda e, t=t: e.dma_start(out=st3[:, :], in_=x3_d[t * 128:(t + 1) * 128, :]),
                  reads=[rx3d], writes=[rst3], tag=f"dbgx3l_{t}")
            p.dma("sync", lambda e, t=t: e.dma_start(out=dbx[t * 128:(t + 1) * 128, :], in_=st3[:, :]),
                  reads=[rst3], writes=[ctx.r["out:dbg_x3"]], tag=f"dbgx3s_{t}")
        p.build(nc, verbose=True)
        ctx.close()
        nc.reset()
        return ctx

    ctx.phase(XO)
    amax = ctx.sb("p5_am", [128, 1], F32)
    ramax = ctx.r["p5_am"]
    tcol = ctx.sb("p5_tc", [128, 1], F32)
    rtcol = ctx.r["p5_tc"]
    p.op("gpsimd", lambda e: e.memset(amax[:, :], 0.0), writes=[ramax], tag="f5_am0")

    def cb5(t, h3, rh3):
        p.op("vector", lambda e, h3=h3: e.tensor_reduce(tcol[:, :], h3, AX.X, ALU.max, apply_absolute_value=True),
             reads=[rh3], writes=[rtcol], tag=f"f5am_r{t}")
        p.op("vector", lambda e: e.tensor_max(amax[:, :], amax[:, :], tcol[:, :]), reads=[ramax, rtcol], writes=[ramax], tag=f"f5am_m{t}")
    emit_h3_tiles(ctx, "f5m1", x3_d, modt, cb5, x_dep=rx3d)
    p.op("tensor", lambda e: e.transpose(paux[0:1, 0:128], amax[:, :], ctx.t["c_ident"][:, :]),
         reads=[ramax, ctx.r["c_ident"]], writes=[rpaux], tag="f5_atr")
    mx = ctx.sb("p5_mx", [1, 1], F32)
    rmx = ctx.r["p5_mx"]
    p.op("vector", lambda e: e.tensor_reduce(mx[:, :], paux[0:1, 0:128], AX.X, ALU.max), reads=[rpaux], writes=[rmx], tag="f5_af")
    p.dma("sync", lambda e: e.dma_start(out=mx1_ai, in_=mx[:, :]), reads=[rmx], writes=[ctx.r["mx1_ai"]], tag="f5_aw")
    p.op("gpsimd", lambda e: e.collective_compute(
        "AllReduce", ALU.max, replica_groups=[list(range(8))],
        ins=[mx1_ai], outs=[mx1_ao]), reads=[ctx.r["mx1_ai"]], writes=[ctx.r["mx1_ao"]],
        collective=True, tag="f_ar_mx1")
    sxc = ctx.sb("p5_sx", [1, 1], F32)
    rsxc = ctx.r["p5_sx"]
    clip = ctx.sb("p5_cl", [1, 1], F32)
    rclip = ctx.r["p5_cl"]
    p.dma("sync", lambda e: e.dma_start(out=sxc[:, :], in_=mx1_ao), reads=[ctx.r["mx1_ao"]], writes=[rsxc], tag="f5_lsx")
    p.dma("sync", lambda e: e.dma_start(out=clip[:, :], in_=clipx_in), writes=[rclip], tag="f5_lcl")
    p.op("scalar", lambda e: e.activation(clip[:, :], clip[:, :], AF.Sigmoid), reads=[rclip], writes=[rclip], tag="f5_sg")
    p.op("vector", lambda e: e.tensor_mul(sxc[:, :], sxc[:, :], clip[:, :]), reads=[rsxc, rclip], writes=[rsxc], tag="f5_m1")
    p.op("vector", lambda e: e.tensor_scalar_mul(sxc[:, :], sxc[:, :], 1.0 / 127.0), reads=[rsxc], writes=[rsxc], tag="f5_m2")
    rqx = ctx.sb("p5_rq", [1, 1], F32)
    rrqx = ctx.r["p5_rq"]
    p.op("vector", lambda e: e.reciprocal(rqx[:, :], sxc[:, :]), reads=[rsxc], writes=[rrqx], tag="f5_rc")

    def bc5(e):
        e.matmul(paux[:, 0:1], ctx.t["c_ones"][0:1, :], rqx[:, :], start=True, stop=True)
        return e.matmul(paux[:, 1:2], ctx.t["c_ones"][0:1, :], sxc[:, :], start=True, stop=True)
    p.op("tensor", bc5, reads=[rrqx, rsxc, ctx.r["c_ones"]], writes=[rpaux], tag="f5_bc")
    scx = ctx.sb("p5_scx", [128, 2], F32)
    rscx = ctx.r["p5_scx"]
    p.op("vector", lambda e: e.tensor_copy(scx[:, :], paux[:, 0:2]), reads=[rpaux], writes=[rscx], tag="f5_be")

    qxT = ctx.sb("p5_qxT", [128, 8 * 2048], BF16)
    rqxT = ctx.r["p5_qxT"]
    qt_t = ctx.sb("p5_qt", [128, 1024], F32)
    rqt5 = ctx.r["p5_qt"]
    qb_t = ctx.sb("p5_qb", [128, 1024], BF16)
    rqb5 = ctx.r["p5_qb"]

    def cb5b(t, h3, rh3):
        p.op("vector", lambda e, h3=h3: e.tensor_scalar(qt_t[:, :], h3, scx[:, 0:1], RND_C, ALU.mult, ALU.add),
             reads=[rh3, rscx], writes=[rqt5], tag=f"f5qx1_{t}")
        p.op("vector", lambda e: e.tensor_scalar(qt_t[:, :], qt_t[:, :], RND_C, 127.0, ALU.subtract, ALU.min),
             reads=[rqt5], writes=[rqt5], tag=f"f5qx2_{t}")
        p.op("vector", lambda e: e.tensor_scalar_max(qb_t[:, :], qt_t[:, :], -128.0), reads=[rqt5], writes=[rqb5], tag=f"f5qx3_{t}")

        def trs(e):
            inst = None
            for j in range(8):
                inst = e.transpose(ptrb[:, j * 128:(j + 1) * 128], qb_t[:, j * 128:(j + 1) * 128], ctx.t["c_ident_b"][:, :])
            return inst
        p.op("tensor", trs, reads=[rqb5, ctx.r["c_ident_b"]], writes=[rptrb], tag=f"f5qxt_{t}")
        p.op("vector", lambda e, t=t: e.tensor_copy(
            bass.AP(qxT, t * 128, [[8 * 2048, 128], [2048, 8], [1, 128]]),
            ptrb[:, :].rearrange("p (j q) -> p j q", j=8)),
            reads=[rptrb], writes=[rqxT], tag=f"f5qxe_{t}")
    emit_h3_tiles(ctx, "f5gu", x3_d, modt, cb5b, x_dep=rx3d)

    wq5 = [ctx.sb(f"p5_wq{i}", [128, 1024], F32) for i in range(2)]
    rwq5 = [ctx.r[f"p5_wq{i}"] for i in range(2)]
    sw5 = ctx.sb("p5_sw", [128, 2], F32)
    rsw5 = ctx.r["p5_sw"]
    swq5 = ctx.sb("p5_swq", [128, 2], F32)
    rswq5 = ctx.r["p5_swq"]
    gq5 = ctx.sb("p5_gq", [128, 2 * 1024], BF16)
    rgq5 = ctx.r["p5_gq"]
    hbuf5 = ctx.sb("p5_hbuf", [128, 2048], BF16)
    rhbuf5 = ctx.r["p5_hbuf"]
    sgt5 = ctx.sb("p5_sgt", [128, 512], F32)
    rsgt5 = ctx.r["p5_sgt"]
    amax2 = ctx.sb("p5_am2", [128, 1], F32)
    ramax2 = ctx.r["p5_am2"]
    p.op("gpsimd", lambda e: e.memset(amax2[:, :], 0.0), writes=[ramax2], tag="f5_am20")
    pmm3b = [ctx.t[f"pmm{i}"] for i in range(4)]
    rpmm3b = [ctx.r[f"pmm{i}"] for i in range(4)]
    for ic in range(22):
        for wi, w_in5 in enumerate((wg_in, wu_in)):
            b = wi
            p.dma("sync", lambda e, ic=ic, b=b, w_in5=w_in5: e.dma_start(out=wq5[b][:, :], in_=w_in5[ic * 128:(ic + 1) * 128, :]),
                  writes=[rwq5[b]], tag=f"f5g{ic}_l{wi}")
            p.op("vector", lambda e, b=b, wi=wi: e.tensor_reduce(sw5[:, wi:wi + 1], wq5[b][:, :], AX.X, ALU.max, apply_absolute_value=True),
                 reads=[rwq5[b]], writes=[rsw5], tag=f"f5g{ic}_r{wi}")
            p.op("vector", lambda e, wi=wi: e.tensor_scalar_mul(sw5[:, wi:wi + 1], sw5[:, wi:wi + 1], 1.0 / 127.0),
                 reads=[rsw5], writes=[rsw5], tag=f"f5g{ic}_s{wi}")
            p.op("vector", lambda e, wi=wi: e.reciprocal(swq5[:, wi:wi + 1], sw5[:, wi:wi + 1]),
                 reads=[rsw5], writes=[rswq5], tag=f"f5g{ic}_c{wi}")
            p.op("vector", lambda e, b=b, wi=wi: e.tensor_scalar(wq5[b][:, :], wq5[b][:, :], swq5[:, wi:wi + 1], RND_C, ALU.mult, ALU.add),
                 reads=[rwq5[b], rswq5], writes=[rwq5[b]], tag=f"f5g{ic}_q1{wi}")
            p.op("vector", lambda e, b=b: e.tensor_scalar(wq5[b][:, :], wq5[b][:, :], RND_C, 127.0, ALU.subtract, ALU.min),
                 reads=[rwq5[b]], writes=[rwq5[b]], tag=f"f5g{ic}_q2{wi}")
            p.op("vector", lambda e, b=b: e.tensor_scalar_max(qt_t[:, :], wq5[b][:, :], -128.0),
                 reads=[rwq5[b]], writes=[rqt5], tag=f"f5g{ic}_q3{wi}")
            p.op("vector", lambda e: e.tensor_copy(qb_t[:, :], qt_t[:, :]), reads=[rqt5], writes=[rqb5], tag=f"f5g{ic}_q4{wi}")

            def trw(e):
                inst = None
                for j in range(8):
                    inst = e.transpose(ptrb[:, j * 128:(j + 1) * 128], qb_t[:, j * 128:(j + 1) * 128], ctx.t["c_ident_b"][:, :])
                return inst
            p.op("tensor", trw, reads=[rqb5, ctx.r["c_ident_b"]], writes=[rptrb], tag=f"f5g{ic}_t{wi}")
            p.op("vector", lambda e, wi=wi: e.tensor_copy(gq5[:, wi * 1024:(wi + 1) * 1024], ptrb[:, :]),
                 reads=[rptrb], writes=[rgq5], tag=f"f5g{ic}_e{wi}")
        p.op("vector", lambda e: e.tensor_scalar(sw5[:, 0:2], sw5[:, 0:2], scx[:, 1:2], None, ALU.mult),
             reads=[rsw5, rscx], writes=[rsw5], tag=f"f5g{ic}_sv")
        for tg in range(4):
            bg, bu = tg % 2, 2 + tg % 2

            def gmm5(e, tg=tg, bg=bg):
                inst = None
                for c in range(8):
                    inst = e.matmul(pmm3b[bg][:, :], gq5[:, c * 128:(c + 1) * 128],
                                    qxT[:, c * 2048 + tg * 512: c * 2048 + tg * 512 + 512], start=(c == 0), stop=(c == 7))
                return inst
            p.op("tensor", gmm5, reads=[rgq5, rqxT], writes=[rpmm3b[bg]], tag=f"f5g{ic}_gm{tg}")

            def umm5(e, tg=tg, bu=bu):
                inst = None
                for c in range(8):
                    inst = e.matmul(pmm3b[bu][:, :], gq5[:, 1024 + c * 128: 1024 + (c + 1) * 128],
                                    qxT[:, c * 2048 + tg * 512: c * 2048 + tg * 512 + 512], start=(c == 0), stop=(c == 7))
                return inst
            p.op("tensor", umm5, reads=[rgq5, rqxT], writes=[rpmm3b[bu]], tag=f"f5g{ic}_um{tg}")
            p.op("scalar", lambda e, bg=bg: e.activation(sgt5[:, :], pmm3b[bg][:, :], AF.Silu, scale=sw5[:, 0:1]),
                 reads=[rpmm3b[bg], rsw5], writes=[rsgt5], tag=f"f5g{ic}_si{tg}")
            p.op("vector", lambda e, tg=tg, bu=bu: e.scalar_tensor_tensor(
                hbuf5[:, tg * 512:(tg + 1) * 512], pmm3b[bu][:, :], sw5[:, 1:2], sgt5[:, :], ALU.mult, ALU.mult),
                reads=[rpmm3b[bu], rsw5, rsgt5], writes=[rhbuf5], tag=f"f5g{ic}_hb{tg}")
        p.op("vector", lambda e: e.tensor_reduce(tcol[:, :], hbuf5[:, :], AX.X, ALU.max, apply_absolute_value=True),
             reads=[rhbuf5], writes=[rtcol], tag=f"f5g{ic}_am")
        p.op("vector", lambda e: e.tensor_max(amax2[:, :], amax2[:, :], tcol[:, :]), reads=[ramax2, rtcol], writes=[ramax2], tag=f"f5g{ic}_a2")
        p.dma("sync", lambda e, ic=ic: e.dma_start(out=hact_d[ic * 128:(ic + 1) * 128, :], in_=hbuf5[:, :]),
              reads=[rhbuf5], writes=[ctx.r["hact_d"]], tag=f"f5g{ic}_st")
    p.op("tensor", lambda e: e.transpose(paux[0:1, 0:128], amax2[:, :], ctx.t["c_ident"][:, :]),
         reads=[ramax2, ctx.r["c_ident"]], writes=[rpaux], tag="f5_a2t")
    p.op("vector", lambda e: e.tensor_reduce(mx[:, :], paux[0:1, 0:128], AX.X, ALU.max), reads=[rpaux], writes=[rmx], tag="f5_a2f")
    p.dma("sync", lambda e: e.dma_start(out=mx2_ai, in_=mx[:, :]), reads=[rmx], writes=[ctx.r["mx2_ai"]], tag="f5_a2w")
    p.op("gpsimd", lambda e: e.collective_compute(
        "AllReduce", ALU.max, replica_groups=[list(range(8))],
        ins=[mx2_ai], outs=[mx2_ao]), reads=[ctx.r["mx2_ai"]], writes=[ctx.r["mx2_ao"]],
        collective=True, tag="f_ar_mx2")

    ctx.phase(0)
    sdc = ctx.sb("p6_sd", [1, 1], F32)
    rsdc = ctx.r["p6_sd"]
    clip2 = ctx.sb("p6_cl", [1, 1], F32)
    rclip2 = ctx.r["p6_cl"]
    p.dma("sync", lambda e: e.dma_start(out=sdc[:, :], in_=mx2_ao), reads=[ctx.r["mx2_ao"]], writes=[rsdc], tag="f6_lsd")
    p.dma("sync", lambda e: e.dma_start(out=clip2[:, :], in_=clipd_in), writes=[rclip2], tag="f6_lcl")
    p.op("scalar", lambda e: e.activation(clip2[:, :], clip2[:, :], AF.Sigmoid), reads=[rclip2], writes=[rclip2], tag="f6_sg")
    p.op("vector", lambda e: e.tensor_mul(sdc[:, :], sdc[:, :], clip2[:, :]), reads=[rsdc, rclip2], writes=[rsdc], tag="f6_m1")
    p.op("vector", lambda e: e.tensor_scalar_mul(sdc[:, :], sdc[:, :], 1.0 / 127.0), reads=[rsdc], writes=[rsdc], tag="f6_m2")
    rqd = ctx.sb("p6_rq", [1, 1], F32)
    rrqd = ctx.r["p6_rq"]
    p.op("vector", lambda e: e.reciprocal(rqd[:, :], sdc[:, :]), reads=[rsdc], writes=[rrqd], tag="f6_rc")

    def bc6(e):
        e.matmul(paux[:, 0:1], ctx.t["c_ones"][0:1, :], rqd[:, :], start=True, stop=True)
        return e.matmul(paux[:, 1:2], ctx.t["c_ones"][0:1, :], sdc[:, :], start=True, stop=True)
    p.op("tensor", bc6, reads=[rrqd, rsdc, ctx.r["c_ones"]], writes=[rpaux], tag="f6_bc")
    scd = ctx.sb("p6_scd", [128, 2], F32)
    rscd = ctx.r["p6_scd"]
    p.op("vector", lambda e: e.tensor_copy(scd[:, :], paux[:, 0:2]), reads=[rpaux], writes=[rscd], tag="f6_be")

    dwT = ctx.sb("p6_dwT", [128, 22 * 1024], BF16)
    rdwT = ctx.r["p6_dwT"]
    qh = ctx.sb("p6_qh", [128, 22 * 2048], BF16)
    rqh = ctx.r["p6_qh"]
    W6A = ctx._phase_cursor
    wd0 = ctx.sb("p6_wd0", [128, 2816], F32)
    rwd0 = ctx.r["p6_wd0"]
    swd = ctx.sb("p6_swd", [128, 8], F32)
    rswd = ctx.r["p6_swd"]
    rq_c = ctx.sb("p6_rqc", [128, 1], F32)
    rrqc = ctx.r["p6_rqc"]
    qn6 = ctx.sb("p6_qn", [128, 2816], BF16)
    rqn6 = ctx.r["p6_qn"]
    for hc in range(8):
        p.dma("sync", lambda e, hc=hc: e.dma_start(out=wd0[:, :], in_=wd_in[hc * 128:(hc + 1) * 128, :]),
              writes=[rwd0], tag=f"f6d{hc}_l")
        p.op("vector", lambda e, hc=hc: e.tensor_reduce(swd[:, hc:hc + 1], wd0[:, :], AX.X, ALU.max, apply_absolute_value=True),
             reads=[rwd0], writes=[rswd], tag=f"f6d{hc}_r")
        p.op("vector", lambda e, hc=hc: e.tensor_scalar_mul(swd[:, hc:hc + 1], swd[:, hc:hc + 1], 1.0 / 127.0),
             reads=[rswd], writes=[rswd], tag=f"f6d{hc}_s")
        p.op("vector", lambda e, hc=hc: e.reciprocal(rq_c[:, :], swd[:, hc:hc + 1]), reads=[rswd], writes=[rrqc], tag=f"f6d{hc}_c")
        p.op("vector", lambda e: e.tensor_scalar(wd0[:, :], wd0[:, :], rq_c[:, 0:1], RND_C, ALU.mult, ALU.add),
             reads=[rwd0, rrqc], writes=[rwd0], tag=f"f6d{hc}_q1")
        p.op("vector", lambda e: e.tensor_scalar(wd0[:, :], wd0[:, :], RND_C, 127.0, ALU.subtract, ALU.min),
             reads=[rwd0], writes=[rwd0], tag=f"f6d{hc}_q2")
        p.op("vector", lambda e: e.tensor_scalar_max(qn6[:, :], wd0[:, :], -128.0), reads=[rwd0], writes=[rqn6], tag=f"f6d{hc}_q3")
        for grp in range(3):
            n_in = min(8, 22 - grp * 8)

            def trg(e, grp=grp, n_in=n_in):
                inst = None
                for k in range(n_in):
                    ic = grp * 8 + k
                    inst = e.transpose(ptrb[:, k * 128:(k + 1) * 128], qn6[:, ic * 128:(ic + 1) * 128], ctx.t["c_ident_b"][:, :])
                return inst
            p.op("tensor", trg, reads=[rqn6, ctx.r["c_ident_b"]], writes=[rptrb], tag=f"f6d{hc}_t{grp}")
            p.op("vector", lambda e, hc=hc, grp=grp, n_in=n_in: e.tensor_copy(
                bass.AP(dwT, grp * 8 * 1024 + hc * 128, [[22 * 1024, 128], [1024, n_in], [1, 128]]),
                ptrb[:, 0:n_in * 128].rearrange("p (k q) -> p k q", k=n_in)),
                reads=[rptrb], writes=[rdwT], tag=f"f6d{hc}_e{grp}")
    swrow = ctx.sb("p6_swr", [1, 1024], F32)
    rswrow = ctx.r["p6_swr"]
    for grp in range(2):
        def swtr(e, grp=grp):
            inst = None
            for k in range(4):
                hc = grp * 4 + k
                inst = e.transpose(paux[0:1, k * 128:(k + 1) * 128], swd[:, hc:hc + 1], ctx.t["c_ident"][:, :])
            return inst
        p.op("tensor", swtr, reads=[rswd, ctx.r["c_ident"]], writes=[rpaux], tag=f"f6sw_t{grp}")
        p.op("vector", lambda e, grp=grp: e.tensor_copy(swrow[:, grp * 512:(grp + 1) * 512], paux[0:1, 0:512]),
             reads=[rpaux], writes=[rswrow], tag=f"f6sw_e{grp}")
    swb = ctx.sb("p6_swb", [128, 1024], F32)
    rswb = ctx.r["p6_swb"]

    def swbc(e):
        e.matmul(pwide[:, 0:512], ctx.t["c_ones"][0:1, :], swrow[:, 0:512], start=True, stop=True)
        return e.matmul(pwide[:, 512:1024], ctx.t["c_ones"][0:1, :], swrow[:, 512:1024], start=True, stop=True)
    p.op("tensor", swbc, reads=[rswrow, ctx.r["c_ones"]], writes=[rpwide], tag="f6sw_bc")
    p.op("vector", lambda e: e.tensor_mul(swb[:, :], pwide[:, :], modt["gb"][0][:, :]),
         reads=[rpwide, modt["gb"][1]], writes=[rswb], tag="f6sw_gb")
    for ic in range(22):
        p.op("vector", lambda e, ic=ic: e.tensor_mul(dwT[:, ic * 1024:(ic + 1) * 1024], dwT[:, ic * 1024:(ic + 1) * 1024], swb[:, :]),
             reads=[rdwT, rswb], writes=[rdwT], tag=f"f6fold{ic}")
    ctx.phase(W6A)
    hstr = [ctx.sb(f"p6_hs{i}", [128, 2048], BF16) for i in range(2)]
    rhstr = [ctx.r[f"p6_hs{i}"] for i in range(2)]
    qtf = ctx.sb("p6_qtf", [128, 2048], F32)
    rqtf = ctx.r["p6_qtf"]
    for ic in range(22):
        b = ic % 2
        p.dma("sync", lambda e, ic=ic, b=b: e.dma_start(out=hstr[b][:, :], in_=hact_d[ic * 128:(ic + 1) * 128, :]),
              reads=[ctx.r["hact_d"]], writes=[rhstr[b]], tag=f"f6h{ic}_l")
        p.op("vector", lambda e, b=b: e.tensor_scalar(qtf[:, :], hstr[b][:, :], scd[:, 0:1], RND_C, ALU.mult, ALU.add),
             reads=[rhstr[b], rscd], writes=[rqtf], tag=f"f6h{ic}_q1")
        p.op("vector", lambda e: e.tensor_scalar(qtf[:, :], qtf[:, :], RND_C, 127.0, ALU.subtract, ALU.min),
             reads=[rqtf], writes=[rqtf], tag=f"f6h{ic}_q2")
        p.op("vector", lambda e, ic=ic: e.tensor_scalar_max(qh[:, ic * 2048:(ic + 1) * 2048], qtf[:, :], -128.0),
             reads=[rqtf], writes=[rqh], tag=f"f6h{ic}_q3")
    xst6 = [ctx.sb(f"p6_xs{i}", [128, 1024], F32) for i in range(2)]
    rxst6 = [ctx.r[f"p6_xs{i}"] for i in range(2)]
    for t in range(16):
        b = t % 2
        p.dma("sync", lambda e, t=t, b=b: e.dma_start(out=xst6[b][:, :], in_=x3_d[t * 128:(t + 1) * 128, :]),
              reads=[rx3d], writes=[rxst6[b]], tag=f"f6o{t}_l")

        def dmm6(e, t=t):
            inst = None
            for hf in range(2):
                for ic in range(22):
                    inst = e.matmul(pwide[:, hf * 512:(hf + 1) * 512],
                                    qh[:, ic * 2048 + t * 128: ic * 2048 + (t + 1) * 128],
                                    dwT[:, ic * 1024 + hf * 512: ic * 1024 + (hf + 1) * 512],
                                    start=(ic == 0), stop=(ic == 21))
            return inst
        p.op("tensor", dmm6, reads=[rqh, rdwT], writes=[rpwide], tag=f"f6o{t}_m")
        for hf in range(2):
            p.op("vector", lambda e, b=b, hf=hf: e.scalar_tensor_tensor(
                xst6[b][:, hf * 512:(hf + 1) * 512], pwide[:, hf * 512:(hf + 1) * 512], scd[:, 1:2],
                xst6[b][:, hf * 512:(hf + 1) * 512], ALU.mult, ALU.add),
                reads=[rpwide, rscd, rxst6[b]], writes=[rxst6[b]], tag=f"f6o{t}_c{hf}")
        p.dma("sync", lambda e, t=t, b=b: e.dma_start(out=x4_out[t * 128:(t + 1) * 128, :], in_=xst6[b][:, :]),
              reads=[rxst6[b]], writes=[ctx.r["out:x4"]], tag=f"f6o{t}_s")
    fin = ctx.sb("p6_fin", [1, 1], F32)
    p.op("gpsimd", lambda e: e.memset(fin[:, :], 0.0), reads=[ctx.r["out:x4"]], writes=[ctx.r["p6_fin"]], tag="f_fin")

    p.build(nc, verbose=True)
    ctx.close()
    nc.reset()
    return ctx



def make_winpad(W_in):
    winT = W_in.T
    winpad = np.zeros((1024, 1024), np.float32)
    for h in range(16):
        cb_ = (h % 4) * 32 + (h // 4) * 128
        winpad[:, cb_:cb_ + 16] = winT[:, 2048 + h * 16: 2048 + (h + 1) * 16]
        winpad[:, 512 + cb_:512 + cb_ + 16] = winT[:, 2304 + h * 16: 2304 + (h + 1) * 16]
    return np.ascontiguousarray(winpad)


def shard_rows(x4, core):
    b, h = core // 2, core % 2
    return np.ascontiguousarray(x4[b, h * 2048:(h + 1) * 2048, :])


def kernel(**inputs):
    g = {k: np.ascontiguousarray(np.asarray(v, dtype=np.float32)) for k, v in inputs.items()}
    x = g["L0_attn_x"]
    c0, c1 = g["L0_c"], g["L1_c"]

    wattnT = np.ascontiguousarray(g["L0_attn_w_attn"].T)
    wprojT = np.ascontiguousarray(g["L0_attn_w_proj"].T)
    gatewT = np.ascontiguousarray(g["L0_ffn_gate_w"].T)
    dgT = np.ascontiguousarray(g["L0_ffn_dyn_gate_w"].transpose(0, 2, 1))
    duT = np.ascontiguousarray(g["L0_ffn_dyn_up_w"].transpose(0, 2, 1))
    ddT = np.ascontiguousarray(g["L0_ffn_dyn_down_w"].transpose(0, 2, 1))
    sgT = np.ascontiguousarray(g["L0_ffn_sh_gate_w"][0].T)
    suT = np.ascontiguousarray(g["L0_ffn_sh_up_w"][0].T)
    sdT = np.ascontiguousarray(g["L0_ffn_sh_down_w"][0].T)

    winT = np.ascontiguousarray(g["L1_attn_W_in"].T)
    winpadT = make_winpad(g["L1_attn_W_in"])

    ctx = build_fused()
    p_ = np.arange(128)
    in_maps = []
    for core in range(8):
        b = core // 2
        hbase = 8 * (core % 2)
        gidx = np.zeros((128, 24), np.int32)
        for ch in range(4):
            for half in range(2):
                gidx[:, ch * 2 + half] = half * 3072 + (hbase + 2 * ch) * 64 + p_
                gidx[:, 8 + ch * 2 + half] = half * 3072 + 1024 + (hbase + 2 * ch) * 64 + p_
        for hp in range(4):
            for half in range(2):
                gidx[:, 16 + hp * 2 + half] = half * 3072 + 2048 + (hbase + 2 * hp) * 64 + p_
        yidx = np.zeros((128, 32), np.int32)
        t0 = (core % 2) * 2048
        for t in range(16):
            for hh in range(2):
                yidx[:, t * 2 + hh] = hh * 4096 + t0 + t * 128 + p_
        in_maps.append({
            "x": shard_rows(x, core),
            "c0_row": np.ascontiguousarray(c0[b:b + 1]),
            "c1_row": np.ascontiguousarray(c1[b:b + 1]),
            "wmod0": g["L0_wmod"], "wmod1": g["L1_wmod"],
            "wattnT": wattnT, "wprojT": wprojT, "gatewT": gatewT,
            "dgT": dgT, "duT": duT, "ddT": ddT,
            "sgT": sgT, "suT": suT, "sdT": sdT,
            "winT": winT, "winpadT": winpadT,
            "dtwT": np.ascontiguousarray(g["L1_attn_dt_W"].T),
            "dtb": np.ascontiguousarray(g["L1_attn_dt_b"].reshape(16, 1)),
            "alog": np.ascontiguousarray(g["L1_attn_A_log"].reshape(16, 1)),
            "dpar": np.ascontiguousarray(g["L1_attn_D_param"].reshape(1, 16)),
            "normw": np.ascontiguousarray(g["L1_attn_norm_w"].reshape(1, 1024)),
            "woutT": np.ascontiguousarray(g["L1_attn_W_out"].T),
            "clip_x": np.ascontiguousarray(g["L1_ffn_clip_x"].reshape(1, 1)),
            "clip_down": np.ascontiguousarray(g["L1_ffn_clip_down"].reshape(1, 1)),
            "w_gate": g["L1_ffn_w_gate"], "w_up": g["L1_ffn_w_up"], "w_down": g["L1_ffn_w_down"],
            "pmask": np.ascontiguousarray((np.arange(8) < core).astype(np.float32).reshape(8, 1)),
            "sflag": np.full((128, 1), 1.0 if core % 2 else 0.0, np.float32),
            "gidx": gidx, "yidx": yidx,
        })
    res = run_bass_kernel_spmd(ctx.nc, in_maps, core_ids=list(range(8))).results
    out = np.empty((4, 4096, 1024), np.float32)
    for core in range(8):
        b, hf = core // 2, core % 2
        out[b, hf * 2048:(hf + 1) * 2048] = np.asarray(res[core]["x4"])
    return out
